# version 13
# baseline (speedup 1.0000x reference)
import numpy as np
import ml_dtypes

import concourse.bass as bass
import concourse.mybir as mybir
from concourse.bass_utils import run_bass_kernel_spmd

F32 = mybir.dt.float32
BF16 = mybir.dt.bfloat16
BF = ml_dtypes.bfloat16
AF = mybir.ActivationFunctionType
ALU = mybir.AluOpType
AX = mybir.AxisListType

NCORE = 8
S = 8192
H = 2048
NH = 16
HD = 128
INTER = 8192
TPC = S // NCORE
HPC = NH // NCORE
NFC = H // 128
NIC = INTER // 128
MASKVAL = -1.0e9
EPS = 1e-5


def _hadamard(n):
    h = np.array([[1.0]], dtype=np.float64)
    while h.shape[0] < n:
        h = np.block([[h, h], [h, -h]])
    return h


def AP2(t, off, dims):
    return bass.AP(t, int(off), [list(d) for d in dims])



def build_l1():
    nc = bass.Bass("TRN2", target_bir_lowering=False)
    x_in = nc.dram_tensor("x", [TPC, H], F32, kind="ExternalInput")
    xn_out = nc.dram_tensor("xn_fm", [H, TPC], BF16, kind="ExternalOutput")
    with nc.sbuf_tensor("xnf_sb", [128, NFC * TPC], BF16) as xn_fm:
        emit_ln(nc, x_in, xn_fm)
        with nc.semaphore("l1_st") as st, nc.Block(no_gpsimd_drain=True) as block:
            @block.sync
            def _(s):
                s.dma_start(out=AP2(xn_out, 0, [[TPC, 128], [128 * TPC, NFC], [1, TPC]]),
                            in_=AP2(xn_fm, 0, [[NFC * TPC, 128], [TPC, NFC], [1, TPC]])
                            ).then_inc(st, 16)
                s.wait_ge(st, 16)
    return nc


def emit_ln(nc, x_in, xn_fm):
    ident = nc.inline_tensor(np.eye(128, dtype=BF), name="ident")

    ntc = TPC // 128
    with (
        nc.sbuf_tensor("xbuf", [128, 2 * H], F32) as xbuf,
        nc.sbuf_tensor("sq", [128, H], F32) as sq,
        nc.sbuf_tensor("stats", [128, 8], F32) as stats,
        nc.sbuf_tensor("xn_tm", [128, 2 * H], BF16) as xn_tm,
        nc.sbuf_tensor("ident_sb", [128, 128], BF16) as ident_sb,
        nc.psum_tensor("tpA", [128, 1024], BF16) as tpA,
        nc.psum_tensor("tpB", [128, 1024], BF16) as tpB,
        nc.semaphore("ld0") as ld0,
        nc.semaphore("ld1") as ld1,
        nc.semaphore("idl") as idl,
        nc.semaphore("vst") as vst,
        nc.semaphore("sst") as sst,
        nc.semaphore("tmm") as tmm,
        nc.semaphore("tev") as tev,
        nc.semaphore("vsy") as vsy,
        nc.semaphore("st") as st,
        nc.Block(no_gpsimd_drain=True) as block,
    ):
        lds = [ld0, ld1]
        tp = [tpA, tpB]

        @block.sync
        def _(s):
            s.dma_start(out=ident_sb[:, :], in_=ident.ap()).then_inc(idl, 16)
            for tc in range(ntc):
                if tc >= 2:
                    s.wait_ge(tmm, (tc - 1) * 16)
                s.dma_start(out=xbuf[:, (tc % 2) * H:(tc % 2) * H + H],
                            in_=x_in[tc * 128:(tc + 1) * 128, :]
                            ).then_inc(lds[tc % 2], 16)

        @block.scalar
        def _(sc):
            for tc in range(ntc):
                sl = (tc % 2) * H
                sc.wait_ge(lds[tc % 2], (tc // 2 + 1) * 16)
                sc.activation(sq[:, :], xbuf[:, sl:sl + H], AF.Square,
                              accum_out=stats[:, 2:3]).then_inc(sst, 1)
                sc.wait_ge(vst, 5 * tc + 3)
                sc.activation(stats[:, 4:5], stats[:, 3:4], AF.Sqrt).then_inc(sst, 1)

        @block.vector
        def _(v):
            vsn = [0]

            def vsy_inc(inst):
                vsn[0] += 1
                inst.then_inc(vsy, 1)

            def vsy_wait():
                v.wait_ge(vsy, vsn[0])

            for tc in range(ntc):
                sl = (tc % 2) * H
                v.wait_ge(lds[tc % 2], (tc // 2 + 1) * 16)
                vsy_inc(v.reduce_sum(stats[:, 0:1], xbuf[:, sl:sl + H], axis=AX.X))
                v.wait_ge(sst, 2 * tc + 1)
                vsy_wait()
                vsy_inc(v.tensor_scalar_mul(stats[:, 1:2], stats[:, 0:1], 1.0 / H))
                vsy_wait()
                v.tensor_mul(stats[:, 5:6], stats[:, 1:2], stats[:, 1:2])
                vsy_inc(v.tensor_scalar(stats[:, 3:4], stats[:, 2:3], 1.0 / H, EPS,
                                        ALU.mult, ALU.add))
                vsy_wait()
                v.tensor_sub(stats[:, 3:4], stats[:, 3:4], stats[:, 5:6]
                             ).then_inc(vst, 3)
                v.wait_ge(sst, 2 * tc + 2)
                vsy_inc(v.reciprocal(stats[:, 6:7], stats[:, 4:5]))
                vsy_wait()
                v.tensor_scalar(xn_tm[:, sl:sl + H], xbuf[:, sl:sl + H],
                                stats[:, 1:2], stats[:, 6:7],
                                ALU.subtract, ALU.mult).then_inc(vst, 2)
                if tc >= 1:
                    for half in range(2):
                        v.wait_ge(tmm, (tc - 1) * 16 + (half + 1) * 8)
                        dst = AP2(xn_fm, (half * 8) * TPC + (tc - 1) * 128,
                                  [[NFC * TPC, 128], [TPC, 8], [1, 128]])
                        v.tensor_copy(dst, tp[half][:, 0:1024]).then_inc(tev, 1)
            for half in range(2):
                v.wait_ge(tmm, (ntc - 1) * 16 + (half + 1) * 8)
                dst = AP2(xn_fm, (half * 8) * TPC + (ntc - 1) * 128,
                          [[NFC * TPC, 128], [TPC, 8], [1, 128]])
                v.tensor_copy(dst, tp[half][:, 0:1024]).then_inc(tev, 1)

        @block.tensor
        def _(t):
            t.wait_ge(idl, 16)
            for tc in range(ntc):
                sl = (tc % 2) * H
                t.wait_ge(vst, 5 * tc + 5)
                for half in range(2):
                    if tc >= 1:
                        t.wait_ge(tev, (tc - 1) * 2 + half + 1)
                    for j in range(8):
                        fc = half * 8 + j
                        inst = t.transpose(tp[half][:, j * 128:(j + 1) * 128],
                                           xn_tm[:, sl + fc * 128:sl + (fc + 1) * 128],
                                           ident_sb[:, :])
                    inst.then_inc(tmm, 8)






def _ffn_persist_alloc(nc, es):
    d = {}
    d["tbl_dp"] = es.enter_context(nc.sbuf_tensor("tbl_dp", [128, NIC * 128], BF16))
    d["wsg"] = es.enter_context(nc.sbuf_tensor("wsg", [128, NIC], F32))
    d["wsdp2"] = es.enter_context(nc.sbuf_tensor("wsdp2_sb", [128, NFC], F32))
    d["y_dram"] = nc.dram_tensor("y_dram", [H, TPC], BF16, kind="Internal")
    return d


def emit_ffn_tables(nc, d, su_ug2, su_dp2, wsu2, wsg2, wsdp2_in, h128f):
    with (
        nc.sbuf_tensor("t_suug", [128, NFC], F32) as suug,
        nc.sbuf_tensor("t_sudp", [128, NIC], F32) as sudp,
        nc.sbuf_tensor("t_wsu", [128, NIC], F32) as wsu,
        nc.sbuf_tensor("t_h128", [128, 128], F32) as h128,
        nc.sbuf_tensor("t_scomb", [128, NIC], F32) as scomb,
        nc.semaphore("tb_ld") as ld,
        nc.semaphore("tb_v") as vs,
    ):
        with nc.Block(no_gpsimd_drain=True) as block:
            @block.sync
            def _(s):
                s.dma_start(out=suug[:, :], in_=su_ug2[:, :]).then_inc(ld, 16)
                s.dma_start(out=sudp[:, :], in_=su_dp2[:, :]).then_inc(ld, 16)
                s.dma_start(out=wsu[:, :], in_=wsu2[:, :]).then_inc(ld, 16)
                s.dma_start(out=d["wsg"][:, :], in_=wsg2[:, :]).then_inc(ld, 16)
                s.dma_start(out=d["wsdp2"][:, :], in_=wsdp2_in[:, :]).then_inc(ld, 16)
                s.dma_start(out=h128[:, :], in_=h128f.ap()).then_inc(ld, 16)

            @block.vector
            def _(v):
                v.wait_ge(ld, 96)
                vseq = [0]

                def stamp(inst):
                    vseq[0] += 1
                    inst.then_inc(vs, 1)

                stamp(v.tensor_scalar_mul(suug[:, :], suug[:, :], 1.0 / np.sqrt(2048.0)))
                v.wait_ge(vs, vseq[0])
                for c in range(NFC):
                    v.tensor_scalar(d["tbl_ug"][:, c * 128:(c + 1) * 128], h128[:, :],
                                    suug[:, c:c + 1], None, ALU.mult)
                stamp(v.tensor_mul(scomb[:, :], sudp[:, :], wsu[:, :]))
                v.wait_ge(vs, vseq[0])
                stamp(v.tensor_scalar_mul(scomb[:, :], scomb[:, :], 1.0 / np.sqrt(8192.0)))
                v.wait_ge(vs, vseq[0])
                for c in range(NIC):
                    v.tensor_scalar(d["tbl_dp"][:, c * 128:(c + 1) * 128], h128[:, :],
                                    scomb[:, c:c + 1], None, ALU.mult)


def emit_ffn_yhad(nc, d, xn_own_sb, pre_gpsimd=None):
    with (
        nc.sbuf_tensor("yz_a", [128, NFC * 512], F32) as za,
        nc.sbuf_tensor("yz_b", [128, NFC * 512], F32) as zb,
        nc.sbuf_tensor("y_sb", [128, NFC * TPC], BF16) as y_sb,
        nc.psum_tensor("y_ps0", [128, 512], F32) as yp0,
        nc.psum_tensor("y_ps1", [128, 512], F32) as yp1,
        nc.semaphore("yh_m") as mm,
        nc.semaphore("yh_e") as ev,
        nc.semaphore("yh_v") as vs,
        nc.semaphore("yh_st") as yst,
        nc.Block(no_gpsimd_drain=True) as block,
    ):
        yp = [yp0, yp1]

        if pre_gpsimd is not None:
            @block.gpsimd
            def _(g):
                pre_gpsimd(g)

        @block.tensor
        def _(t):
            for th in range(2):
                for c in range(NFC):
                    i = th * NFC + c
                    if i >= 2:
                        t.wait_ge(ev, i - 1)
                    t.matmul(yp[i % 2][:, :], d["tbl_ug"][:, c * 128:(c + 1) * 128],
                             xn_own_sb[:, c * TPC + th * 512: c * TPC + th * 512 + 512],
                             start=True, stop=True).then_inc(mm, 1)

        @block.vector
        def _(v):
            vseq = [0]

            def stamp(inst):
                vseq[0] += 1
                inst.then_inc(vs, 1)

            for th in range(2):
                for c in range(NFC):
                    i = th * NFC + c
                    v.wait_ge(mm, i + 1)
                    v.tensor_copy(za[:, c * 512:(c + 1) * 512], yp[i % 2][:, :]
                                  ).then_inc(ev, 1)
                v.wait_ge(ev, (th + 1) * NFC)
                src, dst = za, zb
                for si, dd in enumerate([1, 2, 4, 8]):
                    last = si == 3
                    nq = NFC // (2 * dd)
                    sA = AP2(src, 0, [[NFC * 512, 128], [2 * dd * 512, nq], [512, dd], [1, 512]])
                    sB = AP2(src, dd * 512, [[NFC * 512, 128], [2 * dd * 512, nq], [512, dd], [1, 512]])
                    if last:
                        dlo = AP2(y_sb, th * 512, [[NFC * TPC, 128], [2 * dd * TPC, nq], [TPC, dd], [1, 512]])
                        dhi = AP2(y_sb, dd * TPC + th * 512, [[NFC * TPC, 128], [2 * dd * TPC, nq], [TPC, dd], [1, 512]])
                    else:
                        dlo = AP2(dst, 0, [[NFC * 512, 128], [2 * dd * 512, nq], [512, dd], [1, 512]])
                        dhi = AP2(dst, dd * 512, [[NFC * 512, 128], [2 * dd * 512, nq], [512, dd], [1, 512]])
                    v.tensor_tensor(dlo, sA, sB, ALU.add)
                    stamp(v.tensor_tensor(dhi, sA, sB, ALU.subtract))
                    v.wait_ge(vs, vseq[0])
                    src, dst = dst, src

        @block.sync
        def _(s):
            s.wait_ge(vs, 8)
            s.dma_start(out=AP2(d["y_dram"], 0, [[TPC, 128], [128 * TPC, NFC], [1, TPC]]),
                        in_=AP2(y_sb, 0, [[NFC * TPC, 128], [TPC, NFC], [1, TPC]])
                        ).then_inc(yst, 16)
            s.wait_ge(yst, 16)


def emit_ffn_main(nc, d, w_ugT, w_downT, ffn_out, pre_gpsimd=None):
    WB = 128
    NB = INTER // WB
    import contextlib
    with contextlib.ExitStack() as hs:
        hbuf = [hs.enter_context(nc.sbuf_tensor(f"f_h{t}", [128, NIC * 512], BF16))
                for t in range(2)]
        z1a = hs.enter_context(nc.sbuf_tensor("f_z1a", [128, NIC * 128], BF16))
        yld = hs.enter_context(nc.semaphore("f_yld"))
        ys = hs.enter_context(contextlib.ExitStack())
        yh = ys.enter_context(nc.sbuf_tensor("f_yh", [128, NFC * 512], BF16))
        nyld = [0]

        def ug_scope(es, th):
            E = es.enter_context
            g = {}
            g["wub"] = E(nc.sbuf_tensor(f"f_wu{th}", [128, 3 * NFC * 2 * WB], BF16))
            g["gb"] = E(nc.sbuf_tensor(f"f_g{th}", [128, 512], F32))
            g["pu"] = [E(nc.psum_tensor(f"f_pu{j}{th}", [128, 512], F32)) for j in range(3)]
            g["pg"] = [E(nc.psum_tensor(f"f_pg{j}{th}", [128, 512], F32)) for j in range(3)]
            g["wlu"] = [E(nc.semaphore(f"f_wlu{j}{th}")) for j in range(3)]
            g["mmu"] = E(nc.semaphore(f"f_mmu{th}"))
            g["mmg"] = E(nc.semaphore(f"f_mmg{th}"))
            g["sil"] = E(nc.semaphore(f"f_sil{th}"))
            g["hdn"] = E(nc.semaphore(f"f_hdn{th}"))
            return g

        def fw_scope(es, th):
            E = es.enter_context
            f = {}
            f["s1p"] = [E(nc.psum_tensor(f"f_s1p{j}{th}", [128, 512], F32)) for j in range(2)]
            f["s1m"] = E(nc.semaphore(f"f_s1m{th}"))
            f["s1e"] = E(nc.semaphore(f"f_s1e{th}"))
            f["bfv"] = E(nc.semaphore(f"f_bfv{th}"))
            return f

        def dn_scope(es, th):
            E = es.enter_context
            n = {}
            n["wdn"] = E(nc.sbuf_tensor(f"f_wd{th}", [128, 2 * NIC * 128], BF16))
            n["slab"] = E(nc.sbuf_tensor(f"f_sl{th}", [128, 2 * 512], F32))
            n["dp"] = [E(nc.psum_tensor(f"f_dp{j}{th}", [128, 512], F32)) for j in range(2)]
            n["wdl"] = [[E(nc.semaphore(f"f_wdl{j}{k}{th}")) for k in range(2)]
                        for j in range(2)]
            n["dmm"] = E(nc.semaphore(f"f_dmm{th}"))
            n["dev"] = E(nc.semaphore(f"f_dev{th}"))
            n["fst"] = [E(nc.semaphore(f"f_fst{j}{th}")) for j in range(2)]
            return n

        def ug_gpsimd_dma(gps, g, th, ib):
            slot = ib % 3
            if ib >= 3:
                gps.wait_ge(g["mmg"], ib - 2)
            gps.dma_start(
                out=AP2(g["wub"], slot * NFC * 2 * WB,
                        [[3 * NFC * 2 * WB, 128], [2 * WB, NFC], [1, 2 * WB]]),
                in_=AP2(w_ugT, ib * 2 * WB,
                        [[2 * INTER, 128], [128 * 2 * INTER, NFC], [1, 2 * WB]])
            ).then_inc(g["wlu"][slot], 16)

        def ug_tensor_chunk(t, g, th, c):
            ib, ps, slot = c, c % 3, c % 3
            t.wait_ge(yld, 16 * nyld[0])
            t.wait_ge(g["wlu"][slot], 16 * (ib // 3 + 1))
            if c >= 3:
                t.wait_ge(g["hdn"], c - 2)
                t.wait_ge(g["sil"], c - 2)
            base = slot * NFC * 2 * WB
            for hc in range(NFC):
                inst = t.matmul(g["pu"][ps][:, :],
                                g["wub"][:, base + hc * 2 * WB: base + hc * 2 * WB + 128],
                                yh[:, hc * 512:(hc + 1) * 512],
                                start=(hc == 0), stop=(hc == NFC - 1))
            inst.then_inc(g["mmu"], 1)
            for hc in range(NFC):
                inst = t.matmul(g["pg"][ps][:, :],
                                g["wub"][:, base + hc * 2 * WB + WB: base + hc * 2 * WB + WB + 128],
                                yh[:, hc * 512:(hc + 1) * 512],
                                start=(hc == 0), stop=(hc == NFC - 1))
            inst.then_inc(g["mmg"], 1)

        def ug_scalar_chunk(sc, g, th, c):
            sc.wait_ge(g["mmg"], c + 1)
            if c >= 1:
                sc.wait_ge(g["hdn"], c)
            sc.activation(g["gb"][:, :], g["pg"][c % 3][:, :], AF.Silu,
                          scale=d["wsg"][:, c:c + 1]).then_inc(g["sil"], 1)

        def ug_vector_chunk(v, g, th, c):
            v.wait_ge(g["mmu"], c + 1)
            v.wait_ge(g["sil"], c + 1)
            v.tensor_mul(hbuf[th][:, c * 512:(c + 1) * 512],
                         g["gb"][:, :], g["pu"][c % 3][:, :]).then_inc(g["hdn"], 1)

        def hslab(th, e, chunk_inner=128):
            return (hbuf[th], e * 128, 512)

        def bf_ap(buf_off_stride, dd, hi, cnt=128):
            buf, off, cs = buf_off_stride
            nq = NIC // (2 * dd)
            base = off + (dd * cs if hi else 0)
            return AP2(buf, base, [[NIC * 512 if cs == 512 else NIC * 128, 128],
                                   [2 * dd * cs, nq], [cs, dd], [1, cnt]])

        def fw_tensor_eighth(t, f, th, e):
            for grp in range(16):
                gg = e * 16 + grp
                if gg >= 2:
                    t.wait_ge(f["s1e"], gg - 1)
                for k in range(4):
                    c = grp * 4 + k
                    inst = t.matmul(
                        f["s1p"][gg % 2][:, k * 128:(k + 1) * 128],
                        d["tbl_dp"][:, c * 128:(c + 1) * 128],
                        hbuf[th][:, c * 512 + e * 128: c * 512 + e * 128 + 128],
                        start=True, stop=True)
                inst.then_inc(f["s1m"], 1)

        def fw_scalar_eighth(sc, f, th, e):
            for grp in range(16):
                gg = e * 16 + grp
                sc.wait_ge(f["s1m"], gg + 1)
                dst = AP2(hbuf[th], grp * 4 * 512 + e * 128,
                          [[NIC * 512, 128], [512, 4], [1, 128]])
                sc.activation(dst, f["s1p"][gg % 2][:, :], AF.Copy).then_inc(f["s1e"], 1)

        def fw_vector_gen(v, f, th, e):
            v.wait_ge(f["s1e"], (e + 1) * 16)
            hs_ = hslab(th, e)
            za_ = (z1a, 0, 128)
            src, dst = hs_, za_
            for si, dd in enumerate([1, 2, 4, 8, 16, 32]):
                sA = bf_ap(src, dd, False)
                sB = bf_ap(src, dd, True)
                dlo = bf_ap(dst, dd, False)
                dhi = bf_ap(dst, dd, True)
                v.tensor_tensor(dlo, sA, sB, ALU.add)
                yield
                v.tensor_tensor(dhi, sA, sB, ALU.subtract).then_inc(f["bfv"], 1)
                v.wait_ge(f["bfv"], e * 6 + si + 1)
                yield
                src, dst = dst, src

        def fw_vector_eighth(v, f, th, e):
            v.wait_ge(f["s1e"], (e + 1) * 16)
            hs_ = hslab(th, e)
            za_ = (z1a, 0, 128)
            src, dst = hs_, za_
            for si, dd in enumerate([1, 2, 4, 8, 16, 32]):
                sA = bf_ap(src, dd, False)
                sB = bf_ap(src, dd, True)
                dlo = bf_ap(dst, dd, False)
                dhi = bf_ap(dst, dd, True)
                v.tensor_tensor(dlo, sA, sB, ALU.add)
                v.tensor_tensor(dhi, sA, sB, ALU.subtract).then_inc(f["bfv"], 1)
                v.wait_ge(f["bfv"], e * 6 + si + 1)
                src, dst = dst, src

        def dn_gpsimd_dma(gps, n, th, hc):
            slot = hc % 2
            if hc >= 2:
                gps.wait_ge(n["dmm"], hc - 1)
            for half in range(2):
                gps.dma_start(
                    out=AP2(n["wdn"], slot * NIC * 128 + half * (NIC // 2) * 128,
                            [[2 * NIC * 128, 128], [128, NIC // 2], [1, 128]]),
                    in_=AP2(w_downT, hc * 128 + half * (NIC // 2) * 128 * H,
                            [[H, 128], [128 * H, NIC // 2], [1, 128]])
                ).then_inc(n["wdl"][slot][half], 16)

        def dn_tensor_hc(t, n, th, hc):
            slot = hc % 2
            if hc >= 2:
                t.wait_ge(n["dev"], hc - 1)
            for c in range(NIC):
                if c == 0:
                    t.wait_ge(n["wdl"][slot][0], 16 * (hc // 2 + 1))
                elif c == NIC // 2:
                    t.wait_ge(n["wdl"][slot][1], 16 * (hc // 2 + 1))
                inst = t.matmul(
                    n["dp"][hc % 2][:, :],
                    n["wdn"][:, slot * NIC * 128 + c * 128: slot * NIC * 128 + (c + 1) * 128],
                    hbuf[th][:, c * 512:(c + 1) * 512],
                    start=(c == 0), stop=(c == NIC - 1))
            inst.then_inc(n["dmm"], 1)

        def dn_scalar_hc(sc, n, th, hc):
            sc.wait_ge(n["dmm"], hc + 1)
            if hc >= 2:
                sc.wait_ge(n["fst"][hc % 2], 16 * (hc // 2))
            sc.activation(n["slab"][:, (hc % 2) * 512:(hc % 2) * 512 + 512],
                          n["dp"][hc % 2][:, :], AF.Identity,
                          scale=d["wsdp2"][:, hc:hc + 1]).then_inc(n["dev"], 1)

        def dn_sync_hc(s, n, th, hc):
            s.wait_ge(n["dev"], hc + 1)
            s.dma_start(
                out=ffn_out[hc * 128:(hc + 1) * 128, th * 512:(th + 1) * 512],
                in_=n["slab"][:, (hc % 2) * 512:(hc % 2) * 512 + 512]
            ).then_inc(n["fst"][hc % 2], 16)

        def load_yh(s, th):
            s.dma_start(out=AP2(yh, 0, [[NFC * 512, 128], [512, NFC], [1, 512]]),
                        in_=AP2(d["y_dram"], th * 512,
                                [[TPC, 128], [128 * TPC, NFC], [1, 512]])
                        ).then_inc(yld, 16)
            nyld[0] += 1

        with contextlib.ExitStack() as es:
            g0 = ug_scope(es, 0)
            block = es.enter_context(nc.Block(no_gpsimd_drain=True))

            @block.gpsimd
            def _(gps):
                if pre_gpsimd is not None:
                    pre_gpsimd(gps)
                for ib in range(NB):
                    ug_gpsimd_dma(gps, g0, 0, ib)

            @block.sync
            def _(s):
                load_yh(s, 0)

            @block.tensor
            def _(t):
                for c in range(NIC):
                    ug_tensor_chunk(t, g0, 0, c)

            @block.scalar
            def _(sc):
                for c in range(NIC):
                    ug_scalar_chunk(sc, g0, 0, c)

            @block.vector
            def _(v):
                for c in range(NIC):
                    ug_vector_chunk(v, g0, 0, c)

        with contextlib.ExitStack() as es:
            g1 = ug_scope(es, 1)
            f0 = fw_scope(es, 0)
            block = es.enter_context(nc.Block(no_gpsimd_drain=True))

            @block.sync
            def _(s):
                load_yh(s, 1)

            @block.gpsimd
            def _(gps):
                for ib in range(NB):
                    ug_gpsimd_dma(gps, g1, 1, ib)

            @block.tensor
            def _(t):
                for e in range(4):
                    fw_tensor_eighth(t, f0, 0, e)
                    for c in range(e * 16, (e + 1) * 16):
                        ug_tensor_chunk(t, g1, 1, c)

            @block.scalar
            def _(sc):
                for e in range(4):
                    fw_scalar_eighth(sc, f0, 0, e)
                    for c in range(e * 16, (e + 1) * 16):
                        ug_scalar_chunk(sc, g1, 1, c)

            @block.vector
            def _(v):
                for e in range(4):
                    bf_gen = fw_vector_gen(v, f0, 0, e - 1) if e >= 1 else None
                    for c in range(e * 16, (e + 1) * 16):
                        ug_vector_chunk(v, g1, 1, c)
                        if bf_gen is not None:
                            next(bf_gen, None)
                    if bf_gen is not None:
                        for _ in bf_gen:
                            pass
                fw_vector_eighth(v, f0, 0, 3)

        ys.close()
        with contextlib.ExitStack() as es:
            f1 = fw_scope(es, 1)
            n0 = dn_scope(es, 0)
            block = es.enter_context(nc.Block(no_gpsimd_drain=True))

            @block.gpsimd
            def _(gps):
                for hc in range(NFC):
                    dn_gpsimd_dma(gps, n0, 0, hc)

            @block.tensor
            def _(t):
                for e in range(4):
                    fw_tensor_eighth(t, f1, 1, e)
                    for hc in range(e * 4, (e + 1) * 4):
                        dn_tensor_hc(t, n0, 0, hc)

            @block.scalar
            def _(sc):
                for e in range(4):
                    fw_scalar_eighth(sc, f1, 1, e)
                    for hc in range(e * 4, (e + 1) * 4):
                        dn_scalar_hc(sc, n0, 0, hc)

            @block.vector
            def _(v):
                for e in range(4):
                    fw_vector_eighth(v, f1, 1, e)

            @block.sync
            def _(s):
                for hc in range(NFC):
                    dn_sync_hc(s, n0, 0, hc)
                s.wait_ge(n0["fst"][0], 16 * NFC // 2)
                s.wait_ge(n0["fst"][1], 16 * NFC // 2)

        with contextlib.ExitStack() as es:
            n1 = dn_scope(es, 1)
            block = es.enter_context(nc.Block(no_gpsimd_drain=True))

            @block.gpsimd
            def _(gps):
                for hc in range(NFC):
                    dn_gpsimd_dma(gps, n1, 1, hc)

            @block.tensor
            def _(t):
                for hc in range(NFC):
                    dn_tensor_hc(t, n1, 1, hc)

            @block.scalar
            def _(sc):
                for hc in range(NFC):
                    dn_scalar_hc(sc, n1, 1, hc)

            @block.sync
            def _(s):
                for hc in range(NFC):
                    dn_sync_hc(s, n1, 1, hc)
                s.wait_ge(n1["fst"][0], 16 * NFC // 2)
                s.wait_ge(n1["fst"][1], 16 * NFC // 2)



def build_ffn():
    nc = bass.Bass("TRN2", target_bir_lowering=False)
    xn_in = nc.dram_tensor("xn_own", [H, TPC], BF16, kind="ExternalInput")
    su_ug2 = nc.dram_tensor("su_ug2", [128, NFC], F32, kind="ExternalInput")
    su_dp2 = nc.dram_tensor("su_dp2", [128, NIC], F32, kind="ExternalInput")
    wsu2 = nc.dram_tensor("wsu2", [128, NIC], F32, kind="ExternalInput")
    wsg2 = nc.dram_tensor("wsg2", [128, NIC], F32, kind="ExternalInput")
    wsdp2 = nc.dram_tensor("wsdp2", [128, NFC], F32, kind="ExternalInput")
    w_ugT = nc.dram_tensor("w_ugT", [H, 2 * INTER], F32, kind="ExternalInput")
    w_downT = nc.dram_tensor("w_downT", [INTER, H], F32, kind="ExternalInput")
    ffn_out = nc.dram_tensor("ffn_out", [H, TPC], F32, kind="ExternalOutput")
    h128f = nc.inline_tensor(_hadamard(128).astype(np.float32), name="h128f")

    import contextlib
    with contextlib.ExitStack() as es:
        d = _ffn_persist_alloc(nc, es)
        esu = es.enter_context(contextlib.ExitStack())
        d["tbl_ug"] = esu.enter_context(nc.sbuf_tensor("tbl_ug", [128, NFC * 128], BF16))
        with nc.sbuf_tensor("xn_sb", [128, NFC * TPC], BF16) as xn_sb:
            with (
                nc.semaphore("xl") as xl,
                nc.Block(no_gpsimd_drain=True) as block,
            ):
                @block.sync
                def _(s):
                    s.dma_start(out=AP2(xn_sb, 0, [[NFC * TPC, 128], [TPC, NFC], [1, TPC]]),
                                in_=AP2(xn_in, 0, [[TPC, 128], [128 * TPC, NFC], [1, TPC]])
                                ).then_inc(xl, 16)
                    s.wait_ge(xl, 16)
            emit_ffn_tables(nc, d, su_ug2, su_dp2, wsu2, wsg2, wsdp2, h128f)
            emit_ffn_yhad(nc, d, xn_sb)
        esu.close()
        emit_ffn_main(nc, d, w_ugT, w_downT, ffn_out)
    return nc



def emit_attention(nc, xn_full, wqkv, bqkv2, ctx_out, mask_inl, qk_sb, v_tm, ctx_sb,
                   debug_qkv=None, gate=None, a2a=None, wo_early=None):
    QOFF = 0
    KOFF = HPC * S
    NTB = S // 512
    NOC = HPC * 3
    import contextlib
    with contextlib.ExitStack() as qes:
        E = qes.enter_context
        wq_sb = E(nc.sbuf_tensor("a_wqkv", [128, NFC * 768], BF16))
        bq_sb = E(nc.sbuf_tensor("a_bqkv", [128, 6], F32))
        xnb = E(nc.sbuf_tensor("a_xnb", [128, 2 * NFC * 512], BF16))
        vstage = E(nc.sbuf_tensor("a_vst", [128, 2 * 512], BF16))
        wl = E(nc.semaphore("a_wl"))
        wl2 = E(nc.semaphore("a_wl2"))
        xl0 = E(nc.semaphore("a_xl0"))
        xl1 = E(nc.semaphore("a_xl1"))
        qmm = E(nc.semaphore("a_qmm"))
        qev = E(nc.semaphore("a_qev"))
        vtm = E(nc.semaphore("a_vtm"))
        vte = E(nc.semaphore("a_vte"))
        qp0 = E(nc.psum_tensor("a_qp0", [128, 512], F32))
        qp1 = E(nc.psum_tensor("a_qp1", [128, 512], F32))
        qp2 = E(nc.psum_tensor("a_qp2", [128, 512], F32))
        tp0 = E(nc.psum_tensor("a_tp0", [128, 1024], BF16))
        tp1 = E(nc.psum_tensor("a_tp1", [128, 1024], BF16))
        id_sb = E(nc.sbuf_tensor("a_id", [128, 128], BF16))
        block = E(nc.Block(no_gpsimd_drain=True))
        qp = [qp0, qp1, qp2]
        tpp = [tp0, tp1]
        xl = [xl0, xl1]

        @block.gpsimd
        def _(g):
            g.dma_start(out=AP2(wq_sb, 0, [[NFC * 768, 128], [768, NFC], [1, 768]]),
                        in_=AP2(wqkv, 0, [[768, 128], [128 * 768, NFC], [1, 768]])
                        ).then_inc(wl2, 16)

        @block.sync
        def _(s):
            s.dma_start(out=bq_sb[:, :], in_=bqkv2[:, :]).then_inc(wl, 16)
            s.dma_start(out=id_sb[:, :],
                        in_=nc_ident_ap(nc)).then_inc(wl, 16)
            if gate is not None:
                s.wait_ge(gate[0], gate[1])
            for tb in range(NTB):
                if tb >= 2:
                    s.wait_ge(qmm, (tb - 1) * NOC)
                r, half = tb // 2, tb % 2
                s.dma_start(
                    out=AP2(xnb, (tb % 2) * NFC * 512,
                            [[2 * NFC * 512, 128], [512, NFC], [1, 512]]),
                    in_=AP2(xn_full, r * H * TPC + half * 512,
                            [[TPC, 128], [128 * TPC, NFC], [1, 512]])
                ).then_inc(xl[tb % 2], 16)

        @block.tensor
        def _(t):
            t.wait_ge(wl, 32)
            t.wait_ge(wl2, 16)
            for tb in range(NTB):
                slot = tb % 2
                t.wait_ge(xl[slot], 16 * (tb // 2 + 1))
                for oc in range(NOC):
                    i = tb * NOC + oc
                    if i >= 3:
                        t.wait_ge(qev, i - 2)
                    for fc in range(NFC):
                        inst = t.matmul(
                            qp[i % 3][:, :],
                            wq_sb[:, fc * 768 + oc * 128: fc * 768 + (oc + 1) * 128],
                            xnb[:, slot * NFC * 512 + fc * 512: slot * NFC * 512 + (fc + 1) * 512],
                            start=(fc == 0), stop=(fc == NFC - 1))
                    inst.then_inc(qmm, 1)
                for h in range(HPC):
                    t.wait_ge(qev, tb * NOC + 3 * h + 3)
                    if tb >= 1:
                        t.wait_ge(vte, (tb - 1) * 2 + h + 1)
                    for j in range(4):
                        inst = t.transpose(tpp[h][:, j * 128:(j + 1) * 128],
                                           vstage[:, h * 512 + j * 128: h * 512 + (j + 1) * 128],
                                           id_sb[:, :])
                    inst.then_inc(vtm, 1)

        @block.scalar
        def _(sc):
            for tb in range(NTB):
                for oc in range(NOC):
                    i = tb * NOC + oc
                    h, kind = oc // 3, oc % 3
                    sc.wait_ge(qmm, i + 1)
                    if kind == 0:
                        dst = qk_sb[:, QOFF + h * S + tb * 512: QOFF + h * S + (tb + 1) * 512]
                    elif kind == 1:
                        dst = qk_sb[:, KOFF + h * S + tb * 512: KOFF + h * S + (tb + 1) * 512]
                    else:
                        if tb >= 1:
                            sc.wait_ge(vtm, (tb - 1) * 2 + h + 1)
                        dst = vstage[:, h * 512:(h + 1) * 512]
                    sc.activation(dst, qp[i % 3][:, :], AF.Identity,
                                  bias=bq_sb[:, oc:oc + 1]).then_inc(qev, 1)

        @block.vector
        def _(v):
            for tb in range(NTB):
                for h in range(HPC):
                    v.wait_ge(vtm, tb * 2 + h + 1)
                    v.tensor_copy(v_tm[:, h * S + tb * 512: h * S + (tb + 1) * 512],
                                  tpp[h][:, 0:512]).then_inc(vte, 1)

    if debug_qkv is not None:
        dq, dk, dv = debug_qkv
        with nc.semaphore("dbg_st") as dst_sem, nc.Block(no_gpsimd_drain=True) as block:
            @block.sync
            def _(s):
                s.dma_start(out=dq[:, :], in_=qk_sb[:, QOFF:QOFF + HPC * S]).then_inc(dst_sem, 16)
                s.dma_start(out=dk[:, :], in_=qk_sb[:, KOFF:KOFF + HPC * S]).then_inc(dst_sem, 16)
                s.dma_start(out=dv[:, :], in_=v_tm[:, :]).then_inc(dst_sem, 16)
                s.wait_ge(dst_sem, 48)
        return

    import contextlib
    with contextlib.ExitStack() as aes:
        E = aes.enter_context
        mask_sb = E(nc.sbuf_tensor("a_mask", [128, 1024], F32))
        pt_sb = E(nc.sbuf_tensor("a_pt", [128, 4 * 512], BF16))
        lacc_v = [E(nc.sbuf_tensor(f"a_lav{j}", [128, 512], F32)) for j in range(2)]
        lacc_g = [E(nc.sbuf_tensor(f"a_lag{j}", [128, 512], F32)) for j in range(2)]
        ones_f32 = E(nc.sbuf_tensor("a_onesf", [128, 1], F32))
        ones_k1 = E(nc.sbuf_tensor("a_onesk1", [1, 128], F32))
        l_sb = E(nc.sbuf_tensor("a_l", [1, 512], F32))
        l_inv = E(nc.sbuf_tensor("a_linv", [1, 512], F32))
        lbc_sb = E(nc.sbuf_tensor("a_lbc", [128, 512], F32))
        sp_l = [E(nc.psum_tensor(f"a_sp{j}", [128, 512], F32)) for j in range(4)]
        ctx_ps2 = [E(nc.psum_tensor(f"a_ctxp{j}", [128, 512], F32)) for j in range(2)]
        l_ps2 = [E(nc.psum_tensor(f"a_lp{j}", [128, 512], F32)) for j in range(2)]
        ml = E(nc.semaphore("a_ml"))
        alcv = E(nc.semaphore("a_alcv"))
        alcg = E(nc.semaphore("a_alcg"))
        albr2 = E(nc.semaphore("a_albr2"))
        amm = E(nc.semaphore("a_amm"))
        aex = E(nc.semaphore("a_aex"))
        amk = E(nc.semaphore("a_amk"))
        apv = E(nc.semaphore("a_apv"))
        alv = E(nc.semaphore("a_alv"))
        albc = E(nc.semaphore("a_albc"))
        alev = E(nc.semaphore("a_alev"))
        acev = E(nc.semaphore("a_acev"))
        cst = E(nc.semaphore("a_cst"))
        block = E(nc.Block(no_gpsimd_drain=True))
        sp = sp_l
        groups = [(h, g) for h in range(HPC) for g in range(16)]

        gj_of = {}
        mask_count_before = {}
        gj = 0
        nmask = 0
        for (h, g) in groups:
            kmax = 4 * (g + 1)
            ids = []
            for j in range(kmax):
                diag = j >= kmax - 4
                ids.append((gj, j, diag, nmask))
                if diag:
                    nmask += 1
                gj += 1
            gj_of[(h, g)] = ids
        total_pv = gj

        @block.sync
        def _(s):
            s.dma_start(out=mask_sb[:, :], in_=mask_inl.ap()).then_inc(ml, 16)

        @block.vector
        def _(v):
            v.memset(ones_f32[:, :], 1.0)
            v.memset(ones_k1[:, :], 1.0).then_inc(ml, 1)

        def _emit_pv(t, gi, rec, h, kmax):
            (cj, j, diag, _mc) = rec
            ctx_ps = ctx_ps2[gi % 2]
            t.wait_ge(aex, cj + 1)
            if j == 0 and gi >= 2:
                t.wait_ge(acev, gi - 1)
            t.matmul(ctx_ps[:, :],
                     v_tm[:, h * S + j * 128: h * S + (j + 1) * 128],
                     pt_sb[:, (cj % 4) * 512:(cj % 4 + 1) * 512],
                     start=(j == 0), stop=(j == kmax - 1)).then_inc(apv, 1)

        flat = []
        flat_alcv = {}
        flat_alcg = {}
        flat_has_g = {}
        ladd_sem_of = {}
        nav = nag = 0
        for gi, (h, g) in enumerate(groups):
            ids = gj_of[(h, g)]
            has_g = False
            for rec in ids:
                flat.append((gi, h, g, rec, len(ids)))
                cj, j = rec[0], rec[1]
                if j % 2 == 0:
                    nav += 1
                    ladd_sem_of[cj] = ("v", nav)
                else:
                    nag += 1
                    has_g = True
                    ladd_sem_of[cj] = ("g", nag)
            flat_alcv[gi] = nav
            flat_alcg[gi] = nag
            flat_has_g[gi] = has_g

        def _emit_s(t, item):
            (gi, h, g, (cj, j, diag, _mc), kmax) = item
            if cj >= 4:
                t.wait_ge(aex, cj - 3)
            t.matmul(sp[cj % 4][:, :],
                     qk_sb[:, KOFF + h * S + j * 128: KOFF + h * S + (j + 1) * 128],
                     qk_sb[:, QOFF + h * S + g * 512: QOFF + h * S + (g + 1) * 512],
                     start=True, stop=True).then_inc(amm, 1)

        @block.tensor
        def _(t):
            t.wait_ge(ml, 17)
            DEPTH = 3
            for k in range(DEPTH):
                _emit_s(t, flat[k])
            for n, item in enumerate(flat):
                if n + DEPTH < len(flat):
                    _emit_s(t, flat[n + DEPTH])
                (gi, h, g, rec, kmax) = item
                _emit_pv(t, gi, rec, h, kmax)
                if rec[1] == kmax - 1:
                    t.wait_ge(alcv, flat_alcv[gi])
                    t.wait_ge(alcg, flat_alcg[gi])
                    if gi >= 1:
                        t.wait_ge(alev, gi)
                    single = flat_has_g[gi] is False
                    inst = t.matmul(l_ps2[gi % 2][0:1, :], ones_f32[:, :],
                                    lacc_v[gi % 2][:, :], start=True, stop=single)
                    if not single:
                        inst = t.matmul(l_ps2[gi % 2][0:1, :], ones_f32[:, :],
                                        lacc_g[gi % 2][:, :], start=False, stop=True)
                    inst.then_inc(albr2, 1)
                    t.wait_ge(alv, 2 * gi + 2)
                    t.matmul(l_ps2[gi % 2][:, :], ones_k1[:, :], l_inv[:, :],
                             start=True, stop=True).then_inc(albc, 1)

        @block.vector
        def _(v):
            ncv = [0]
            for gi, (h, g) in enumerate(groups):
                ids = gj_of[(h, g)]
                for (cj, j, diag, mc) in ids:
                    if diag:
                        jloc = j - (len(ids) - 4)
                        u0 = 512 - 128 * jloc
                        v.wait_ge(amm, cj + 1)
                        v.tensor_add(sp[cj % 4][:, :], sp[cj % 4][:, :],
                                     mask_sb[:, u0:u0 + 512]).then_inc(amk, 1)
                    if j % 2 == 0:
                        v.wait_ge(aex, cj + 1)
                        pt = pt_sb[:, (cj % 4) * 512:(cj % 4 + 1) * 512]
                        if j == 0:
                            v.tensor_copy(lacc_v[gi % 2][:, :], pt).then_inc(alcv, 1)
                        else:
                            v.wait_ge(alcv, ncv[0])
                            v.tensor_add(lacc_v[gi % 2][:, :], lacc_v[gi % 2][:, :],
                                         pt).then_inc(alcv, 1)
                        ncv[0] += 1
                v.wait_ge(apv, ids[-1][0] + 1)
                v.wait_ge(albr2, gi + 1)
                v.tensor_copy(l_sb[:, :], l_ps2[gi % 2][0:1, :]).then_inc(alv, 1)
                v.wait_ge(alv, 2 * gi + 1)
                v.reciprocal(l_inv[:, :], l_sb[:, :]).then_inc(alv, 1)
                v.wait_ge(albc, gi + 1)
                v.tensor_copy(lbc_sb[:, :], l_ps2[gi % 2][:, :]).then_inc(alev, 1)
                v.wait_ge(alev, gi + 1)
                v.tensor_mul(ctx_sb[:, h * S + g * 512: h * S + (g + 1) * 512],
                             ctx_ps2[gi % 2][:, :], lbc_sb[:, :]).then_inc(acev, 1)

        @block.scalar
        def _(sc):
            for gi, (h, g) in enumerate(groups):
                for (cj, j, diag, mc) in gj_of[(h, g)]:
                    sc.wait_ge(amm, cj + 1)
                    if diag:
                        sc.wait_ge(amk, mc + 1)
                    if cj >= 4:
                        sc.wait_ge(apv, cj - 3)
                        eng, cnt = ladd_sem_of[cj - 4]
                        sc.wait_ge(alcv if eng == "v" else alcg, cnt)
                    sc.activation(pt_sb[:, (cj % 4) * 512:(cj % 4 + 1) * 512],
                                  sp[cj % 4][:, :], AF.Exp,
                                  scale=float(1.0 / np.sqrt(HD))).then_inc(aex, 1)

        if a2a is None:
            @block.sync
            def _(s):
                s.wait_ge(acev, len(groups))
                for h in range(HPC):
                    s.dma_start(out=ctx_out[h * 128:(h + 1) * 128, :],
                                in_=ctx_sb[:, h * S:(h + 1) * S]).then_inc(cst, 16)
                s.wait_ge(cst, 32)
        else:
            ag1_ins, ag1_outs, ccsem = a2a

            @block.gpsimd
            def _(g):
                ncg = [0]
                for gi, (h, gq) in enumerate(groups):
                    for (cj, j, diag, mc) in gj_of[(h, gq)]:
                        if j % 2 == 1:
                            g.wait_ge(aex, cj + 1)
                            pt = pt_sb[:, (cj % 4) * 512:(cj % 4 + 1) * 512]
                            if j == 1:
                                g.tensor_copy(lacc_g[gi % 2][:, :], pt
                                              ).then_inc(alcg, 1)
                            else:
                                g.wait_ge(alcg, ncg[0])
                                g.tensor_add(lacc_g[gi % 2][:, :],
                                             lacc_g[gi % 2][:, :], pt
                                             ).then_inc(alcg, 1)
                            ncg[0] += 1
                from concourse.aot_env import allow_ici_all_to_all
                if wo_early is not None:
                    w_out_dram, wl2 = wo_early
                    g.wait_ge(amm, gj)
                    for jj in range(4):
                        g.dma_start(
                            out=AP2(qk_sb, jj * 4 * H, [[NFC * H, 128], [H, 4], [1, H]]),
                            in_=AP2(w_out_dram, jj * 4 * 128 * H,
                                    [[H, 128], [128 * H, 4], [1, H]])
                        ).then_inc(wl2, 16)
                pieces = [(0, 0, TPC, 16), (1, 0, TPC, 32)]
                ncst = [0]
                for pi, (h, t0, tw, gate_ev) in enumerate(pieces):
                    g.wait_ge(acev, gate_ev)
                    g.dma_start(
                        out=AP2(ag1_ins[pi], 0,
                                [[tw, 128], [128 * tw, NCORE], [1, tw]]),
                        in_=AP2(ctx_sb, h * S + t0,
                                [[HPC * S, 128], [TPC, NCORE], [1, tw]])
                    ).then_inc(cst, 16)
                    ncst[0] += 16
                    g.wait_ge(cst, ncst[0])
                    with allow_ici_all_to_all():
                        g.collective_compute(
                            "AllToAll", ALU.bypass,
                            replica_groups=[list(range(NCORE))],
                            ins=[ag1_ins[pi].ap().opt()],
                            outs=[ag1_outs[pi].ap().opt()],
                        ).then_inc(ccsem, 1)


_IDENT_CACHE = {}


def nc_ident_ap(nc):
    key = id(nc)
    if key not in _IDENT_CACHE:
        _IDENT_CACHE[key] = nc.inline_tensor(np.eye(128, dtype=BF), name="ident_a")
    return _IDENT_CACHE[key].ap()


def _mask_inline(nc):
    m = np.full((128, 1024), MASKVAL, dtype=np.float32)
    for ki in range(128):
        m[ki, ki + 512:] = 0.0
    return nc.inline_tensor(m, name="mask_master")


def build_attn(debug=False):
    nc = bass.Bass("TRN2", target_bir_lowering=False)
    xn_full = nc.dram_tensor("xn_full", [NCORE * H, TPC], BF16, kind="ExternalInput")
    wqkv = nc.dram_tensor("wqkv", [H, 768], F32, kind="ExternalInput")
    bqkv2 = nc.dram_tensor("bqkv2", [128, 6], F32, kind="ExternalInput")
    ctx_out = nc.dram_tensor("ctx", [HPC * 128, S], BF16, kind="ExternalOutput")
    dbg = None
    if debug:
        dbg = (nc.dram_tensor("dq", [128, HPC * S], BF16, kind="ExternalOutput"),
               nc.dram_tensor("dk", [128, HPC * S], BF16, kind="ExternalOutput"),
               nc.dram_tensor("dv", [128, HPC * S], BF16, kind="ExternalOutput"))
    mask_inl = _mask_inline(nc)
    import contextlib
    with contextlib.ExitStack() as es:
        qk_sb = es.enter_context(nc.sbuf_tensor("qk_sb", [128, 2 * HPC * S], BF16))
        v_tm = es.enter_context(nc.sbuf_tensor("v_tm", [128, HPC * S], BF16))
        ctx_sb = es.enter_context(nc.sbuf_tensor("ctx_sb", [128, HPC * S], BF16))
        emit_attention(nc, xn_full, wqkv, bqkv2, ctx_out, mask_inl,
                       qk_sb, v_tm, ctx_sb, debug_qkv=dbg)
    return nc



def build_l3():
    nc = bass.Bass("TRN2", target_bir_lowering=False)
    ctx_my = nc.dram_tensor("ctx_my", [H, TPC], BF16, kind="ExternalInput")
    x_in = nc.dram_tensor("x", [TPC, H], F32, kind="ExternalInput")
    ffn_in = nc.dram_tensor("ffn", [H, TPC], F32, kind="ExternalInput")
    w_out = nc.dram_tensor("w_out", [H, H], F32, kind="ExternalInput")
    bout2 = nc.dram_tensor("bout2", [128, NFC], F32, kind="ExternalInput")
    y_out = nc.dram_tensor("y", [TPC, H], F32, kind="ExternalOutput")
    emit_final(nc, ctx_my, x_in, ffn_in, w_out, bout2, y_out)
    return nc


def emit_final(nc, ctx_my, x_in, ffn_in, w_out, bout2, y_out, gate=None,
               wo_pre=None):
    import contextlib
    NIT = 2 * NFC
    with contextlib.ExitStack() as es:
        E = es.enter_context
        ctx_sb = E(nc.sbuf_tensor("o_ctx", [128, NFC * TPC], BF16))
        if wo_pre is None:
            wo_sb = E(nc.sbuf_tensor("o_wo", [128, NFC * H], BF16))
        else:
            wo_sb, wl2_pre = wo_pre
        b2_sb = E(nc.sbuf_tensor("o_b2", [128, NFC], F32))
        id_f32 = E(nc.sbuf_tensor("o_idf", [128, 128], F32))
        fsl = E(nc.sbuf_tensor("o_f", [128, 2 * 512], F32))
        ssl = E(nc.sbuf_tensor("o_s", [128, 2 * 512], F32))
        xsl = E(nc.sbuf_tensor("o_x", [128, 4 * H], F32))
        osl = E(nc.sbuf_tensor("o_o", [128, 4 * H], F32))
        op0 = E(nc.psum_tensor("o_p0", [128, 512], F32))
        op1 = E(nc.psum_tensor("o_p1", [128, 512], F32))
        tp0 = E(nc.psum_tensor("o_t0", [128, 512], F32))
        tp1 = E(nc.psum_tensor("o_t1", [128, 512], F32))
        ld = E(nc.semaphore("o_ld"))
        wl2 = E(nc.semaphore("o_wl2"))
        fld = [E(nc.semaphore("o_fl0")), E(nc.semaphore("o_fl1"))]
        xld = [E(nc.semaphore("o_xl0")), E(nc.semaphore("o_xl1"))]
        omm = E(nc.semaphore("o_omm"))
        osm = E(nc.semaphore("o_osm"))
        ovv = E(nc.semaphore("o_ovv"))
        otm = E(nc.semaphore("o_otm"))
        oev = E(nc.semaphore("o_oev"))
        ost = [E(nc.semaphore("o_st0")), E(nc.semaphore("o_st1"))]
        op = [op0, op1]
        tp = [tp0, tp1]
        idf = nc.inline_tensor(np.eye(128, dtype=np.float32), name="ident_f32")

        with nc.Block(no_gpsimd_drain=True) as block:
            if wo_pre is None:
                @block.gpsimd
                def _(g):
                    for j in range(4):
                        g.dma_start(
                            out=AP2(wo_sb, j * 4 * H, [[NFC * H, 128], [H, 4], [1, H]]),
                            in_=AP2(w_out, j * 4 * 128 * H, [[H, 128], [128 * H, 4], [1, H]])
                        ).then_inc(wl2, 16)
            else:
                wl2 = wl2_pre

            @block.sync
            def _(s):
                s.dma_start(out=b2_sb[:, :], in_=bout2[:, :]).then_inc(ld, 16)
                s.dma_start(out=id_f32[:, :], in_=idf.ap()).then_inc(ld, 16)
                if gate is not None:
                    s.wait_ge(gate[0], gate[1])
                if isinstance(ctx_my, list):
                    pieces = [(0, 0, TPC, ctx_my[0]),
                              (NFC // 2, 0, TPC, ctx_my[1])]
                    nparts = 2
                    for cb, co, tw, part in pieces:
                        s.dma_start(out=AP2(ctx_sb, cb * TPC + co,
                                            [[NFC * TPC, 128], [TPC, NFC // 2], [1, tw]]),
                                    in_=AP2(part, 0, [[tw, 128], [128 * tw, NFC // 2], [1, tw]])
                                    ).then_inc(ld, 16)
                else:
                    nparts = 1
                    s.dma_start(out=AP2(ctx_sb, 0, [[NFC * TPC, 128], [TPC, NFC], [1, TPC]]),
                                in_=AP2(ctx_my, 0, [[TPC, 128], [128 * TPC, NFC], [1, TPC]])
                                ).then_inc(ld, 16)
                s.wait_ge(ld, 32 + 16 * nparts)
                for tb2 in range(2):
                    if tb2 == 1:
                        s.wait_ge(oev, NFC)
                        for tc in range(4):
                            s.dma_start(out=y_out[tc * 128:(tc + 1) * 128, :],
                                        in_=osl[:, tc * H:(tc + 1) * H]
                                        ).then_inc(ost[tc % 2], 16)
                    for tcq in range(4):
                        tc = tb2 * 4 + tcq
                        s.dma_start(out=xsl[:, tcq * H:(tcq + 1) * H],
                                    in_=x_in[tc * 128:(tc + 1) * 128, :]
                                    ).then_inc(xld[tb2], 16)
                    for hc in range(NFC):
                        i = tb2 * NFC + hc
                        sl = i % 2
                        if i >= 2:
                            s.wait_ge(ovv, i - 1)
                        s.dma_start(out=fsl[:, sl * 512:(sl + 1) * 512],
                                    in_=ffn_in[hc * 128:(hc + 1) * 128,
                                               tb2 * 512:(tb2 + 1) * 512]
                                    ).then_inc(fld[sl], 16)
                s.wait_ge(oev, 2 * NFC)
                for tc in range(4, 8):
                    s.dma_start(out=y_out[tc * 128:(tc + 1) * 128, :],
                                in_=osl[:, (tc - 4) * H:(tc - 3) * H]
                                ).then_inc(ost[tc % 2], 16)
                s.wait_ge(ost[0], 16 * 4)
                s.wait_ge(ost[1], 16 * 4)

            def _tr(t, i):
                t.wait_ge(ovv, i + 1)
                if i >= 2:
                    t.wait_ge(oev, i - 1)
                for q in range(4):
                    inst = t.transpose(
                        tp[i % 2][:, q * 128:(q + 1) * 128],
                        ssl[:, (i % 2) * 512 + q * 128:(i % 2) * 512 + (q + 1) * 128],
                        id_f32[:, :])
                inst.then_inc(otm, 1)

            @block.tensor
            def _(t):
                t.wait_ge(wl2, 64)
                t.wait_ge(ld, 32 + 16 * (2 if isinstance(ctx_my, list) else 1))
                for i in range(NIT):
                    tb2, hc = i // NFC, i % NFC
                    if i >= 2:
                        t.wait_ge(osm, i - 1)
                    for fc in range(NFC):
                        inst = t.matmul(
                            op[i % 2][:, :],
                            wo_sb[:, fc * H + hc * 128: fc * H + (hc + 1) * 128],
                            ctx_sb[:, fc * TPC + tb2 * 512: fc * TPC + (tb2 + 1) * 512],
                            start=(fc == 0), stop=(fc == NFC - 1))
                    inst.then_inc(omm, 1)
                    if i >= 1:
                        _tr(t, i - 1)
                _tr(t, NIT - 1)

            @block.scalar
            def _(sc):
                for i in range(NIT):
                    sl = i % 2
                    hc = i % NFC
                    sc.wait_ge(omm, i + 1)
                    if i >= 2:
                        sc.wait_ge(otm, i - 1)
                    sc.activation(ssl[:, sl * 512:(sl + 1) * 512], op[i % 2][:, :],
                                  AF.Identity,
                                  bias=b2_sb[:, hc:hc + 1]).then_inc(osm, 1)

            def _ev(v, i):
                tb2, hc = i // NFC, i % NFC
                v.wait_ge(otm, i + 1)
                v.wait_ge(xld[tb2], 16 * 4)
                if tb2 == 1 and hc == 0:
                    v.wait_ge(ost[0], 32)
                    v.wait_ge(ost[1], 32)
                for q in range(4):
                    inst = v.scalar_tensor_tensor(
                        osl[:, q * H + hc * 128: q * H + (hc + 1) * 128],
                        tp[i % 2][:, q * 128:(q + 1) * 128], 1.0,
                        xsl[:, q * H + hc * 128: q * H + (hc + 1) * 128],
                        ALU.mult, ALU.add)
                inst.then_inc(oev, 1)

            @block.vector
            def _(v):
                for i in range(NIT):
                    sl = i % 2
                    if i >= 1:
                        _ev(v, i - 1)
                    v.wait_ge(osm, i + 1)
                    v.wait_ge(fld[sl], 16 * (i // 2 + 1))
                    v.tensor_add(ssl[:, sl * 512:(sl + 1) * 512],
                                 ssl[:, sl * 512:(sl + 1) * 512],
                                 fsl[:, sl * 512:(sl + 1) * 512]).then_inc(ovv, 1)
                _ev(v, NIT - 1)




def kernel_unfused(L0_attn_hidden_states, L0_attn_w_qkv, L0_attn_b_qkv, L0_attn_w_out,
           L0_attn_b_out, L0_ffn_SU_ug, L0_ffn_SU_dp, L0_ffn_Wscale_ug,
           L0_ffn_Wscale_dp, L0_ffn_W_up, L0_ffn_W_gate, L0_ffn_W_down):
    x = np.ascontiguousarray(
        np.asarray(L0_attn_hidden_states, dtype=np.float32).reshape(S, H))
    w_qkv = np.asarray(L0_attn_w_qkv, dtype=np.float32)
    b_qkv = np.asarray(L0_attn_b_qkv, dtype=np.float32)
    w_out = np.ascontiguousarray(np.asarray(L0_attn_w_out, dtype=np.float32))
    b_out = np.asarray(L0_attn_b_out, dtype=np.float32)
    su_ug = np.asarray(L0_ffn_SU_ug, dtype=np.float32)
    su_dp = np.asarray(L0_ffn_SU_dp, dtype=np.float32)
    ws_ug = np.asarray(L0_ffn_Wscale_ug, dtype=np.float32)
    ws_dp = np.asarray(L0_ffn_Wscale_dp, dtype=np.float32)
    w_upT = np.ascontiguousarray(np.asarray(L0_ffn_W_up, dtype=np.float32).T)
    w_gateT = np.ascontiguousarray(np.asarray(L0_ffn_W_gate, dtype=np.float32).T)
    w_downT = np.ascontiguousarray(np.asarray(L0_ffn_W_down, dtype=np.float32).T)

    cores = list(range(NCORE))

    nc1 = build_l1()
    res1 = run_bass_kernel_spmd(
        nc1, [{"x": x[c * TPC:(c + 1) * TPC]} for c in cores], core_ids=cores)
    xn_shards = [np.asarray(r["xn_fm"]) for r in res1.results]
    xn_full = np.ascontiguousarray(np.concatenate(xn_shards, axis=0))

    nc2 = build_ffn()
    ffn_maps = []
    for c in cores:
        ffn_maps.append({
            "xn_own": xn_shards[c],
            "su_ug2": np.ascontiguousarray(su_ug.reshape(NFC, 128).T),
            "su_dp2": np.ascontiguousarray(su_dp.reshape(NIC, 128).T),
            "wsu2": np.ascontiguousarray(ws_ug[:INTER].reshape(NIC, 128).T),
            "wsg2": np.ascontiguousarray(ws_ug[INTER:].reshape(NIC, 128).T),
            "wsdp2": np.ascontiguousarray(ws_dp.reshape(NFC, 128).T),
            "w_upT": w_upT, "w_gateT": w_gateT, "w_downT": w_downT,
        })
    res2 = run_bass_kernel_spmd(nc2, ffn_maps, core_ids=cores)
    ffn_shards = [np.asarray(r["ffn_out"]) for r in res2.results]

    nc3 = build_attn()
    attn_maps = []
    for c in cores:
        attn_maps.append({
            "xn_full": xn_full,
            "wqkv": np.ascontiguousarray(w_qkv[:, 768 * c:768 * (c + 1)]),
            "bqkv2": np.ascontiguousarray(
                b_qkv[768 * c:768 * (c + 1)].reshape(6, 128).T),
        })
    res3 = run_bass_kernel_spmd(nc3, attn_maps, core_ids=cores)
    ctx_full = np.concatenate([np.asarray(r["ctx"]) for r in res3.results], axis=0)

    nc4 = build_l3()
    l3_maps = []
    for c in cores:
        l3_maps.append({
            "ctx_my": np.ascontiguousarray(ctx_full[:, c * TPC:(c + 1) * TPC]),
            "x": x[c * TPC:(c + 1) * TPC],
            "ffn": ffn_shards[c],
            "w_out": w_out,
            "bout2": np.ascontiguousarray(b_out.reshape(NFC, 128).T),
        })
    res4 = run_bass_kernel_spmd(nc4, l3_maps, core_ids=cores)
    y = np.concatenate([np.asarray(r["y"]) for r in res4.results], axis=0)
    return y.reshape(1, S, H)



def build_fused():
    nc = bass.Bass("TRN2", target_bir_lowering=False)
    x_in = nc.dram_tensor("x", [TPC, H], F32, kind="ExternalInput")
    wqkv = nc.dram_tensor("wqkv", [H, 768], F32, kind="ExternalInput")
    bqkv2 = nc.dram_tensor("bqkv2", [128, 6], F32, kind="ExternalInput")
    w_out = nc.dram_tensor("w_out", [H, H], F32, kind="ExternalInput")
    bout2 = nc.dram_tensor("bout2", [128, NFC], F32, kind="ExternalInput")
    su_ug2 = nc.dram_tensor("su_ug2", [128, NFC], F32, kind="ExternalInput")
    su_dp2 = nc.dram_tensor("su_dp2", [128, NIC], F32, kind="ExternalInput")
    wsu2 = nc.dram_tensor("wsu2", [128, NIC], F32, kind="ExternalInput")
    wsg2 = nc.dram_tensor("wsg2", [128, NIC], F32, kind="ExternalInput")
    wsdp2 = nc.dram_tensor("wsdp2", [128, NFC], F32, kind="ExternalInput")
    w_ugT = nc.dram_tensor("w_ugT", [H, 2 * INTER], F32, kind="ExternalInput")
    w_downT = nc.dram_tensor("w_downT", [INTER, H], F32, kind="ExternalInput")
    y_out = nc.dram_tensor("y", [TPC, H], F32, kind="ExternalOutput")

    h128f = nc.inline_tensor(_hadamard(128).astype(np.float32), name="h128f")
    mask_inl = _mask_inline(nc)

    ag0_in = nc.dram_tensor("ag0_in", [H, TPC], BF16, kind="Internal")
    ag0_out = nc.dram_tensor("ag0_out", [NCORE * H, TPC], BF16, kind="Internal",
                             addr_space="Shared")
    _a2a_shapes = [TPC, TPC]
    ag1_ins = [nc.dram_tensor(f"ag1_in{i}", [NCORE * 128, w], BF16, kind="Internal")
               for i, w in enumerate(_a2a_shapes)]
    ag1_outs = [nc.dram_tensor(f"ag1_out{i}", [NCORE * 128, w], BF16, kind="Internal")
                for i, w in enumerate(_a2a_shapes)]
    ffn_dram = nc.dram_tensor("ffn_sp", [H, TPC], F32, kind="Internal")

    import contextlib
    with contextlib.ExitStack() as es:
        cc = es.enter_context(nc.semaphore("cc"))
        ag0st = es.enter_context(nc.semaphore("ag0st"))
        esf = es.enter_context(contextlib.ExitStack())
        d = _ffn_persist_alloc(nc, esf)
        esu = es.enter_context(contextlib.ExitStack())
        d["tbl_ug"] = esu.enter_context(nc.sbuf_tensor("tbl_ug", [128, NFC * 128], BF16))
        with nc.named_scope("tables"):
            emit_ffn_tables(nc, d, su_ug2, su_dp2, wsu2, wsg2, wsdp2, h128f)
        def _trigger_ag0(g):
            g.wait_ge(ag0st, 16)
            g.collective_compute(
                "AllGather", ALU.bypass,
                replica_groups=[list(range(NCORE))],
                ins=[ag0_in.ap().opt()], outs=[ag0_out.ap().opt()],
            ).then_inc(cc, 1)

        with nc.sbuf_tensor("xn_sb", [128, NFC * TPC], BF16) as xn_sb:
            with nc.named_scope("ln"):
                emit_ln(nc, x_in, xn_sb)
            with nc.Block(no_gpsimd_drain=True) as block:
                @block.gpsimd
                def _(g):
                    g.dma_start(
                        out=AP2(ag0_in, 0, [[TPC, 128], [128 * TPC, NFC], [1, TPC]]),
                        in_=AP2(xn_sb, 0, [[NFC * TPC, 128], [TPC, NFC], [1, TPC]])
                    ).then_inc(ag0st, 16)
            with nc.named_scope("yhad"):
                emit_ffn_yhad(nc, d, xn_sb)
        esu.close()
        with nc.named_scope("ffn"):
            emit_ffn_main(nc, d, w_ugT, w_downT, ffn_dram,
                          pre_gpsimd=_trigger_ag0)
        esf.close()
        qk_sb = es.enter_context(nc.sbuf_tensor("qk_sb", [128, 2 * HPC * S], BF16))
        wl2e = es.enter_context(nc.semaphore("wl2e"))
        with contextlib.ExitStack() as es2:
            E = es2.enter_context
            ctx_sb = E(nc.sbuf_tensor("ctx_sb", [128, HPC * S], BF16))
            v_tm = E(nc.sbuf_tensor("v_tm", [128, HPC * S], BF16))
            with nc.named_scope("attn"):
                emit_attention(nc, ag0_out, wqkv, bqkv2, None, mask_inl,
                               qk_sb, v_tm, ctx_sb,
                               gate=(cc, 1), a2a=(ag1_ins, ag1_outs, cc),
                               wo_early=(w_out, wl2e))
        with nc.named_scope("final"):
            emit_final(nc, ag1_outs, x_in, ffn_dram, w_out, bout2, y_out,
                       gate=(cc, 3), wo_pre=(qk_sb, wl2e))
    return nc


def _host_maps(x, w_qkv, b_qkv, w_out, b_out, su_ug, su_dp, ws_ug, ws_dp,
               w_upT, w_gateT, w_downT):
    maps = []
    head_order = [2 * r for r in range(NCORE)] + [2 * r + 1 for r in range(NCORE)]
    feat_perm = np.concatenate([np.arange(128) + hd * 128 for hd in head_order])
    w_out = np.ascontiguousarray(w_out[feat_perm, :])
    WB = 128
    w_ugT = np.ascontiguousarray(
        np.stack([w_upT.reshape(H, INTER // WB, WB),
                  w_gateT.reshape(H, INTER // WB, WB)], axis=2).reshape(H, 2 * INTER))
    su_ug2 = np.ascontiguousarray(su_ug.reshape(NFC, 128).T)
    su_dp2 = np.ascontiguousarray(su_dp.reshape(NIC, 128).T)
    wsu2 = np.ascontiguousarray(ws_ug[:INTER].reshape(NIC, 128).T)
    wsg2 = np.ascontiguousarray(ws_ug[INTER:].reshape(NIC, 128).T)
    wsdp2 = np.ascontiguousarray(ws_dp.reshape(NFC, 128).T)
    bout2 = np.ascontiguousarray(b_out.reshape(NFC, 128).T)
    for c in range(NCORE):
        maps.append({
            "x": x[c * TPC:(c + 1) * TPC],
            "wqkv": np.ascontiguousarray(w_qkv[:, 768 * c:768 * (c + 1)]),
            "bqkv2": np.ascontiguousarray(
                b_qkv[768 * c:768 * (c + 1)].reshape(6, 128).T),
            "w_out": w_out, "bout2": bout2,
            "su_ug2": su_ug2, "su_dp2": su_dp2, "wsu2": wsu2, "wsg2": wsg2,
            "wsdp2": wsdp2,
            "w_ugT": w_ugT, "w_downT": w_downT,
        })
    return maps


def kernel_fused(L0_attn_hidden_states, L0_attn_w_qkv, L0_attn_b_qkv,
                 L0_attn_w_out, L0_attn_b_out, L0_ffn_SU_ug, L0_ffn_SU_dp,
                 L0_ffn_Wscale_ug, L0_ffn_Wscale_dp, L0_ffn_W_up,
                 L0_ffn_W_gate, L0_ffn_W_down):
    x = np.ascontiguousarray(
        np.asarray(L0_attn_hidden_states, dtype=np.float32).reshape(S, H))
    nc = build_fused()
    maps = _host_maps(
        x,
        np.asarray(L0_attn_w_qkv, dtype=np.float32),
        np.asarray(L0_attn_b_qkv, dtype=np.float32),
        np.ascontiguousarray(np.asarray(L0_attn_w_out, dtype=np.float32)),
        np.asarray(L0_attn_b_out, dtype=np.float32),
        np.asarray(L0_ffn_SU_ug, dtype=np.float32),
        np.asarray(L0_ffn_SU_dp, dtype=np.float32),
        np.asarray(L0_ffn_Wscale_ug, dtype=np.float32),
        np.asarray(L0_ffn_Wscale_dp, dtype=np.float32),
        np.ascontiguousarray(np.asarray(L0_ffn_W_up, dtype=np.float32).T),
        np.ascontiguousarray(np.asarray(L0_ffn_W_gate, dtype=np.float32).T),
        np.ascontiguousarray(np.asarray(L0_ffn_W_down, dtype=np.float32).T))
    res = run_bass_kernel_spmd(nc, maps, core_ids=list(range(NCORE)))
    y = np.concatenate([np.asarray(r["y"]) for r in res.results], axis=0)
    return y.reshape(1, S, H)


kernel = kernel_fused


# version 14
# speedup vs baseline: 1.0515x; 1.0515x over previous
import numpy as np
import ml_dtypes

import concourse.bass as bass
import concourse.mybir as mybir
from concourse.bass_utils import run_bass_kernel_spmd

F32 = mybir.dt.float32
BF16 = mybir.dt.bfloat16
BF = ml_dtypes.bfloat16
AF = mybir.ActivationFunctionType
ALU = mybir.AluOpType
AX = mybir.AxisListType

NCORE = 8
S = 8192
H = 2048
NH = 16
HD = 128
INTER = 8192
TPC = S // NCORE
HPC = NH // NCORE
NFC = H // 128
NIC = INTER // 128
MASKVAL = -1.0e9
EPS = 1e-5


def _hadamard(n):
    h = np.array([[1.0]], dtype=np.float64)
    while h.shape[0] < n:
        h = np.block([[h, h], [h, -h]])
    return h


def AP2(t, off, dims):
    return bass.AP(t, int(off), [list(d) for d in dims])



def build_l1():
    nc = bass.Bass("TRN2", target_bir_lowering=False)
    x_in = nc.dram_tensor("x", [TPC, H], F32, kind="ExternalInput")
    xn_out = nc.dram_tensor("xn_fm", [H, TPC], BF16, kind="ExternalOutput")
    with nc.sbuf_tensor("xnf_sb", [128, NFC * TPC], BF16) as xn_fm:
        emit_ln(nc, x_in, xn_fm)
        with nc.semaphore("l1_st") as st, nc.Block(no_gpsimd_drain=True) as block:
            @block.sync
            def _(s):
                s.dma_start(out=AP2(xn_out, 0, [[TPC, 128], [128 * TPC, NFC], [1, TPC]]),
                            in_=AP2(xn_fm, 0, [[NFC * TPC, 128], [TPC, NFC], [1, TPC]])
                            ).then_inc(st, 16)
                s.wait_ge(st, 16)
    return nc


def emit_ln(nc, x_in, xn_fm):
    ident = nc.inline_tensor(np.eye(128, dtype=BF), name="ident")

    ntc = TPC // 128
    with (
        nc.sbuf_tensor("xbuf", [128, 2 * H], F32) as xbuf,
        nc.sbuf_tensor("sq", [128, H], F32) as sq,
        nc.sbuf_tensor("stats", [128, 8], F32) as stats,
        nc.sbuf_tensor("xn_tm", [128, 2 * H], BF16) as xn_tm,
        nc.sbuf_tensor("ident_sb", [128, 128], BF16) as ident_sb,
        nc.psum_tensor("tpA", [128, 1024], BF16) as tpA,
        nc.psum_tensor("tpB", [128, 1024], BF16) as tpB,
        nc.semaphore("ld0") as ld0,
        nc.semaphore("ld1") as ld1,
        nc.semaphore("idl") as idl,
        nc.semaphore("vst") as vst,
        nc.semaphore("sst") as sst,
        nc.semaphore("tmm") as tmm,
        nc.semaphore("tev") as tev,
        nc.semaphore("vsy") as vsy,
        nc.semaphore("st") as st,
        nc.Block(no_gpsimd_drain=True) as block,
    ):
        lds = [ld0, ld1]
        tp = [tpA, tpB]

        @block.sync
        def _(s):
            s.dma_start(out=ident_sb[:, :], in_=ident.ap()).then_inc(idl, 16)
            for tc in range(ntc):
                if tc >= 2:
                    s.wait_ge(tmm, (tc - 1) * 16)
                s.dma_start(out=xbuf[:, (tc % 2) * H:(tc % 2) * H + H],
                            in_=x_in[tc * 128:(tc + 1) * 128, :]
                            ).then_inc(lds[tc % 2], 16)

        @block.scalar
        def _(sc):
            for tc in range(ntc):
                sl = (tc % 2) * H
                sc.wait_ge(lds[tc % 2], (tc // 2 + 1) * 16)
                sc.activation(sq[:, :], xbuf[:, sl:sl + H], AF.Square,
                              accum_out=stats[:, 2:3]).then_inc(sst, 1)
                sc.wait_ge(vst, 5 * tc + 3)
                sc.activation(stats[:, 4:5], stats[:, 3:4], AF.Sqrt).then_inc(sst, 1)

        @block.vector
        def _(v):
            vsn = [0]

            def vsy_inc(inst):
                vsn[0] += 1
                inst.then_inc(vsy, 1)

            def vsy_wait():
                v.wait_ge(vsy, vsn[0])

            for tc in range(ntc):
                sl = (tc % 2) * H
                v.wait_ge(lds[tc % 2], (tc // 2 + 1) * 16)
                vsy_inc(v.reduce_sum(stats[:, 0:1], xbuf[:, sl:sl + H], axis=AX.X))
                v.wait_ge(sst, 2 * tc + 1)
                vsy_wait()
                vsy_inc(v.tensor_scalar_mul(stats[:, 1:2], stats[:, 0:1], 1.0 / H))
                vsy_wait()
                v.tensor_mul(stats[:, 5:6], stats[:, 1:2], stats[:, 1:2])
                vsy_inc(v.tensor_scalar(stats[:, 3:4], stats[:, 2:3], 1.0 / H, EPS,
                                        ALU.mult, ALU.add))
                vsy_wait()
                v.tensor_sub(stats[:, 3:4], stats[:, 3:4], stats[:, 5:6]
                             ).then_inc(vst, 3)
                v.wait_ge(sst, 2 * tc + 2)
                vsy_inc(v.reciprocal(stats[:, 6:7], stats[:, 4:5]))
                vsy_wait()
                v.tensor_scalar(xn_tm[:, sl:sl + H], xbuf[:, sl:sl + H],
                                stats[:, 1:2], stats[:, 6:7],
                                ALU.subtract, ALU.mult).then_inc(vst, 2)
                if tc >= 1:
                    for half in range(2):
                        v.wait_ge(tmm, (tc - 1) * 16 + (half + 1) * 8)
                        dst = AP2(xn_fm, (half * 8) * TPC + (tc - 1) * 128,
                                  [[NFC * TPC, 128], [TPC, 8], [1, 128]])
                        v.tensor_copy(dst, tp[half][:, 0:1024]).then_inc(tev, 1)
            for half in range(2):
                v.wait_ge(tmm, (ntc - 1) * 16 + (half + 1) * 8)
                dst = AP2(xn_fm, (half * 8) * TPC + (ntc - 1) * 128,
                          [[NFC * TPC, 128], [TPC, 8], [1, 128]])
                v.tensor_copy(dst, tp[half][:, 0:1024]).then_inc(tev, 1)

        @block.tensor
        def _(t):
            t.wait_ge(idl, 16)
            for tc in range(ntc):
                sl = (tc % 2) * H
                t.wait_ge(vst, 5 * tc + 5)
                for half in range(2):
                    if tc >= 1:
                        t.wait_ge(tev, (tc - 1) * 2 + half + 1)
                    for j in range(8):
                        fc = half * 8 + j
                        inst = t.transpose(tp[half][:, j * 128:(j + 1) * 128],
                                           xn_tm[:, sl + fc * 128:sl + (fc + 1) * 128],
                                           ident_sb[:, :])
                    inst.then_inc(tmm, 8)






def _ffn_persist_alloc(nc, es):
    d = {}
    d["tbl_dp"] = es.enter_context(nc.sbuf_tensor("tbl_dp", [128, NIC * 128], BF16))
    d["wsg"] = es.enter_context(nc.sbuf_tensor("wsg", [128, NIC], F32))
    d["wsdp2"] = es.enter_context(nc.sbuf_tensor("wsdp2_sb", [128, NFC], F32))
    d["y_dram"] = nc.dram_tensor("y_dram", [H, TPC], BF16, kind="Internal")
    return d


def emit_ffn_tables(nc, d, su_ug2, su_dp2, wsu2, wsg2, wsdp2_in, h128f):
    with (
        nc.sbuf_tensor("t_suug", [128, NFC], F32) as suug,
        nc.sbuf_tensor("t_sudp", [128, NIC], F32) as sudp,
        nc.sbuf_tensor("t_wsu", [128, NIC], F32) as wsu,
        nc.sbuf_tensor("t_h128", [128, 128], F32) as h128,
        nc.sbuf_tensor("t_scomb", [128, NIC], F32) as scomb,
        nc.semaphore("tb_ld") as ld,
        nc.semaphore("tb_v") as vs,
    ):
        with nc.Block(no_gpsimd_drain=True) as block:
            @block.sync
            def _(s):
                s.dma_start(out=suug[:, :], in_=su_ug2[:, :]).then_inc(ld, 16)
                s.dma_start(out=sudp[:, :], in_=su_dp2[:, :]).then_inc(ld, 16)
                s.dma_start(out=wsu[:, :], in_=wsu2[:, :]).then_inc(ld, 16)
                s.dma_start(out=d["wsg"][:, :], in_=wsg2[:, :]).then_inc(ld, 16)
                s.dma_start(out=d["wsdp2"][:, :], in_=wsdp2_in[:, :]).then_inc(ld, 16)
                s.dma_start(out=h128[:, :], in_=h128f.ap()).then_inc(ld, 16)

            @block.vector
            def _(v):
                v.wait_ge(ld, 96)
                vseq = [0]

                def stamp(inst):
                    vseq[0] += 1
                    inst.then_inc(vs, 1)

                stamp(v.tensor_scalar_mul(suug[:, :], suug[:, :], 1.0 / np.sqrt(2048.0)))
                v.wait_ge(vs, vseq[0])
                for c in range(NFC):
                    v.tensor_scalar(d["tbl_ug"][:, c * 128:(c + 1) * 128], h128[:, :],
                                    suug[:, c:c + 1], None, ALU.mult)
                stamp(v.tensor_mul(scomb[:, :], sudp[:, :], wsu[:, :]))
                v.wait_ge(vs, vseq[0])
                stamp(v.tensor_scalar_mul(scomb[:, :], scomb[:, :], 1.0 / np.sqrt(8192.0)))
                v.wait_ge(vs, vseq[0])
                for c in range(NIC):
                    v.tensor_scalar(d["tbl_dp"][:, c * 128:(c + 1) * 128], h128[:, :],
                                    scomb[:, c:c + 1], None, ALU.mult)


def emit_ffn_yhad(nc, d, xn_own_sb, pre_gpsimd=None):
    with (
        nc.sbuf_tensor("yz_a", [128, NFC * 512], F32) as za,
        nc.sbuf_tensor("yz_b", [128, NFC * 512], F32) as zb,
        nc.sbuf_tensor("y_sb", [128, NFC * TPC], BF16) as y_sb,
        nc.psum_tensor("y_ps0", [128, 512], F32) as yp0,
        nc.psum_tensor("y_ps1", [128, 512], F32) as yp1,
        nc.semaphore("yh_m") as mm,
        nc.semaphore("yh_e") as ev,
        nc.semaphore("yh_v") as vs,
        nc.semaphore("yh_st") as yst,
        nc.Block(no_gpsimd_drain=True) as block,
    ):
        yp = [yp0, yp1]

        if pre_gpsimd is not None:
            @block.gpsimd
            def _(g):
                pre_gpsimd(g)

        @block.tensor
        def _(t):
            for th in range(2):
                for c in range(NFC):
                    i = th * NFC + c
                    if i >= 2:
                        t.wait_ge(ev, i - 1)
                    t.matmul(yp[i % 2][:, :], d["tbl_ug"][:, c * 128:(c + 1) * 128],
                             xn_own_sb[:, c * TPC + th * 512: c * TPC + th * 512 + 512],
                             start=True, stop=True).then_inc(mm, 1)

        @block.vector
        def _(v):
            vseq = [0]

            def stamp(inst):
                vseq[0] += 1
                inst.then_inc(vs, 1)

            for th in range(2):
                for c in range(NFC):
                    i = th * NFC + c
                    v.wait_ge(mm, i + 1)
                    v.tensor_copy(za[:, c * 512:(c + 1) * 512], yp[i % 2][:, :]
                                  ).then_inc(ev, 1)
                v.wait_ge(ev, (th + 1) * NFC)
                src, dst = za, zb
                for si, dd in enumerate([1, 2, 4, 8]):
                    last = si == 3
                    nq = NFC // (2 * dd)
                    sA = AP2(src, 0, [[NFC * 512, 128], [2 * dd * 512, nq], [512, dd], [1, 512]])
                    sB = AP2(src, dd * 512, [[NFC * 512, 128], [2 * dd * 512, nq], [512, dd], [1, 512]])
                    if last:
                        dlo = AP2(y_sb, th * 512, [[NFC * TPC, 128], [2 * dd * TPC, nq], [TPC, dd], [1, 512]])
                        dhi = AP2(y_sb, dd * TPC + th * 512, [[NFC * TPC, 128], [2 * dd * TPC, nq], [TPC, dd], [1, 512]])
                    else:
                        dlo = AP2(dst, 0, [[NFC * 512, 128], [2 * dd * 512, nq], [512, dd], [1, 512]])
                        dhi = AP2(dst, dd * 512, [[NFC * 512, 128], [2 * dd * 512, nq], [512, dd], [1, 512]])
                    v.tensor_tensor(dlo, sA, sB, ALU.add)
                    stamp(v.tensor_tensor(dhi, sA, sB, ALU.subtract))
                    v.wait_ge(vs, vseq[0])
                    src, dst = dst, src

        @block.sync
        def _(s):
            s.wait_ge(vs, 8)
            s.dma_start(out=AP2(d["y_dram"], 0, [[TPC, 128], [128 * TPC, NFC], [1, TPC]]),
                        in_=AP2(y_sb, 0, [[NFC * TPC, 128], [TPC, NFC], [1, TPC]])
                        ).then_inc(yst, 16)
            s.wait_ge(yst, 16)


def emit_ffn_main(nc, d, w_ugT, w_downT, ffn_out, pre_gpsimd=None):
    WB = 128
    NB = INTER // WB
    import contextlib
    with contextlib.ExitStack() as hs:
        hbuf = [hs.enter_context(nc.sbuf_tensor(f"f_h{t}", [128, NIC * 512], BF16))
                for t in range(2)]
        z1a = hs.enter_context(nc.sbuf_tensor("f_z1a", [128, NIC * 128], BF16))
        yld = hs.enter_context(nc.semaphore("f_yld"))
        ys = hs.enter_context(contextlib.ExitStack())
        yh = ys.enter_context(nc.sbuf_tensor("f_yh", [128, NFC * 512], BF16))
        nyld = [0]

        def ug_scope(es, th):
            E = es.enter_context
            g = {}
            g["wub"] = E(nc.sbuf_tensor(f"f_wu{th}", [128, 3 * NFC * 2 * WB], BF16))
            g["gb"] = E(nc.sbuf_tensor(f"f_g{th}", [128, 512], F32))
            g["pu"] = [E(nc.psum_tensor(f"f_pu{j}{th}", [128, 512], F32)) for j in range(3)]
            g["pg"] = [E(nc.psum_tensor(f"f_pg{j}{th}", [128, 512], F32)) for j in range(3)]
            g["wlu"] = [E(nc.semaphore(f"f_wlu{j}{th}")) for j in range(3)]
            g["mmu"] = E(nc.semaphore(f"f_mmu{th}"))
            g["mmg"] = E(nc.semaphore(f"f_mmg{th}"))
            g["sil"] = E(nc.semaphore(f"f_sil{th}"))
            g["hdn"] = E(nc.semaphore(f"f_hdn{th}"))
            return g

        def fw_scope(es, th):
            E = es.enter_context
            f = {}
            f["s1p"] = [E(nc.psum_tensor(f"f_s1p{j}{th}", [128, 512], F32)) for j in range(2)]
            f["s1m"] = E(nc.semaphore(f"f_s1m{th}"))
            f["s1e"] = E(nc.semaphore(f"f_s1e{th}"))
            f["bfv"] = E(nc.semaphore(f"f_bfv{th}"))
            return f

        def dn_scope(es, th):
            E = es.enter_context
            n = {}
            n["wdn"] = E(nc.sbuf_tensor(f"f_wd{th}", [128, 2 * NIC * 128], BF16))
            n["slab"] = E(nc.sbuf_tensor(f"f_sl{th}", [128, 2 * 512], F32))
            n["dp"] = [E(nc.psum_tensor(f"f_dp{j}{th}", [128, 512], F32)) for j in range(2)]
            n["wdl"] = [[E(nc.semaphore(f"f_wdl{j}{k}{th}")) for k in range(2)]
                        for j in range(2)]
            n["dmm"] = E(nc.semaphore(f"f_dmm{th}"))
            n["dev"] = E(nc.semaphore(f"f_dev{th}"))
            n["fst"] = [E(nc.semaphore(f"f_fst{j}{th}")) for j in range(2)]
            return n

        def ug_gpsimd_dma(gps, g, th, ib):
            slot = ib % 3
            if ib >= 3:
                gps.wait_ge(g["mmg"], ib - 2)
            gps.dma_start(
                out=AP2(g["wub"], slot * NFC * 2 * WB,
                        [[3 * NFC * 2 * WB, 128], [2 * WB, NFC], [1, 2 * WB]]),
                in_=AP2(w_ugT, ib * 2 * WB,
                        [[2 * INTER, 128], [128 * 2 * INTER, NFC], [1, 2 * WB]])
            ).then_inc(g["wlu"][slot], 16)

        def ug_tensor_chunk(t, g, th, c):
            ib, ps, slot = c, c % 3, c % 3
            t.wait_ge(yld, 16 * nyld[0])
            t.wait_ge(g["wlu"][slot], 16 * (ib // 3 + 1))
            if c >= 3:
                t.wait_ge(g["hdn"], c - 2)
                t.wait_ge(g["sil"], c - 2)
            base = slot * NFC * 2 * WB
            for hc in range(NFC):
                inst = t.matmul(g["pu"][ps][:, :],
                                g["wub"][:, base + hc * 2 * WB: base + hc * 2 * WB + 128],
                                yh[:, hc * 512:(hc + 1) * 512],
                                start=(hc == 0), stop=(hc == NFC - 1))
            inst.then_inc(g["mmu"], 1)
            for hc in range(NFC):
                inst = t.matmul(g["pg"][ps][:, :],
                                g["wub"][:, base + hc * 2 * WB + WB: base + hc * 2 * WB + WB + 128],
                                yh[:, hc * 512:(hc + 1) * 512],
                                start=(hc == 0), stop=(hc == NFC - 1))
            inst.then_inc(g["mmg"], 1)

        def ug_scalar_chunk(sc, g, th, c):
            sc.wait_ge(g["mmg"], c + 1)
            if c >= 1:
                sc.wait_ge(g["hdn"], c)
            sc.activation(g["gb"][:, :], g["pg"][c % 3][:, :], AF.Silu,
                          scale=d["wsg"][:, c:c + 1]).then_inc(g["sil"], 1)

        def ug_vector_chunk(v, g, th, c):
            v.wait_ge(g["mmu"], c + 1)
            v.wait_ge(g["sil"], c + 1)
            v.tensor_mul(hbuf[th][:, c * 512:(c + 1) * 512],
                         g["gb"][:, :], g["pu"][c % 3][:, :]).then_inc(g["hdn"], 1)

        def hslab(th, e, chunk_inner=128):
            return (hbuf[th], e * 128, 512)

        def bf_ap(buf_off_stride, dd, hi, cnt=128):
            buf, off, cs = buf_off_stride
            nq = NIC // (2 * dd)
            base = off + (dd * cs if hi else 0)
            return AP2(buf, base, [[NIC * 512 if cs == 512 else NIC * 128, 128],
                                   [2 * dd * cs, nq], [cs, dd], [1, cnt]])

        def fw_tensor_eighth(t, f, th, e):
            for grp in range(16):
                gg = e * 16 + grp
                if gg >= 2:
                    t.wait_ge(f["s1e"], gg - 1)
                for k in range(4):
                    c = grp * 4 + k
                    inst = t.matmul(
                        f["s1p"][gg % 2][:, k * 128:(k + 1) * 128],
                        d["tbl_dp"][:, c * 128:(c + 1) * 128],
                        hbuf[th][:, c * 512 + e * 128: c * 512 + e * 128 + 128],
                        start=True, stop=True)
                inst.then_inc(f["s1m"], 1)

        def fw_scalar_eighth(sc, f, th, e):
            for grp in range(16):
                gg = e * 16 + grp
                sc.wait_ge(f["s1m"], gg + 1)
                dst = AP2(hbuf[th], grp * 4 * 512 + e * 128,
                          [[NIC * 512, 128], [512, 4], [1, 128]])
                sc.activation(dst, f["s1p"][gg % 2][:, :], AF.Copy).then_inc(f["s1e"], 1)

        def fw_vector_gen(v, f, th, e):
            v.wait_ge(f["s1e"], (e + 1) * 16)
            hs_ = hslab(th, e)
            za_ = (z1a, 0, 128)
            src, dst = hs_, za_
            for si, dd in enumerate([1, 2, 4, 8, 16, 32]):
                sA = bf_ap(src, dd, False)
                sB = bf_ap(src, dd, True)
                dlo = bf_ap(dst, dd, False)
                dhi = bf_ap(dst, dd, True)
                v.tensor_tensor(dlo, sA, sB, ALU.add)
                yield
                v.tensor_tensor(dhi, sA, sB, ALU.subtract).then_inc(f["bfv"], 1)
                v.wait_ge(f["bfv"], e * 6 + si + 1)
                yield
                src, dst = dst, src

        def fw_vector_eighth(v, f, th, e):
            v.wait_ge(f["s1e"], (e + 1) * 16)
            hs_ = hslab(th, e)
            za_ = (z1a, 0, 128)
            src, dst = hs_, za_
            for si, dd in enumerate([1, 2, 4, 8, 16, 32]):
                sA = bf_ap(src, dd, False)
                sB = bf_ap(src, dd, True)
                dlo = bf_ap(dst, dd, False)
                dhi = bf_ap(dst, dd, True)
                v.tensor_tensor(dlo, sA, sB, ALU.add)
                v.tensor_tensor(dhi, sA, sB, ALU.subtract).then_inc(f["bfv"], 1)
                v.wait_ge(f["bfv"], e * 6 + si + 1)
                src, dst = dst, src

        def dn_gpsimd_dma(gps, n, th, hc):
            slot = hc % 2
            if hc >= 2:
                gps.wait_ge(n["dmm"], hc - 1)
            for half in range(2):
                gps.dma_start(
                    out=AP2(n["wdn"], slot * NIC * 128 + half * (NIC // 2) * 128,
                            [[2 * NIC * 128, 128], [128, NIC // 2], [1, 128]]),
                    in_=AP2(w_downT, hc * 128 + half * (NIC // 2) * 128 * H,
                            [[H, 128], [128 * H, NIC // 2], [1, 128]])
                ).then_inc(n["wdl"][slot][half], 16)

        def dn_tensor_hc(t, n, th, hc):
            slot = hc % 2
            if hc >= 2:
                t.wait_ge(n["dev"], hc - 1)
            for c in range(NIC):
                if c == 0:
                    t.wait_ge(n["wdl"][slot][0], 16 * (hc // 2 + 1))
                elif c == NIC // 2:
                    t.wait_ge(n["wdl"][slot][1], 16 * (hc // 2 + 1))
                inst = t.matmul(
                    n["dp"][hc % 2][:, :],
                    n["wdn"][:, slot * NIC * 128 + c * 128: slot * NIC * 128 + (c + 1) * 128],
                    hbuf[th][:, c * 512:(c + 1) * 512],
                    start=(c == 0), stop=(c == NIC - 1))
            inst.then_inc(n["dmm"], 1)

        def dn_scalar_hc(sc, n, th, hc):
            sc.wait_ge(n["dmm"], hc + 1)
            if hc >= 2:
                sc.wait_ge(n["fst"][hc % 2], 16 * (hc // 2))
            sc.activation(n["slab"][:, (hc % 2) * 512:(hc % 2) * 512 + 512],
                          n["dp"][hc % 2][:, :], AF.Identity,
                          scale=d["wsdp2"][:, hc:hc + 1]).then_inc(n["dev"], 1)

        def dn_sync_hc(s, n, th, hc):
            s.wait_ge(n["dev"], hc + 1)
            s.dma_start(
                out=ffn_out[hc * 128:(hc + 1) * 128, th * 512:(th + 1) * 512],
                in_=n["slab"][:, (hc % 2) * 512:(hc % 2) * 512 + 512]
            ).then_inc(n["fst"][hc % 2], 16)

        def load_yh(s, th):
            s.dma_start(out=AP2(yh, 0, [[NFC * 512, 128], [512, NFC], [1, 512]]),
                        in_=AP2(d["y_dram"], th * 512,
                                [[TPC, 128], [128 * TPC, NFC], [1, 512]])
                        ).then_inc(yld, 16)
            nyld[0] += 1

        with contextlib.ExitStack() as es:
            g0 = ug_scope(es, 0)
            block = es.enter_context(nc.Block(no_gpsimd_drain=True))

            @block.gpsimd
            def _(gps):
                if pre_gpsimd is not None:
                    pre_gpsimd(gps)
                for ib in range(NB):
                    ug_gpsimd_dma(gps, g0, 0, ib)

            @block.sync
            def _(s):
                load_yh(s, 0)

            @block.tensor
            def _(t):
                for c in range(NIC):
                    ug_tensor_chunk(t, g0, 0, c)

            @block.scalar
            def _(sc):
                for c in range(NIC):
                    ug_scalar_chunk(sc, g0, 0, c)

            @block.vector
            def _(v):
                for c in range(NIC):
                    ug_vector_chunk(v, g0, 0, c)

        with contextlib.ExitStack() as es:
            g1 = ug_scope(es, 1)
            f0 = fw_scope(es, 0)
            block = es.enter_context(nc.Block(no_gpsimd_drain=True))

            @block.sync
            def _(s):
                load_yh(s, 1)

            @block.gpsimd
            def _(gps):
                for ib in range(NB):
                    ug_gpsimd_dma(gps, g1, 1, ib)

            @block.tensor
            def _(t):
                for e in range(4):
                    fw_tensor_eighth(t, f0, 0, e)
                    for c in range(e * 16, (e + 1) * 16):
                        ug_tensor_chunk(t, g1, 1, c)

            @block.scalar
            def _(sc):
                for e in range(4):
                    fw_scalar_eighth(sc, f0, 0, e)
                    for c in range(e * 16, (e + 1) * 16):
                        ug_scalar_chunk(sc, g1, 1, c)

            @block.vector
            def _(v):
                for e in range(4):
                    bf_gen = fw_vector_gen(v, f0, 0, e - 1) if e >= 1 else None
                    for c in range(e * 16, (e + 1) * 16):
                        ug_vector_chunk(v, g1, 1, c)
                        if bf_gen is not None:
                            next(bf_gen, None)
                    if bf_gen is not None:
                        for _ in bf_gen:
                            pass
                fw_vector_eighth(v, f0, 0, 3)

        ys.close()
        with contextlib.ExitStack() as es:
            f1 = fw_scope(es, 1)
            n0 = dn_scope(es, 0)
            block = es.enter_context(nc.Block(no_gpsimd_drain=True))

            @block.gpsimd
            def _(gps):
                for hc in range(NFC):
                    dn_gpsimd_dma(gps, n0, 0, hc)

            @block.tensor
            def _(t):
                for e in range(4):
                    fw_tensor_eighth(t, f1, 1, e)
                    for hc in range(e * 4, (e + 1) * 4):
                        dn_tensor_hc(t, n0, 0, hc)

            @block.scalar
            def _(sc):
                for e in range(4):
                    fw_scalar_eighth(sc, f1, 1, e)
                    for hc in range(e * 4, (e + 1) * 4):
                        dn_scalar_hc(sc, n0, 0, hc)

            @block.vector
            def _(v):
                for e in range(4):
                    fw_vector_eighth(v, f1, 1, e)

            @block.sync
            def _(s):
                for hc in range(NFC):
                    dn_sync_hc(s, n0, 0, hc)
                s.wait_ge(n0["fst"][0], 16 * NFC // 2)
                s.wait_ge(n0["fst"][1], 16 * NFC // 2)

        with contextlib.ExitStack() as es:
            n1 = dn_scope(es, 1)
            block = es.enter_context(nc.Block(no_gpsimd_drain=True))

            @block.gpsimd
            def _(gps):
                for hc in range(NFC):
                    dn_gpsimd_dma(gps, n1, 1, hc)

            @block.tensor
            def _(t):
                for hc in range(NFC):
                    dn_tensor_hc(t, n1, 1, hc)

            @block.scalar
            def _(sc):
                for hc in range(NFC):
                    dn_scalar_hc(sc, n1, 1, hc)

            @block.sync
            def _(s):
                for hc in range(NFC):
                    dn_sync_hc(s, n1, 1, hc)
                s.wait_ge(n1["fst"][0], 16 * NFC // 2)
                s.wait_ge(n1["fst"][1], 16 * NFC // 2)



def build_ffn():
    nc = bass.Bass("TRN2", target_bir_lowering=False)
    xn_in = nc.dram_tensor("xn_own", [H, TPC], BF16, kind="ExternalInput")
    su_ug2 = nc.dram_tensor("su_ug2", [128, NFC], F32, kind="ExternalInput")
    su_dp2 = nc.dram_tensor("su_dp2", [128, NIC], F32, kind="ExternalInput")
    wsu2 = nc.dram_tensor("wsu2", [128, NIC], F32, kind="ExternalInput")
    wsg2 = nc.dram_tensor("wsg2", [128, NIC], F32, kind="ExternalInput")
    wsdp2 = nc.dram_tensor("wsdp2", [128, NFC], F32, kind="ExternalInput")
    w_ugT = nc.dram_tensor("w_ugT", [H, 2 * INTER], F32, kind="ExternalInput")
    w_downT = nc.dram_tensor("w_downT", [INTER, H], F32, kind="ExternalInput")
    ffn_out = nc.dram_tensor("ffn_out", [H, TPC], F32, kind="ExternalOutput")
    h128f = nc.inline_tensor(_hadamard(128).astype(np.float32), name="h128f")

    import contextlib
    with contextlib.ExitStack() as es:
        d = _ffn_persist_alloc(nc, es)
        esu = es.enter_context(contextlib.ExitStack())
        d["tbl_ug"] = esu.enter_context(nc.sbuf_tensor("tbl_ug", [128, NFC * 128], BF16))
        with nc.sbuf_tensor("xn_sb", [128, NFC * TPC], BF16) as xn_sb:
            with (
                nc.semaphore("xl") as xl,
                nc.Block(no_gpsimd_drain=True) as block,
            ):
                @block.sync
                def _(s):
                    s.dma_start(out=AP2(xn_sb, 0, [[NFC * TPC, 128], [TPC, NFC], [1, TPC]]),
                                in_=AP2(xn_in, 0, [[TPC, 128], [128 * TPC, NFC], [1, TPC]])
                                ).then_inc(xl, 16)
                    s.wait_ge(xl, 16)
            emit_ffn_tables(nc, d, su_ug2, su_dp2, wsu2, wsg2, wsdp2, h128f)
            emit_ffn_yhad(nc, d, xn_sb)
        esu.close()
        emit_ffn_main(nc, d, w_ugT, w_downT, ffn_out)
    return nc



def emit_attention(nc, xn_full, wqkv, bqkv2, ctx_out, mask_inl, qk_sb, v_tm, ctx_sb,
                   debug_qkv=None, gate=None, a2a=None, wo_early=None):
    QOFF = 0
    KOFF = HPC * S
    NTB = S // 512
    NOC = HPC * 3
    import contextlib
    with contextlib.ExitStack() as qes:
        E = qes.enter_context
        wq_sb = E(nc.sbuf_tensor("a_wqkv", [128, NFC * 768], BF16))
        bq_sb = E(nc.sbuf_tensor("a_bqkv", [128, 6], F32))
        xnb = E(nc.sbuf_tensor("a_xnb", [128, 2 * NFC * 512], BF16))
        vstage = E(nc.sbuf_tensor("a_vst", [128, 2 * 512], BF16))
        wl = E(nc.semaphore("a_wl"))
        wl2 = E(nc.semaphore("a_wl2"))
        xl0 = E(nc.semaphore("a_xl0"))
        xl1 = E(nc.semaphore("a_xl1"))
        qmm = E(nc.semaphore("a_qmm"))
        qev = E(nc.semaphore("a_qev"))
        vtm = E(nc.semaphore("a_vtm"))
        vte = E(nc.semaphore("a_vte"))
        qp0 = E(nc.psum_tensor("a_qp0", [128, 512], F32))
        qp1 = E(nc.psum_tensor("a_qp1", [128, 512], F32))
        qp2 = E(nc.psum_tensor("a_qp2", [128, 512], F32))
        tp0 = E(nc.psum_tensor("a_tp0", [128, 1024], BF16))
        tp1 = E(nc.psum_tensor("a_tp1", [128, 1024], BF16))
        id_sb = E(nc.sbuf_tensor("a_id", [128, 128], BF16))
        block = E(nc.Block(no_gpsimd_drain=True))
        qp = [qp0, qp1, qp2]
        tpp = [tp0, tp1]
        xl = [xl0, xl1]

        @block.gpsimd
        def _(g):
            g.dma_start(out=AP2(wq_sb, 0, [[NFC * 768, 128], [768, NFC], [1, 768]]),
                        in_=AP2(wqkv, 0, [[768, 128], [128 * 768, NFC], [1, 768]])
                        ).then_inc(wl2, 16)

        @block.sync
        def _(s):
            s.dma_start(out=bq_sb[:, :], in_=bqkv2[:, :]).then_inc(wl, 16)
            s.dma_start(out=id_sb[:, :],
                        in_=nc_ident_ap(nc)).then_inc(wl, 16)
            if gate is not None:
                s.wait_ge(gate[0], gate[1])
            for tb in range(NTB):
                if tb >= 2:
                    s.wait_ge(qmm, (tb - 1) * NOC)
                r, half = tb // 2, tb % 2
                s.dma_start(
                    out=AP2(xnb, (tb % 2) * NFC * 512,
                            [[2 * NFC * 512, 128], [512, NFC], [1, 512]]),
                    in_=AP2(xn_full, r * H * TPC + half * 512,
                            [[TPC, 128], [128 * TPC, NFC], [1, 512]])
                ).then_inc(xl[tb % 2], 16)

        @block.tensor
        def _(t):
            t.wait_ge(wl, 32)
            t.wait_ge(wl2, 16)
            for tb in range(NTB):
                slot = tb % 2
                t.wait_ge(xl[slot], 16 * (tb // 2 + 1))
                for oc in range(NOC):
                    i = tb * NOC + oc
                    if i >= 3:
                        t.wait_ge(qev, i - 2)
                    for fc in range(NFC):
                        inst = t.matmul(
                            qp[i % 3][:, :],
                            wq_sb[:, fc * 768 + oc * 128: fc * 768 + (oc + 1) * 128],
                            xnb[:, slot * NFC * 512 + fc * 512: slot * NFC * 512 + (fc + 1) * 512],
                            start=(fc == 0), stop=(fc == NFC - 1))
                    inst.then_inc(qmm, 1)
                for h in range(HPC):
                    t.wait_ge(qev, tb * NOC + 3 * h + 3)
                    if tb >= 1:
                        t.wait_ge(vte, (tb - 1) * 2 + h + 1)
                    for j in range(4):
                        inst = t.transpose(tpp[h][:, j * 128:(j + 1) * 128],
                                           vstage[:, h * 512 + j * 128: h * 512 + (j + 1) * 128],
                                           id_sb[:, :])
                    inst.then_inc(vtm, 1)

        @block.scalar
        def _(sc):
            for tb in range(NTB):
                for oc in range(NOC):
                    i = tb * NOC + oc
                    h, kind = oc // 3, oc % 3
                    sc.wait_ge(qmm, i + 1)
                    if kind == 0:
                        dst = qk_sb[:, QOFF + h * S + tb * 512: QOFF + h * S + (tb + 1) * 512]
                    elif kind == 1:
                        dst = qk_sb[:, KOFF + h * S + tb * 512: KOFF + h * S + (tb + 1) * 512]
                    else:
                        if tb >= 1:
                            sc.wait_ge(vtm, (tb - 1) * 2 + h + 1)
                        dst = vstage[:, h * 512:(h + 1) * 512]
                    sc.activation(dst, qp[i % 3][:, :], AF.Identity,
                                  bias=bq_sb[:, oc:oc + 1]).then_inc(qev, 1)

        @block.vector
        def _(v):
            for tb in range(NTB):
                for h in range(HPC):
                    v.wait_ge(vtm, tb * 2 + h + 1)
                    v.tensor_copy(v_tm[:, h * S + tb * 512: h * S + (tb + 1) * 512],
                                  tpp[h][:, 0:512]).then_inc(vte, 1)

    if debug_qkv is not None:
        dq, dk, dv = debug_qkv
        with nc.semaphore("dbg_st") as dst_sem, nc.Block(no_gpsimd_drain=True) as block:
            @block.sync
            def _(s):
                s.dma_start(out=dq[:, :], in_=qk_sb[:, QOFF:QOFF + HPC * S]).then_inc(dst_sem, 16)
                s.dma_start(out=dk[:, :], in_=qk_sb[:, KOFF:KOFF + HPC * S]).then_inc(dst_sem, 16)
                s.dma_start(out=dv[:, :], in_=v_tm[:, :]).then_inc(dst_sem, 16)
                s.wait_ge(dst_sem, 48)
        return

    import contextlib
    with contextlib.ExitStack() as aes:
        E = aes.enter_context
        mask_sb = E(nc.sbuf_tensor("a_mask", [128, 1024], F32))
        pt_sb = E(nc.sbuf_tensor("a_pt", [128, 4 * 512], BF16))
        ones_col = E(nc.sbuf_tensor("a_ones", [128, 1], BF16))
        ones_k1 = E(nc.sbuf_tensor("a_onesk1", [1, 128], F32))
        l_sb = E(nc.sbuf_tensor("a_l", [1, 512], F32))
        l_inv = E(nc.sbuf_tensor("a_linv", [1, 512], F32))
        lbc_sb = E(nc.sbuf_tensor("a_lbc", [128, 512], F32))
        s4 = E(nc.psum_tensor("a_s4", [128, 4 * 512], F32))
        ctx_ps2 = [E(nc.psum_tensor(f"a_ctxp{j}", [128, 512], F32)) for j in range(2)]
        l_ps2 = [E(nc.psum_tensor(f"a_lp{j}", [128, 512], F32)) for j in range(2)]
        ml = E(nc.semaphore("a_ml"))
        alc = E(nc.semaphore("a_alc"))
        albr = E(nc.semaphore("a_albr"))
        amm = E(nc.semaphore("a_amm"))
        aex = E(nc.semaphore("a_aex"))
        amk = E(nc.semaphore("a_amk"))
        apv = E(nc.semaphore("a_apv"))
        alv = E(nc.semaphore("a_alv"))
        albc = E(nc.semaphore("a_albc"))
        alev = E(nc.semaphore("a_alev"))
        acev = E(nc.semaphore("a_acev"))
        cst = E(nc.semaphore("a_cst"))
        block = E(nc.Block(no_gpsimd_drain=True))
        groups = [(h, g) for h in range(HPC) for g in range(16)]

        gj_of = {}
        mask_count_before = {}
        gj = 0
        nmask = 0
        for (h, g) in groups:
            kmax = 4 * (g + 1)
            ids = []
            for j in range(kmax):
                diag = j >= kmax - 4
                ids.append((gj, j, diag, nmask))
                if diag:
                    nmask += 1
                gj += 1
            gj_of[(h, g)] = ids
        total_pv = gj

        @block.sync
        def _(s):
            s.dma_start(out=mask_sb[:, :], in_=mask_inl.ap()).then_inc(ml, 16)

        @block.vector
        def _(v):
            v.memset(ones_col[:, :], 1.0)
            v.memset(ones_k1[:, :], 1.0).then_inc(ml, 1)

        def _emit_pv(t, gi, rec, h, kmax):
            (cj, j, diag, _mc) = rec
            ctx_ps = ctx_ps2[gi % 2]
            l_ps = l_ps2[gi % 2]
            t.wait_ge(aex, (cj + 2) // 2 * 2)
            if j == 0 and gi >= 2:
                t.wait_ge(acev, gi - 1)
            if j == 0 and gi >= 1:
                t.wait_ge(alev, gi)
            t.matmul(ctx_ps[:, :],
                     v_tm[:, h * S + j * 128: h * S + (j + 1) * 128],
                     pt_sb[:, (cj % 4) * 512:(cj % 4 + 1) * 512],
                     start=(j == 0), stop=(j == kmax - 1))
            t.matmul(l_ps[0:1, :], ones_col[:, :],
                     pt_sb[:, (cj % 4) * 512:(cj % 4 + 1) * 512],
                     start=(j == 0), stop=(j == kmax - 1)).then_inc(apv, 1)

        flat = []
        nmask_thru = {}
        nm = 0
        for gi, (h, g) in enumerate(groups):
            ids = gj_of[(h, g)]
            for rec in ids:
                flat.append((gi, h, g, rec, len(ids)))
                if rec[2]:
                    nm += 1
                nmask_thru[rec[0]] = nm

        def _emit_s(t, item):
            (gi, h, g, (cj, j, diag, _mc), kmax) = item
            if cj >= 4:
                t.wait_ge(aex, ((cj - 3) + 1) // 2 * 2)
            t.matmul(s4[:, (cj % 4) * 512:(cj % 4 + 1) * 512],
                     qk_sb[:, KOFF + h * S + j * 128: KOFF + h * S + (j + 1) * 128],
                     qk_sb[:, QOFF + h * S + g * 512: QOFF + h * S + (g + 1) * 512],
                     start=True, stop=True).then_inc(amm, 1)

        @block.tensor
        def _(t):
            t.wait_ge(ml, 17)
            DEPTH = 3
            for k in range(DEPTH):
                _emit_s(t, flat[k])
            for n, item in enumerate(flat):
                if n + DEPTH < len(flat):
                    _emit_s(t, flat[n + DEPTH])
                (gi, h, g, rec, kmax) = item
                _emit_pv(t, gi, rec, h, kmax)
                if rec[1] == kmax - 1:
                    t.wait_ge(alv, 2 * gi + 2)
                    t.matmul(l_ps2[gi % 2][:, :], ones_k1[:, :], l_inv[:, :],
                             start=True, stop=True).then_inc(albc, 1)

        @block.vector
        def _(v):
            for gi, (h, g) in enumerate(groups):
                ids = gj_of[(h, g)]
                for (cj, j, diag, mc) in ids:
                    if diag:
                        jloc = j - (len(ids) - 4)
                        u0 = 512 - 128 * jloc
                        v.wait_ge(amm, cj + 1)
                        v.tensor_add(s4[:, (cj % 4) * 512:(cj % 4 + 1) * 512],
                                     s4[:, (cj % 4) * 512:(cj % 4 + 1) * 512],
                                     mask_sb[:, u0:u0 + 512]).then_inc(amk, 1)
                v.wait_ge(apv, ids[-1][0] + 1)
                v.tensor_copy(l_sb[:, :], l_ps2[gi % 2][0:1, :]).then_inc(alv, 1)
                v.wait_ge(alv, 2 * gi + 1)
                v.reciprocal(l_inv[:, :], l_sb[:, :]).then_inc(alv, 1)
                v.wait_ge(albc, gi + 1)
                v.tensor_copy(lbc_sb[:, :], l_ps2[gi % 2][:, :]).then_inc(alev, 1)
                v.wait_ge(alev, gi + 1)
                v.tensor_mul(ctx_sb[:, h * S + g * 512: h * S + (g + 1) * 512],
                             ctx_ps2[gi % 2][:, :], lbc_sb[:, :]).then_inc(acev, 1)

        @block.scalar
        def _(sc):
            for gi, (h, g) in enumerate(groups):
                for (cj, j, diag, mc) in gj_of[(h, g)]:
                    if cj % 2 == 0:
                        continue
                    sc.wait_ge(amm, cj + 1)
                    sc.wait_ge(amk, nmask_thru[cj])
                    if cj >= 4:
                        sc.wait_ge(apv, cj - 3)
                    lo = ((cj - 1) % 4) * 512
                    sc.activation(pt_sb[:, lo:lo + 1024],
                                  s4[:, lo:lo + 1024], AF.Exp,
                                  scale=float(1.0 / np.sqrt(HD))).then_inc(aex, 2)

        if a2a is None:
            @block.sync
            def _(s):
                s.wait_ge(acev, len(groups))
                for h in range(HPC):
                    s.dma_start(out=ctx_out[h * 128:(h + 1) * 128, :],
                                in_=ctx_sb[:, h * S:(h + 1) * S]).then_inc(cst, 16)
                s.wait_ge(cst, 32)
        else:
            ag1_ins, ag1_outs, ccsem = a2a

            @block.gpsimd
            def _(g):
                from concourse.aot_env import allow_ici_all_to_all
                if wo_early is not None:
                    w_out_dram, wl2 = wo_early
                    g.wait_ge(amm, gj)
                    for jj in range(4):
                        g.dma_start(
                            out=AP2(qk_sb, jj * 4 * H, [[NFC * H, 128], [H, 4], [1, H]]),
                            in_=AP2(w_out_dram, jj * 4 * 128 * H,
                                    [[H, 128], [128 * H, 4], [1, H]])
                        ).then_inc(wl2, 16)
                pieces = [(0, 0, TPC, 16), (1, 0, TPC, 32)]
                ncst = [0]
                for pi, (h, t0, tw, gate_ev) in enumerate(pieces):
                    g.wait_ge(acev, gate_ev)
                    g.dma_start(
                        out=AP2(ag1_ins[pi], 0,
                                [[tw, 128], [128 * tw, NCORE], [1, tw]]),
                        in_=AP2(ctx_sb, h * S + t0,
                                [[HPC * S, 128], [TPC, NCORE], [1, tw]])
                    ).then_inc(cst, 16)
                    ncst[0] += 16
                    g.wait_ge(cst, ncst[0])
                    with allow_ici_all_to_all():
                        g.collective_compute(
                            "AllToAll", ALU.bypass,
                            replica_groups=[list(range(NCORE))],
                            ins=[ag1_ins[pi].ap().opt()],
                            outs=[ag1_outs[pi].ap().opt()],
                        ).then_inc(ccsem, 1)


_IDENT_CACHE = {}


def nc_ident_ap(nc):
    key = id(nc)
    if key not in _IDENT_CACHE:
        _IDENT_CACHE[key] = nc.inline_tensor(np.eye(128, dtype=BF), name="ident_a")
    return _IDENT_CACHE[key].ap()


def _mask_inline(nc):
    m = np.full((128, 1024), MASKVAL, dtype=np.float32)
    for ki in range(128):
        m[ki, ki + 512:] = 0.0
    return nc.inline_tensor(m, name="mask_master")


def build_attn(debug=False):
    nc = bass.Bass("TRN2", target_bir_lowering=False)
    xn_full = nc.dram_tensor("xn_full", [NCORE * H, TPC], BF16, kind="ExternalInput")
    wqkv = nc.dram_tensor("wqkv", [H, 768], F32, kind="ExternalInput")
    bqkv2 = nc.dram_tensor("bqkv2", [128, 6], F32, kind="ExternalInput")
    ctx_out = nc.dram_tensor("ctx", [HPC * 128, S], BF16, kind="ExternalOutput")
    dbg = None
    if debug:
        dbg = (nc.dram_tensor("dq", [128, HPC * S], BF16, kind="ExternalOutput"),
               nc.dram_tensor("dk", [128, HPC * S], BF16, kind="ExternalOutput"),
               nc.dram_tensor("dv", [128, HPC * S], BF16, kind="ExternalOutput"))
    mask_inl = _mask_inline(nc)
    import contextlib
    with contextlib.ExitStack() as es:
        qk_sb = es.enter_context(nc.sbuf_tensor("qk_sb", [128, 2 * HPC * S], BF16))
        v_tm = es.enter_context(nc.sbuf_tensor("v_tm", [128, HPC * S], BF16))
        ctx_sb = es.enter_context(nc.sbuf_tensor("ctx_sb", [128, HPC * S], BF16))
        emit_attention(nc, xn_full, wqkv, bqkv2, ctx_out, mask_inl,
                       qk_sb, v_tm, ctx_sb, debug_qkv=dbg)
    return nc



def build_l3():
    nc = bass.Bass("TRN2", target_bir_lowering=False)
    ctx_my = nc.dram_tensor("ctx_my", [H, TPC], BF16, kind="ExternalInput")
    x_in = nc.dram_tensor("x", [TPC, H], F32, kind="ExternalInput")
    ffn_in = nc.dram_tensor("ffn", [H, TPC], F32, kind="ExternalInput")
    w_out = nc.dram_tensor("w_out", [H, H], F32, kind="ExternalInput")
    bout2 = nc.dram_tensor("bout2", [128, NFC], F32, kind="ExternalInput")
    y_out = nc.dram_tensor("y", [TPC, H], F32, kind="ExternalOutput")
    emit_final(nc, ctx_my, x_in, ffn_in, w_out, bout2, y_out)
    return nc


def emit_final(nc, ctx_my, x_in, ffn_in, w_out, bout2, y_out, gate=None,
               wo_pre=None):
    import contextlib
    NIT = 2 * NFC
    with contextlib.ExitStack() as es:
        E = es.enter_context
        ctx_sb = E(nc.sbuf_tensor("o_ctx", [128, NFC * TPC], BF16))
        if wo_pre is None:
            wo_sb = E(nc.sbuf_tensor("o_wo", [128, NFC * H], BF16))
        else:
            wo_sb, wl2_pre = wo_pre
        b2_sb = E(nc.sbuf_tensor("o_b2", [128, NFC], F32))
        id_f32 = E(nc.sbuf_tensor("o_idf", [128, 128], F32))
        fsl = E(nc.sbuf_tensor("o_f", [128, 2 * 512], F32))
        ssl = E(nc.sbuf_tensor("o_s", [128, 2 * 512], F32))
        xsl = E(nc.sbuf_tensor("o_x", [128, 4 * H], F32))
        osl = E(nc.sbuf_tensor("o_o", [128, 4 * H], F32))
        op0 = E(nc.psum_tensor("o_p0", [128, 512], F32))
        op1 = E(nc.psum_tensor("o_p1", [128, 512], F32))
        tp0 = E(nc.psum_tensor("o_t0", [128, 512], F32))
        tp1 = E(nc.psum_tensor("o_t1", [128, 512], F32))
        ld = E(nc.semaphore("o_ld"))
        wl2 = E(nc.semaphore("o_wl2"))
        fld = [E(nc.semaphore("o_fl0")), E(nc.semaphore("o_fl1"))]
        xld = [E(nc.semaphore("o_xl0")), E(nc.semaphore("o_xl1"))]
        omm = E(nc.semaphore("o_omm"))
        osm = E(nc.semaphore("o_osm"))
        ovv = E(nc.semaphore("o_ovv"))
        otm = E(nc.semaphore("o_otm"))
        oev = E(nc.semaphore("o_oev"))
        ost = [E(nc.semaphore("o_st0")), E(nc.semaphore("o_st1"))]
        op = [op0, op1]
        tp = [tp0, tp1]
        idf = nc.inline_tensor(np.eye(128, dtype=np.float32), name="ident_f32")

        with nc.Block(no_gpsimd_drain=True) as block:
            if wo_pre is None:
                @block.gpsimd
                def _(g):
                    for j in range(4):
                        g.dma_start(
                            out=AP2(wo_sb, j * 4 * H, [[NFC * H, 128], [H, 4], [1, H]]),
                            in_=AP2(w_out, j * 4 * 128 * H, [[H, 128], [128 * H, 4], [1, H]])
                        ).then_inc(wl2, 16)
            else:
                wl2 = wl2_pre

            @block.sync
            def _(s):
                s.dma_start(out=b2_sb[:, :], in_=bout2[:, :]).then_inc(ld, 16)
                s.dma_start(out=id_f32[:, :], in_=idf.ap()).then_inc(ld, 16)
                if gate is not None:
                    s.wait_ge(gate[0], gate[1])
                if isinstance(ctx_my, list):
                    pieces = [(0, 0, TPC, ctx_my[0]),
                              (NFC // 2, 0, TPC, ctx_my[1])]
                    nparts = 2
                    for cb, co, tw, part in pieces:
                        s.dma_start(out=AP2(ctx_sb, cb * TPC + co,
                                            [[NFC * TPC, 128], [TPC, NFC // 2], [1, tw]]),
                                    in_=AP2(part, 0, [[tw, 128], [128 * tw, NFC // 2], [1, tw]])
                                    ).then_inc(ld, 16)
                else:
                    nparts = 1
                    s.dma_start(out=AP2(ctx_sb, 0, [[NFC * TPC, 128], [TPC, NFC], [1, TPC]]),
                                in_=AP2(ctx_my, 0, [[TPC, 128], [128 * TPC, NFC], [1, TPC]])
                                ).then_inc(ld, 16)
                s.wait_ge(ld, 32 + 16 * nparts)
                for tb2 in range(2):
                    if tb2 == 1:
                        s.wait_ge(oev, NFC)
                        for tc in range(4):
                            s.dma_start(out=y_out[tc * 128:(tc + 1) * 128, :],
                                        in_=osl[:, tc * H:(tc + 1) * H]
                                        ).then_inc(ost[tc % 2], 16)
                    for tcq in range(4):
                        tc = tb2 * 4 + tcq
                        s.dma_start(out=xsl[:, tcq * H:(tcq + 1) * H],
                                    in_=x_in[tc * 128:(tc + 1) * 128, :]
                                    ).then_inc(xld[tb2], 16)
                    for hc in range(NFC):
                        i = tb2 * NFC + hc
                        sl = i % 2
                        if i >= 2:
                            s.wait_ge(ovv, i - 1)
                        s.dma_start(out=fsl[:, sl * 512:(sl + 1) * 512],
                                    in_=ffn_in[hc * 128:(hc + 1) * 128,
                                               tb2 * 512:(tb2 + 1) * 512]
                                    ).then_inc(fld[sl], 16)
                s.wait_ge(oev, 2 * NFC)
                for tc in range(4, 8):
                    s.dma_start(out=y_out[tc * 128:(tc + 1) * 128, :],
                                in_=osl[:, (tc - 4) * H:(tc - 3) * H]
                                ).then_inc(ost[tc % 2], 16)
                s.wait_ge(ost[0], 16 * 4)
                s.wait_ge(ost[1], 16 * 4)

            def _tr(t, i):
                t.wait_ge(ovv, i + 1)
                if i >= 2:
                    t.wait_ge(oev, i - 1)
                for q in range(4):
                    inst = t.transpose(
                        tp[i % 2][:, q * 128:(q + 1) * 128],
                        ssl[:, (i % 2) * 512 + q * 128:(i % 2) * 512 + (q + 1) * 128],
                        id_f32[:, :])
                inst.then_inc(otm, 1)

            @block.tensor
            def _(t):
                t.wait_ge(wl2, 64)
                t.wait_ge(ld, 32 + 16 * (2 if isinstance(ctx_my, list) else 1))
                for i in range(NIT):
                    tb2, hc = i // NFC, i % NFC
                    if i >= 2:
                        t.wait_ge(osm, i - 1)
                    for fc in range(NFC):
                        inst = t.matmul(
                            op[i % 2][:, :],
                            wo_sb[:, fc * H + hc * 128: fc * H + (hc + 1) * 128],
                            ctx_sb[:, fc * TPC + tb2 * 512: fc * TPC + (tb2 + 1) * 512],
                            start=(fc == 0), stop=(fc == NFC - 1))
                    inst.then_inc(omm, 1)
                    if i >= 1:
                        _tr(t, i - 1)
                _tr(t, NIT - 1)

            @block.scalar
            def _(sc):
                for i in range(NIT):
                    sl = i % 2
                    hc = i % NFC
                    sc.wait_ge(omm, i + 1)
                    if i >= 2:
                        sc.wait_ge(otm, i - 1)
                    sc.activation(ssl[:, sl * 512:(sl + 1) * 512], op[i % 2][:, :],
                                  AF.Identity,
                                  bias=b2_sb[:, hc:hc + 1]).then_inc(osm, 1)

            def _ev(v, i):
                tb2, hc = i // NFC, i % NFC
                v.wait_ge(otm, i + 1)
                v.wait_ge(xld[tb2], 16 * 4)
                if tb2 == 1 and hc == 0:
                    v.wait_ge(ost[0], 32)
                    v.wait_ge(ost[1], 32)
                for q in range(4):
                    inst = v.scalar_tensor_tensor(
                        osl[:, q * H + hc * 128: q * H + (hc + 1) * 128],
                        tp[i % 2][:, q * 128:(q + 1) * 128], 1.0,
                        xsl[:, q * H + hc * 128: q * H + (hc + 1) * 128],
                        ALU.mult, ALU.add)
                inst.then_inc(oev, 1)

            @block.vector
            def _(v):
                for i in range(NIT):
                    sl = i % 2
                    if i >= 1:
                        _ev(v, i - 1)
                    v.wait_ge(osm, i + 1)
                    v.wait_ge(fld[sl], 16 * (i // 2 + 1))
                    v.tensor_add(ssl[:, sl * 512:(sl + 1) * 512],
                                 ssl[:, sl * 512:(sl + 1) * 512],
                                 fsl[:, sl * 512:(sl + 1) * 512]).then_inc(ovv, 1)
                _ev(v, NIT - 1)




def kernel_unfused(L0_attn_hidden_states, L0_attn_w_qkv, L0_attn_b_qkv, L0_attn_w_out,
           L0_attn_b_out, L0_ffn_SU_ug, L0_ffn_SU_dp, L0_ffn_Wscale_ug,
           L0_ffn_Wscale_dp, L0_ffn_W_up, L0_ffn_W_gate, L0_ffn_W_down):
    x = np.ascontiguousarray(
        np.asarray(L0_attn_hidden_states, dtype=np.float32).reshape(S, H))
    w_qkv = np.asarray(L0_attn_w_qkv, dtype=np.float32)
    b_qkv = np.asarray(L0_attn_b_qkv, dtype=np.float32)
    w_out = np.ascontiguousarray(np.asarray(L0_attn_w_out, dtype=np.float32))
    b_out = np.asarray(L0_attn_b_out, dtype=np.float32)
    su_ug = np.asarray(L0_ffn_SU_ug, dtype=np.float32)
    su_dp = np.asarray(L0_ffn_SU_dp, dtype=np.float32)
    ws_ug = np.asarray(L0_ffn_Wscale_ug, dtype=np.float32)
    ws_dp = np.asarray(L0_ffn_Wscale_dp, dtype=np.float32)
    w_upT = np.ascontiguousarray(np.asarray(L0_ffn_W_up, dtype=np.float32).T)
    w_gateT = np.ascontiguousarray(np.asarray(L0_ffn_W_gate, dtype=np.float32).T)
    w_downT = np.ascontiguousarray(np.asarray(L0_ffn_W_down, dtype=np.float32).T)

    cores = list(range(NCORE))

    nc1 = build_l1()
    res1 = run_bass_kernel_spmd(
        nc1, [{"x": x[c * TPC:(c + 1) * TPC]} for c in cores], core_ids=cores)
    xn_shards = [np.asarray(r["xn_fm"]) for r in res1.results]
    xn_full = np.ascontiguousarray(np.concatenate(xn_shards, axis=0))

    nc2 = build_ffn()
    ffn_maps = []
    for c in cores:
        ffn_maps.append({
            "xn_own": xn_shards[c],
            "su_ug2": np.ascontiguousarray(su_ug.reshape(NFC, 128).T),
            "su_dp2": np.ascontiguousarray(su_dp.reshape(NIC, 128).T),
            "wsu2": np.ascontiguousarray(ws_ug[:INTER].reshape(NIC, 128).T),
            "wsg2": np.ascontiguousarray(ws_ug[INTER:].reshape(NIC, 128).T),
            "wsdp2": np.ascontiguousarray(ws_dp.reshape(NFC, 128).T),
            "w_upT": w_upT, "w_gateT": w_gateT, "w_downT": w_downT,
        })
    res2 = run_bass_kernel_spmd(nc2, ffn_maps, core_ids=cores)
    ffn_shards = [np.asarray(r["ffn_out"]) for r in res2.results]

    nc3 = build_attn()
    attn_maps = []
    for c in cores:
        attn_maps.append({
            "xn_full": xn_full,
            "wqkv": np.ascontiguousarray(w_qkv[:, 768 * c:768 * (c + 1)]),
            "bqkv2": np.ascontiguousarray(
                b_qkv[768 * c:768 * (c + 1)].reshape(6, 128).T),
        })
    res3 = run_bass_kernel_spmd(nc3, attn_maps, core_ids=cores)
    ctx_full = np.concatenate([np.asarray(r["ctx"]) for r in res3.results], axis=0)

    nc4 = build_l3()
    l3_maps = []
    for c in cores:
        l3_maps.append({
            "ctx_my": np.ascontiguousarray(ctx_full[:, c * TPC:(c + 1) * TPC]),
            "x": x[c * TPC:(c + 1) * TPC],
            "ffn": ffn_shards[c],
            "w_out": w_out,
            "bout2": np.ascontiguousarray(b_out.reshape(NFC, 128).T),
        })
    res4 = run_bass_kernel_spmd(nc4, l3_maps, core_ids=cores)
    y = np.concatenate([np.asarray(r["y"]) for r in res4.results], axis=0)
    return y.reshape(1, S, H)



def build_fused():
    nc = bass.Bass("TRN2", target_bir_lowering=False)
    x_in = nc.dram_tensor("x", [TPC, H], F32, kind="ExternalInput")
    wqkv = nc.dram_tensor("wqkv", [H, 768], F32, kind="ExternalInput")
    bqkv2 = nc.dram_tensor("bqkv2", [128, 6], F32, kind="ExternalInput")
    w_out = nc.dram_tensor("w_out", [H, H], F32, kind="ExternalInput")
    bout2 = nc.dram_tensor("bout2", [128, NFC], F32, kind="ExternalInput")
    su_ug2 = nc.dram_tensor("su_ug2", [128, NFC], F32, kind="ExternalInput")
    su_dp2 = nc.dram_tensor("su_dp2", [128, NIC], F32, kind="ExternalInput")
    wsu2 = nc.dram_tensor("wsu2", [128, NIC], F32, kind="ExternalInput")
    wsg2 = nc.dram_tensor("wsg2", [128, NIC], F32, kind="ExternalInput")
    wsdp2 = nc.dram_tensor("wsdp2", [128, NFC], F32, kind="ExternalInput")
    w_ugT = nc.dram_tensor("w_ugT", [H, 2 * INTER], F32, kind="ExternalInput")
    w_downT = nc.dram_tensor("w_downT", [INTER, H], F32, kind="ExternalInput")
    y_out = nc.dram_tensor("y", [TPC, H], F32, kind="ExternalOutput")

    h128f = nc.inline_tensor(_hadamard(128).astype(np.float32), name="h128f")
    mask_inl = _mask_inline(nc)

    ag0_in = nc.dram_tensor("ag0_in", [H, TPC], BF16, kind="Internal")
    ag0_out = nc.dram_tensor("ag0_out", [NCORE * H, TPC], BF16, kind="Internal",
                             addr_space="Shared")
    _a2a_shapes = [TPC, TPC]
    ag1_ins = [nc.dram_tensor(f"ag1_in{i}", [NCORE * 128, w], BF16, kind="Internal")
               for i, w in enumerate(_a2a_shapes)]
    ag1_outs = [nc.dram_tensor(f"ag1_out{i}", [NCORE * 128, w], BF16, kind="Internal")
                for i, w in enumerate(_a2a_shapes)]
    ffn_dram = nc.dram_tensor("ffn_sp", [H, TPC], F32, kind="Internal")

    import contextlib
    with contextlib.ExitStack() as es:
        cc = es.enter_context(nc.semaphore("cc"))
        ag0st = es.enter_context(nc.semaphore("ag0st"))
        esf = es.enter_context(contextlib.ExitStack())
        d = _ffn_persist_alloc(nc, esf)
        esu = es.enter_context(contextlib.ExitStack())
        d["tbl_ug"] = esu.enter_context(nc.sbuf_tensor("tbl_ug", [128, NFC * 128], BF16))
        with nc.named_scope("tables"):
            emit_ffn_tables(nc, d, su_ug2, su_dp2, wsu2, wsg2, wsdp2, h128f)
        def _trigger_ag0(g):
            g.wait_ge(ag0st, 16)
            g.collective_compute(
                "AllGather", ALU.bypass,
                replica_groups=[list(range(NCORE))],
                ins=[ag0_in.ap().opt()], outs=[ag0_out.ap().opt()],
            ).then_inc(cc, 1)

        with nc.sbuf_tensor("xn_sb", [128, NFC * TPC], BF16) as xn_sb:
            with nc.named_scope("ln"):
                emit_ln(nc, x_in, xn_sb)
            with nc.Block(no_gpsimd_drain=True) as block:
                @block.gpsimd
                def _(g):
                    g.dma_start(
                        out=AP2(ag0_in, 0, [[TPC, 128], [128 * TPC, NFC], [1, TPC]]),
                        in_=AP2(xn_sb, 0, [[NFC * TPC, 128], [TPC, NFC], [1, TPC]])
                    ).then_inc(ag0st, 16)
            with nc.named_scope("yhad"):
                emit_ffn_yhad(nc, d, xn_sb)
        esu.close()
        with nc.named_scope("ffn"):
            emit_ffn_main(nc, d, w_ugT, w_downT, ffn_dram,
                          pre_gpsimd=_trigger_ag0)
        esf.close()
        qk_sb = es.enter_context(nc.sbuf_tensor("qk_sb", [128, 2 * HPC * S], BF16))
        wl2e = es.enter_context(nc.semaphore("wl2e"))
        with contextlib.ExitStack() as es2:
            E = es2.enter_context
            ctx_sb = E(nc.sbuf_tensor("ctx_sb", [128, HPC * S], BF16))
            v_tm = E(nc.sbuf_tensor("v_tm", [128, HPC * S], BF16))
            with nc.named_scope("attn"):
                emit_attention(nc, ag0_out, wqkv, bqkv2, None, mask_inl,
                               qk_sb, v_tm, ctx_sb,
                               gate=(cc, 1), a2a=(ag1_ins, ag1_outs, cc),
                               wo_early=(w_out, wl2e))
        with nc.named_scope("final"):
            emit_final(nc, ag1_outs, x_in, ffn_dram, w_out, bout2, y_out,
                       gate=(cc, 3), wo_pre=(qk_sb, wl2e))
    return nc


def _host_maps(x, w_qkv, b_qkv, w_out, b_out, su_ug, su_dp, ws_ug, ws_dp,
               w_upT, w_gateT, w_downT):
    maps = []
    head_order = [2 * r for r in range(NCORE)] + [2 * r + 1 for r in range(NCORE)]
    feat_perm = np.concatenate([np.arange(128) + hd * 128 for hd in head_order])
    w_out = np.ascontiguousarray(w_out[feat_perm, :])
    WB = 128
    w_ugT = np.ascontiguousarray(
        np.stack([w_upT.reshape(H, INTER // WB, WB),
                  w_gateT.reshape(H, INTER // WB, WB)], axis=2).reshape(H, 2 * INTER))
    su_ug2 = np.ascontiguousarray(su_ug.reshape(NFC, 128).T)
    su_dp2 = np.ascontiguousarray(su_dp.reshape(NIC, 128).T)
    wsu2 = np.ascontiguousarray(ws_ug[:INTER].reshape(NIC, 128).T)
    wsg2 = np.ascontiguousarray(ws_ug[INTER:].reshape(NIC, 128).T)
    wsdp2 = np.ascontiguousarray(ws_dp.reshape(NFC, 128).T)
    bout2 = np.ascontiguousarray(b_out.reshape(NFC, 128).T)
    for c in range(NCORE):
        maps.append({
            "x": x[c * TPC:(c + 1) * TPC],
            "wqkv": np.ascontiguousarray(w_qkv[:, 768 * c:768 * (c + 1)]),
            "bqkv2": np.ascontiguousarray(
                b_qkv[768 * c:768 * (c + 1)].reshape(6, 128).T),
            "w_out": w_out, "bout2": bout2,
            "su_ug2": su_ug2, "su_dp2": su_dp2, "wsu2": wsu2, "wsg2": wsg2,
            "wsdp2": wsdp2,
            "w_ugT": w_ugT, "w_downT": w_downT,
        })
    return maps


def kernel_fused(L0_attn_hidden_states, L0_attn_w_qkv, L0_attn_b_qkv,
                 L0_attn_w_out, L0_attn_b_out, L0_ffn_SU_ug, L0_ffn_SU_dp,
                 L0_ffn_Wscale_ug, L0_ffn_Wscale_dp, L0_ffn_W_up,
                 L0_ffn_W_gate, L0_ffn_W_down):
    x = np.ascontiguousarray(
        np.asarray(L0_attn_hidden_states, dtype=np.float32).reshape(S, H))
    nc = build_fused()
    maps = _host_maps(
        x,
        np.asarray(L0_attn_w_qkv, dtype=np.float32),
        np.asarray(L0_attn_b_qkv, dtype=np.float32),
        np.ascontiguousarray(np.asarray(L0_attn_w_out, dtype=np.float32)),
        np.asarray(L0_attn_b_out, dtype=np.float32),
        np.asarray(L0_ffn_SU_ug, dtype=np.float32),
        np.asarray(L0_ffn_SU_dp, dtype=np.float32),
        np.asarray(L0_ffn_Wscale_ug, dtype=np.float32),
        np.asarray(L0_ffn_Wscale_dp, dtype=np.float32),
        np.ascontiguousarray(np.asarray(L0_ffn_W_up, dtype=np.float32).T),
        np.ascontiguousarray(np.asarray(L0_ffn_W_gate, dtype=np.float32).T),
        np.ascontiguousarray(np.asarray(L0_ffn_W_down, dtype=np.float32).T))
    res = run_bass_kernel_spmd(nc, maps, core_ids=list(range(NCORE)))
    y = np.concatenate([np.asarray(r["y"]) for r in res.results], axis=0)
    return y.reshape(1, S, H)


kernel = kernel_fused


# version 15
# speedup vs baseline: 1.0564x; 1.0047x over previous
import numpy as np
import ml_dtypes

import concourse.bass as bass
import concourse.mybir as mybir
from concourse.bass_utils import run_bass_kernel_spmd

F32 = mybir.dt.float32
BF16 = mybir.dt.bfloat16
BF = ml_dtypes.bfloat16
AF = mybir.ActivationFunctionType
ALU = mybir.AluOpType
AX = mybir.AxisListType

NCORE = 8
S = 8192
H = 2048
NH = 16
HD = 128
INTER = 8192
TPC = S // NCORE
HPC = NH // NCORE
NFC = H // 128
NIC = INTER // 128
MASKVAL = -1.0e9
EPS = 1e-5


def _hadamard(n):
    h = np.array([[1.0]], dtype=np.float64)
    while h.shape[0] < n:
        h = np.block([[h, h], [h, -h]])
    return h


def AP2(t, off, dims):
    return bass.AP(t, int(off), [list(d) for d in dims])



def build_l1():
    nc = bass.Bass("TRN2", target_bir_lowering=False)
    x_in = nc.dram_tensor("x", [TPC, H], F32, kind="ExternalInput")
    xn_out = nc.dram_tensor("xn_fm", [H, TPC], BF16, kind="ExternalOutput")
    with nc.sbuf_tensor("xnf_sb", [128, NFC * TPC], BF16) as xn_fm:
        emit_ln(nc, x_in, xn_fm)
        with nc.semaphore("l1_st") as st, nc.Block(no_gpsimd_drain=True) as block:
            @block.sync
            def _(s):
                s.dma_start(out=AP2(xn_out, 0, [[TPC, 128], [128 * TPC, NFC], [1, TPC]]),
                            in_=AP2(xn_fm, 0, [[NFC * TPC, 128], [TPC, NFC], [1, TPC]])
                            ).then_inc(st, 16)
                s.wait_ge(st, 16)
    return nc


def emit_ln(nc, x_in, xn_fm):
    ident = nc.inline_tensor(np.eye(128, dtype=BF), name="ident")

    ntc = TPC // 128
    with (
        nc.sbuf_tensor("xbuf", [128, 2 * H], F32) as xbuf,
        nc.sbuf_tensor("sq", [128, H], F32) as sq,
        nc.sbuf_tensor("stats", [128, 8], F32) as stats,
        nc.sbuf_tensor("xn_tm", [128, 2 * H], BF16) as xn_tm,
        nc.sbuf_tensor("ident_sb", [128, 128], BF16) as ident_sb,
        nc.psum_tensor("tpA", [128, 1024], BF16) as tpA,
        nc.psum_tensor("tpB", [128, 1024], BF16) as tpB,
        nc.semaphore("ld0") as ld0,
        nc.semaphore("ld1") as ld1,
        nc.semaphore("idl") as idl,
        nc.semaphore("vst") as vst,
        nc.semaphore("sst") as sst,
        nc.semaphore("tmm") as tmm,
        nc.semaphore("tev") as tev,
        nc.semaphore("vsy") as vsy,
        nc.semaphore("st") as st,
        nc.Block(no_gpsimd_drain=True) as block,
    ):
        lds = [ld0, ld1]
        tp = [tpA, tpB]

        @block.sync
        def _(s):
            s.dma_start(out=ident_sb[:, :], in_=ident.ap()).then_inc(idl, 16)
            for tc in range(ntc):
                if tc >= 2:
                    s.wait_ge(tmm, (tc - 1) * 16)
                s.dma_start(out=xbuf[:, (tc % 2) * H:(tc % 2) * H + H],
                            in_=x_in[tc * 128:(tc + 1) * 128, :]
                            ).then_inc(lds[tc % 2], 16)

        @block.scalar
        def _(sc):
            for tc in range(ntc):
                sl = (tc % 2) * H
                sc.wait_ge(lds[tc % 2], (tc // 2 + 1) * 16)
                sc.activation(sq[:, :], xbuf[:, sl:sl + H], AF.Square,
                              accum_out=stats[:, 2:3]).then_inc(sst, 1)
                sc.wait_ge(vst, 5 * tc + 3)
                sc.activation(stats[:, 4:5], stats[:, 3:4], AF.Sqrt).then_inc(sst, 1)

        @block.vector
        def _(v):
            vsn = [0]

            def vsy_inc(inst):
                vsn[0] += 1
                inst.then_inc(vsy, 1)

            def vsy_wait():
                v.wait_ge(vsy, vsn[0])

            for tc in range(ntc):
                sl = (tc % 2) * H
                v.wait_ge(lds[tc % 2], (tc // 2 + 1) * 16)
                vsy_inc(v.reduce_sum(stats[:, 0:1], xbuf[:, sl:sl + H], axis=AX.X))
                v.wait_ge(sst, 2 * tc + 1)
                vsy_wait()
                vsy_inc(v.tensor_scalar_mul(stats[:, 1:2], stats[:, 0:1], 1.0 / H))
                vsy_wait()
                v.tensor_mul(stats[:, 5:6], stats[:, 1:2], stats[:, 1:2])
                vsy_inc(v.tensor_scalar(stats[:, 3:4], stats[:, 2:3], 1.0 / H, EPS,
                                        ALU.mult, ALU.add))
                vsy_wait()
                v.tensor_sub(stats[:, 3:4], stats[:, 3:4], stats[:, 5:6]
                             ).then_inc(vst, 3)
                v.wait_ge(sst, 2 * tc + 2)
                vsy_inc(v.reciprocal(stats[:, 6:7], stats[:, 4:5]))
                vsy_wait()
                v.tensor_scalar(xn_tm[:, sl:sl + H], xbuf[:, sl:sl + H],
                                stats[:, 1:2], stats[:, 6:7],
                                ALU.subtract, ALU.mult).then_inc(vst, 2)
                if tc >= 1:
                    for half in range(2):
                        v.wait_ge(tmm, (tc - 1) * 16 + (half + 1) * 8)
                        dst = AP2(xn_fm, (half * 8) * TPC + (tc - 1) * 128,
                                  [[NFC * TPC, 128], [TPC, 8], [1, 128]])
                        v.tensor_copy(dst, tp[half][:, 0:1024]).then_inc(tev, 1)
            for half in range(2):
                v.wait_ge(tmm, (ntc - 1) * 16 + (half + 1) * 8)
                dst = AP2(xn_fm, (half * 8) * TPC + (ntc - 1) * 128,
                          [[NFC * TPC, 128], [TPC, 8], [1, 128]])
                v.tensor_copy(dst, tp[half][:, 0:1024]).then_inc(tev, 1)

        @block.tensor
        def _(t):
            t.wait_ge(idl, 16)
            for tc in range(ntc):
                sl = (tc % 2) * H
                t.wait_ge(vst, 5 * tc + 5)
                for half in range(2):
                    if tc >= 1:
                        t.wait_ge(tev, (tc - 1) * 2 + half + 1)
                    for j in range(8):
                        fc = half * 8 + j
                        inst = t.transpose(tp[half][:, j * 128:(j + 1) * 128],
                                           xn_tm[:, sl + fc * 128:sl + (fc + 1) * 128],
                                           ident_sb[:, :])
                    inst.then_inc(tmm, 8)






def _ffn_persist_alloc(nc, es):
    d = {}
    d["tbl_dp"] = es.enter_context(nc.sbuf_tensor("tbl_dp", [128, NIC * 128], BF16))
    d["wsg"] = es.enter_context(nc.sbuf_tensor("wsg", [128, NIC], F32))
    d["wsdp2"] = es.enter_context(nc.sbuf_tensor("wsdp2_sb", [128, NFC], F32))
    d["y_dram"] = nc.dram_tensor("y_dram", [H, TPC], BF16, kind="Internal")
    return d


def emit_ffn_tables(nc, d, su_ug2, su_dp2, wsu2, wsg2, wsdp2_in, h128f):
    with (
        nc.sbuf_tensor("t_suug", [128, NFC], F32) as suug,
        nc.sbuf_tensor("t_sudp", [128, NIC], F32) as sudp,
        nc.sbuf_tensor("t_wsu", [128, NIC], F32) as wsu,
        nc.sbuf_tensor("t_h128", [128, 128], F32) as h128,
        nc.sbuf_tensor("t_scomb", [128, NIC], F32) as scomb,
        nc.semaphore("tb_ld") as ld,
        nc.semaphore("tb_v") as vs,
    ):
        with nc.Block(no_gpsimd_drain=True) as block:
            @block.sync
            def _(s):
                s.dma_start(out=suug[:, :], in_=su_ug2[:, :]).then_inc(ld, 16)
                s.dma_start(out=sudp[:, :], in_=su_dp2[:, :]).then_inc(ld, 16)
                s.dma_start(out=wsu[:, :], in_=wsu2[:, :]).then_inc(ld, 16)
                s.dma_start(out=d["wsg"][:, :], in_=wsg2[:, :]).then_inc(ld, 16)
                s.dma_start(out=d["wsdp2"][:, :], in_=wsdp2_in[:, :]).then_inc(ld, 16)
                s.dma_start(out=h128[:, :], in_=h128f.ap()).then_inc(ld, 16)

            @block.vector
            def _(v):
                v.wait_ge(ld, 96)
                vseq = [0]

                def stamp(inst):
                    vseq[0] += 1
                    inst.then_inc(vs, 1)

                stamp(v.tensor_scalar_mul(suug[:, :], suug[:, :], 1.0 / np.sqrt(2048.0)))
                v.wait_ge(vs, vseq[0])
                for c in range(NFC):
                    v.tensor_scalar(d["tbl_ug"][:, c * 128:(c + 1) * 128], h128[:, :],
                                    suug[:, c:c + 1], None, ALU.mult)
                stamp(v.tensor_mul(scomb[:, :], sudp[:, :], wsu[:, :]))
                v.wait_ge(vs, vseq[0])
                stamp(v.tensor_scalar_mul(scomb[:, :], scomb[:, :], 1.0 / np.sqrt(8192.0)))
                v.wait_ge(vs, vseq[0])
                for c in range(NIC):
                    v.tensor_scalar(d["tbl_dp"][:, c * 128:(c + 1) * 128], h128[:, :],
                                    scomb[:, c:c + 1], None, ALU.mult)


def emit_ffn_yhad(nc, d, xn_own_sb, pre_gpsimd=None):
    with (
        nc.sbuf_tensor("yz_a", [128, NFC * 512], F32) as za,
        nc.sbuf_tensor("yz_b", [128, NFC * 512], F32) as zb,
        nc.sbuf_tensor("y_sb", [128, NFC * TPC], BF16) as y_sb,
        nc.psum_tensor("y_ps0", [128, 512], F32) as yp0,
        nc.psum_tensor("y_ps1", [128, 512], F32) as yp1,
        nc.semaphore("yh_m") as mm,
        nc.semaphore("yh_e") as ev,
        nc.semaphore("yh_v") as vs,
        nc.semaphore("yh_st") as yst,
        nc.Block(no_gpsimd_drain=True) as block,
    ):
        yp = [yp0, yp1]

        if pre_gpsimd is not None:
            @block.gpsimd
            def _(g):
                pre_gpsimd(g)

        @block.tensor
        def _(t):
            for th in range(2):
                for c in range(NFC):
                    i = th * NFC + c
                    if i >= 2:
                        t.wait_ge(ev, i - 1)
                    t.matmul(yp[i % 2][:, :], d["tbl_ug"][:, c * 128:(c + 1) * 128],
                             xn_own_sb[:, c * TPC + th * 512: c * TPC + th * 512 + 512],
                             start=True, stop=True).then_inc(mm, 1)

        @block.vector
        def _(v):
            vseq = [0]

            def stamp(inst):
                vseq[0] += 1
                inst.then_inc(vs, 1)

            for th in range(2):
                for c in range(NFC):
                    i = th * NFC + c
                    v.wait_ge(mm, i + 1)
                    v.tensor_copy(za[:, c * 512:(c + 1) * 512], yp[i % 2][:, :]
                                  ).then_inc(ev, 1)
                v.wait_ge(ev, (th + 1) * NFC)
                src, dst = za, zb
                for si, dd in enumerate([1, 2, 4, 8]):
                    last = si == 3
                    nq = NFC // (2 * dd)
                    sA = AP2(src, 0, [[NFC * 512, 128], [2 * dd * 512, nq], [512, dd], [1, 512]])
                    sB = AP2(src, dd * 512, [[NFC * 512, 128], [2 * dd * 512, nq], [512, dd], [1, 512]])
                    if last:
                        dlo = AP2(y_sb, th * 512, [[NFC * TPC, 128], [2 * dd * TPC, nq], [TPC, dd], [1, 512]])
                        dhi = AP2(y_sb, dd * TPC + th * 512, [[NFC * TPC, 128], [2 * dd * TPC, nq], [TPC, dd], [1, 512]])
                    else:
                        dlo = AP2(dst, 0, [[NFC * 512, 128], [2 * dd * 512, nq], [512, dd], [1, 512]])
                        dhi = AP2(dst, dd * 512, [[NFC * 512, 128], [2 * dd * 512, nq], [512, dd], [1, 512]])
                    v.tensor_tensor(dlo, sA, sB, ALU.add)
                    stamp(v.tensor_tensor(dhi, sA, sB, ALU.subtract))
                    v.wait_ge(vs, vseq[0])
                    src, dst = dst, src

        @block.sync
        def _(s):
            s.wait_ge(vs, 8)
            s.dma_start(out=AP2(d["y_dram"], 0, [[TPC, 128], [128 * TPC, NFC], [1, TPC]]),
                        in_=AP2(y_sb, 0, [[NFC * TPC, 128], [TPC, NFC], [1, TPC]])
                        ).then_inc(yst, 16)
            s.wait_ge(yst, 16)


def emit_ffn_main(nc, d, w_ugT, w_downT, ffn_out, pre_gpsimd=None):
    WB = 128
    NB = INTER // WB
    import contextlib
    with contextlib.ExitStack() as hs:
        hbuf = [hs.enter_context(nc.sbuf_tensor(f"f_h{t}", [128, NIC * 512], BF16))
                for t in range(2)]
        z1a = hs.enter_context(nc.sbuf_tensor("f_z1a", [128, NIC * 128], BF16))
        yld = hs.enter_context(nc.semaphore("f_yld"))
        ys = hs.enter_context(contextlib.ExitStack())
        yh = ys.enter_context(nc.sbuf_tensor("f_yh", [128, NFC * 512], BF16))
        nyld = [0]

        def ug_scope(es, th):
            E = es.enter_context
            g = {}
            g["wub"] = E(nc.sbuf_tensor(f"f_wu{th}", [128, 3 * NFC * 2 * WB], BF16))
            g["gb"] = E(nc.sbuf_tensor(f"f_g{th}", [128, 512], F32))
            g["pu"] = [E(nc.psum_tensor(f"f_pu{j}{th}", [128, 512], F32)) for j in range(3)]
            g["pg"] = [E(nc.psum_tensor(f"f_pg{j}{th}", [128, 512], F32)) for j in range(3)]
            g["wlu"] = [E(nc.semaphore(f"f_wlu{j}{th}")) for j in range(3)]
            g["mmu"] = E(nc.semaphore(f"f_mmu{th}"))
            g["mmg"] = E(nc.semaphore(f"f_mmg{th}"))
            g["sil"] = E(nc.semaphore(f"f_sil{th}"))
            g["hdn"] = E(nc.semaphore(f"f_hdn{th}"))
            return g

        def fw_scope(es, th):
            E = es.enter_context
            f = {}
            f["s1p"] = [E(nc.psum_tensor(f"f_s1p{j}{th}", [128, 512], F32)) for j in range(2)]
            f["s1m"] = E(nc.semaphore(f"f_s1m{th}"))
            f["s1e"] = E(nc.semaphore(f"f_s1e{th}"))
            f["bfv"] = E(nc.semaphore(f"f_bfv{th}"))
            return f

        def dn_scope(es, th):
            E = es.enter_context
            n = {}
            n["wdn"] = E(nc.sbuf_tensor(f"f_wd{th}", [128, 2 * NIC * 128], BF16))
            n["slab"] = E(nc.sbuf_tensor(f"f_sl{th}", [128, 2 * 512], F32))
            n["dp"] = [E(nc.psum_tensor(f"f_dp{j}{th}", [128, 512], F32)) for j in range(2)]
            n["wdl"] = [[E(nc.semaphore(f"f_wdl{j}{k}{th}")) for k in range(2)]
                        for j in range(2)]
            n["dmm"] = E(nc.semaphore(f"f_dmm{th}"))
            n["dev"] = E(nc.semaphore(f"f_dev{th}"))
            n["fst"] = [E(nc.semaphore(f"f_fst{j}{th}")) for j in range(2)]
            return n

        def ug_gpsimd_dma(gps, g, th, ib):
            slot = ib % 3
            if ib >= 3:
                gps.wait_ge(g["mmg"], ib - 2)
            gps.dma_start(
                out=AP2(g["wub"], slot * NFC * 2 * WB,
                        [[3 * NFC * 2 * WB, 128], [2 * WB, NFC], [1, 2 * WB]]),
                in_=AP2(w_ugT, ib * 2 * WB,
                        [[2 * INTER, 128], [128 * 2 * INTER, NFC], [1, 2 * WB]])
            ).then_inc(g["wlu"][slot], 16)

        def ug_tensor_chunk(t, g, th, c):
            ib, ps, slot = c, c % 3, c % 3
            t.wait_ge(yld, 16 * nyld[0])
            t.wait_ge(g["wlu"][slot], 16 * (ib // 3 + 1))
            if c >= 3:
                t.wait_ge(g["hdn"], c - 2)
                t.wait_ge(g["sil"], c - 2)
            base = slot * NFC * 2 * WB
            for hc in range(NFC):
                inst = t.matmul(g["pu"][ps][:, :],
                                g["wub"][:, base + hc * 2 * WB: base + hc * 2 * WB + 128],
                                yh[:, hc * 512:(hc + 1) * 512],
                                start=(hc == 0), stop=(hc == NFC - 1))
            inst.then_inc(g["mmu"], 1)
            for hc in range(NFC):
                inst = t.matmul(g["pg"][ps][:, :],
                                g["wub"][:, base + hc * 2 * WB + WB: base + hc * 2 * WB + WB + 128],
                                yh[:, hc * 512:(hc + 1) * 512],
                                start=(hc == 0), stop=(hc == NFC - 1))
            inst.then_inc(g["mmg"], 1)

        def ug_scalar_chunk(sc, g, th, c):
            sc.wait_ge(g["mmg"], c + 1)
            if c >= 1:
                sc.wait_ge(g["hdn"], c)
            sc.activation(g["gb"][:, :], g["pg"][c % 3][:, :], AF.Silu,
                          scale=d["wsg"][:, c:c + 1]).then_inc(g["sil"], 1)

        def ug_vector_chunk(v, g, th, c):
            v.wait_ge(g["mmu"], c + 1)
            v.wait_ge(g["sil"], c + 1)
            v.tensor_mul(hbuf[th][:, c * 512:(c + 1) * 512],
                         g["gb"][:, :], g["pu"][c % 3][:, :]).then_inc(g["hdn"], 1)

        def hslab(th, e, chunk_inner=128):
            return (hbuf[th], e * 128, 512)

        def bf_ap(buf_off_stride, dd, hi, cnt=128):
            buf, off, cs = buf_off_stride
            nq = NIC // (2 * dd)
            base = off + (dd * cs if hi else 0)
            return AP2(buf, base, [[NIC * 512 if cs == 512 else NIC * 128, 128],
                                   [2 * dd * cs, nq], [cs, dd], [1, cnt]])

        def fw_tensor_eighth(t, f, th, e):
            for grp in range(16):
                gg = e * 16 + grp
                if gg >= 2:
                    t.wait_ge(f["s1e"], gg - 1)
                for k in range(4):
                    c = grp * 4 + k
                    inst = t.matmul(
                        f["s1p"][gg % 2][:, k * 128:(k + 1) * 128],
                        d["tbl_dp"][:, c * 128:(c + 1) * 128],
                        hbuf[th][:, c * 512 + e * 128: c * 512 + e * 128 + 128],
                        start=True, stop=True)
                inst.then_inc(f["s1m"], 1)

        def fw_scalar_eighth(sc, f, th, e):
            for grp in range(16):
                gg = e * 16 + grp
                sc.wait_ge(f["s1m"], gg + 1)
                dst = AP2(hbuf[th], grp * 4 * 512 + e * 128,
                          [[NIC * 512, 128], [512, 4], [1, 128]])
                sc.activation(dst, f["s1p"][gg % 2][:, :], AF.Copy).then_inc(f["s1e"], 1)

        def fw_vector_gen(v, f, th, e):
            v.wait_ge(f["s1e"], (e + 1) * 16)
            hs_ = hslab(th, e)
            za_ = (z1a, 0, 128)
            src, dst = hs_, za_
            for si, dd in enumerate([1, 2, 4, 8, 16, 32]):
                sA = bf_ap(src, dd, False)
                sB = bf_ap(src, dd, True)
                dlo = bf_ap(dst, dd, False)
                dhi = bf_ap(dst, dd, True)
                v.tensor_tensor(dlo, sA, sB, ALU.add)
                yield
                v.tensor_tensor(dhi, sA, sB, ALU.subtract).then_inc(f["bfv"], 1)
                v.wait_ge(f["bfv"], e * 6 + si + 1)
                yield
                src, dst = dst, src

        def fw_vector_eighth(v, f, th, e):
            v.wait_ge(f["s1e"], (e + 1) * 16)
            hs_ = hslab(th, e)
            za_ = (z1a, 0, 128)
            src, dst = hs_, za_
            for si, dd in enumerate([1, 2, 4, 8, 16, 32]):
                sA = bf_ap(src, dd, False)
                sB = bf_ap(src, dd, True)
                dlo = bf_ap(dst, dd, False)
                dhi = bf_ap(dst, dd, True)
                v.tensor_tensor(dlo, sA, sB, ALU.add)
                v.tensor_tensor(dhi, sA, sB, ALU.subtract).then_inc(f["bfv"], 1)
                v.wait_ge(f["bfv"], e * 6 + si + 1)
                src, dst = dst, src

        def dn_gpsimd_dma(gps, n, th, hc):
            slot = hc % 2
            if hc >= 2:
                gps.wait_ge(n["dmm"], hc - 1)
            for half in range(2):
                gps.dma_start(
                    out=AP2(n["wdn"], slot * NIC * 128 + half * (NIC // 2) * 128,
                            [[2 * NIC * 128, 128], [128, NIC // 2], [1, 128]]),
                    in_=AP2(w_downT, hc * 128 + half * (NIC // 2) * 128 * H,
                            [[H, 128], [128 * H, NIC // 2], [1, 128]])
                ).then_inc(n["wdl"][slot][half], 16)

        def dn_tensor_hc(t, n, th, hc):
            slot = hc % 2
            if hc >= 2:
                t.wait_ge(n["dev"], hc - 1)
            for c in range(NIC):
                if c == 0:
                    t.wait_ge(n["wdl"][slot][0], 16 * (hc // 2 + 1))
                elif c == NIC // 2:
                    t.wait_ge(n["wdl"][slot][1], 16 * (hc // 2 + 1))
                inst = t.matmul(
                    n["dp"][hc % 2][:, :],
                    n["wdn"][:, slot * NIC * 128 + c * 128: slot * NIC * 128 + (c + 1) * 128],
                    hbuf[th][:, c * 512:(c + 1) * 512],
                    start=(c == 0), stop=(c == NIC - 1))
            inst.then_inc(n["dmm"], 1)

        def dn_scalar_hc(sc, n, th, hc):
            sc.wait_ge(n["dmm"], hc + 1)
            if hc >= 2:
                sc.wait_ge(n["fst"][hc % 2], 16 * (hc // 2))
            sc.activation(n["slab"][:, (hc % 2) * 512:(hc % 2) * 512 + 512],
                          n["dp"][hc % 2][:, :], AF.Identity,
                          scale=d["wsdp2"][:, hc:hc + 1]).then_inc(n["dev"], 1)

        def dn_sync_hc(s, n, th, hc):
            s.wait_ge(n["dev"], hc + 1)
            s.dma_start(
                out=ffn_out[hc * 128:(hc + 1) * 128, th * 512:(th + 1) * 512],
                in_=n["slab"][:, (hc % 2) * 512:(hc % 2) * 512 + 512]
            ).then_inc(n["fst"][hc % 2], 16)

        def load_yh(s, th):
            s.dma_start(out=AP2(yh, 0, [[NFC * 512, 128], [512, NFC], [1, 512]]),
                        in_=AP2(d["y_dram"], th * 512,
                                [[TPC, 128], [128 * TPC, NFC], [1, 512]])
                        ).then_inc(yld, 16)
            nyld[0] += 1

        with contextlib.ExitStack() as es:
            g0 = ug_scope(es, 0)
            block = es.enter_context(nc.Block(no_gpsimd_drain=True))

            @block.gpsimd
            def _(gps):
                if pre_gpsimd is not None:
                    pre_gpsimd(gps)
                for ib in range(NB):
                    ug_gpsimd_dma(gps, g0, 0, ib)

            @block.sync
            def _(s):
                load_yh(s, 0)

            @block.tensor
            def _(t):
                for c in range(NIC):
                    ug_tensor_chunk(t, g0, 0, c)

            @block.scalar
            def _(sc):
                for c in range(NIC):
                    ug_scalar_chunk(sc, g0, 0, c)

            @block.vector
            def _(v):
                for c in range(NIC):
                    ug_vector_chunk(v, g0, 0, c)

        with contextlib.ExitStack() as es:
            g1 = ug_scope(es, 1)
            f0 = fw_scope(es, 0)
            block = es.enter_context(nc.Block(no_gpsimd_drain=True))

            @block.sync
            def _(s):
                load_yh(s, 1)

            @block.gpsimd
            def _(gps):
                for ib in range(NB):
                    ug_gpsimd_dma(gps, g1, 1, ib)

            @block.tensor
            def _(t):
                for e in range(4):
                    fw_tensor_eighth(t, f0, 0, e)
                    for c in range(e * 16, (e + 1) * 16):
                        ug_tensor_chunk(t, g1, 1, c)

            @block.scalar
            def _(sc):
                for e in range(4):
                    fw_scalar_eighth(sc, f0, 0, e)
                    for c in range(e * 16, (e + 1) * 16):
                        ug_scalar_chunk(sc, g1, 1, c)

            @block.vector
            def _(v):
                for e in range(4):
                    bf_gen = fw_vector_gen(v, f0, 0, e - 1) if e >= 1 else None
                    for c in range(e * 16, (e + 1) * 16):
                        ug_vector_chunk(v, g1, 1, c)
                        if bf_gen is not None:
                            next(bf_gen, None)
                    if bf_gen is not None:
                        for _ in bf_gen:
                            pass
                fw_vector_eighth(v, f0, 0, 3)

        ys.close()
        with contextlib.ExitStack() as es:
            f1 = fw_scope(es, 1)
            n0 = dn_scope(es, 0)
            block = es.enter_context(nc.Block(no_gpsimd_drain=True))

            @block.gpsimd
            def _(gps):
                for hc in range(NFC):
                    dn_gpsimd_dma(gps, n0, 0, hc)

            @block.tensor
            def _(t):
                for e in range(4):
                    fw_tensor_eighth(t, f1, 1, e)
                    for hc in range(e * 4, (e + 1) * 4):
                        dn_tensor_hc(t, n0, 0, hc)

            @block.scalar
            def _(sc):
                for e in range(4):
                    fw_scalar_eighth(sc, f1, 1, e)
                    for hc in range(e * 4, (e + 1) * 4):
                        dn_scalar_hc(sc, n0, 0, hc)

            @block.vector
            def _(v):
                for e in range(4):
                    fw_vector_eighth(v, f1, 1, e)

            @block.sync
            def _(s):
                for hc in range(NFC):
                    dn_sync_hc(s, n0, 0, hc)
                s.wait_ge(n0["fst"][0], 16 * NFC // 2)
                s.wait_ge(n0["fst"][1], 16 * NFC // 2)

        with contextlib.ExitStack() as es:
            n1 = dn_scope(es, 1)
            block = es.enter_context(nc.Block(no_gpsimd_drain=True))

            @block.gpsimd
            def _(gps):
                for hc in range(NFC):
                    dn_gpsimd_dma(gps, n1, 1, hc)

            @block.tensor
            def _(t):
                for hc in range(NFC):
                    dn_tensor_hc(t, n1, 1, hc)

            @block.scalar
            def _(sc):
                for hc in range(NFC):
                    dn_scalar_hc(sc, n1, 1, hc)

            @block.sync
            def _(s):
                for hc in range(NFC):
                    dn_sync_hc(s, n1, 1, hc)
                s.wait_ge(n1["fst"][0], 16 * NFC // 2)
                s.wait_ge(n1["fst"][1], 16 * NFC // 2)



def build_ffn():
    nc = bass.Bass("TRN2", target_bir_lowering=False)
    xn_in = nc.dram_tensor("xn_own", [H, TPC], BF16, kind="ExternalInput")
    su_ug2 = nc.dram_tensor("su_ug2", [128, NFC], F32, kind="ExternalInput")
    su_dp2 = nc.dram_tensor("su_dp2", [128, NIC], F32, kind="ExternalInput")
    wsu2 = nc.dram_tensor("wsu2", [128, NIC], F32, kind="ExternalInput")
    wsg2 = nc.dram_tensor("wsg2", [128, NIC], F32, kind="ExternalInput")
    wsdp2 = nc.dram_tensor("wsdp2", [128, NFC], F32, kind="ExternalInput")
    w_ugT = nc.dram_tensor("w_ugT", [H, 2 * INTER], F32, kind="ExternalInput")
    w_downT = nc.dram_tensor("w_downT", [INTER, H], F32, kind="ExternalInput")
    ffn_out = nc.dram_tensor("ffn_out", [H, TPC], F32, kind="ExternalOutput")
    h128f = nc.inline_tensor(_hadamard(128).astype(np.float32), name="h128f")

    import contextlib
    with contextlib.ExitStack() as es:
        d = _ffn_persist_alloc(nc, es)
        esu = es.enter_context(contextlib.ExitStack())
        d["tbl_ug"] = esu.enter_context(nc.sbuf_tensor("tbl_ug", [128, NFC * 128], BF16))
        with nc.sbuf_tensor("xn_sb", [128, NFC * TPC], BF16) as xn_sb:
            with (
                nc.semaphore("xl") as xl,
                nc.Block(no_gpsimd_drain=True) as block,
            ):
                @block.sync
                def _(s):
                    s.dma_start(out=AP2(xn_sb, 0, [[NFC * TPC, 128], [TPC, NFC], [1, TPC]]),
                                in_=AP2(xn_in, 0, [[TPC, 128], [128 * TPC, NFC], [1, TPC]])
                                ).then_inc(xl, 16)
                    s.wait_ge(xl, 16)
            emit_ffn_tables(nc, d, su_ug2, su_dp2, wsu2, wsg2, wsdp2, h128f)
            emit_ffn_yhad(nc, d, xn_sb)
        esu.close()
        emit_ffn_main(nc, d, w_ugT, w_downT, ffn_out)
    return nc



def emit_attention(nc, xn_full, wqkv, bqkv2, ctx_out, mask_inl, qk_sb, v_tm, ctx_sb,
                   debug_qkv=None, gate=None, a2a=None, wo_early=None):
    QOFF = 0
    KOFF = HPC * S
    NTB = S // 512
    NOC = HPC * 3
    import contextlib
    with contextlib.ExitStack() as qes:
        E = qes.enter_context
        wq_sb = E(nc.sbuf_tensor("a_wqkv", [128, NFC * 768], BF16))
        bq_sb = E(nc.sbuf_tensor("a_bqkv", [128, 6], F32))
        xnb = E(nc.sbuf_tensor("a_xnb", [128, 2 * NFC * 512], BF16))
        vstage = E(nc.sbuf_tensor("a_vst", [128, 2 * 512], BF16))
        wl = E(nc.semaphore("a_wl"))
        wl2 = E(nc.semaphore("a_wl2"))
        xl0 = E(nc.semaphore("a_xl0"))
        xl1 = E(nc.semaphore("a_xl1"))
        qmm = E(nc.semaphore("a_qmm"))
        qev = E(nc.semaphore("a_qev"))
        vtm = E(nc.semaphore("a_vtm"))
        vte = E(nc.semaphore("a_vte"))
        qp0 = E(nc.psum_tensor("a_qp0", [128, 512], F32))
        qp1 = E(nc.psum_tensor("a_qp1", [128, 512], F32))
        qp2 = E(nc.psum_tensor("a_qp2", [128, 512], F32))
        tp0 = E(nc.psum_tensor("a_tp0", [128, 1024], BF16))
        tp1 = E(nc.psum_tensor("a_tp1", [128, 1024], BF16))
        id_sb = E(nc.sbuf_tensor("a_id", [128, 128], BF16))
        block = E(nc.Block(no_gpsimd_drain=True))
        qp = [qp0, qp1, qp2]
        tpp = [tp0, tp1]
        xl = [xl0, xl1]

        @block.gpsimd
        def _(g):
            g.dma_start(out=AP2(wq_sb, 0, [[NFC * 768, 128], [768, NFC], [1, 768]]),
                        in_=AP2(wqkv, 0, [[768, 128], [128 * 768, NFC], [1, 768]])
                        ).then_inc(wl2, 16)

        @block.sync
        def _(s):
            s.dma_start(out=bq_sb[:, :], in_=bqkv2[:, :]).then_inc(wl, 16)
            s.dma_start(out=id_sb[:, :],
                        in_=nc_ident_ap(nc)).then_inc(wl, 16)
            if gate is not None:
                s.wait_ge(gate[0], gate[1])
            for tb in range(NTB):
                if tb >= 2:
                    s.wait_ge(qmm, (tb - 1) * NOC)
                r, half = tb // 2, tb % 2
                s.dma_start(
                    out=AP2(xnb, (tb % 2) * NFC * 512,
                            [[2 * NFC * 512, 128], [512, NFC], [1, 512]]),
                    in_=AP2(xn_full, r * H * TPC + half * 512,
                            [[TPC, 128], [128 * TPC, NFC], [1, 512]])
                ).then_inc(xl[tb % 2], 16)

        @block.tensor
        def _(t):
            t.wait_ge(wl, 32)
            t.wait_ge(wl2, 16)
            for tb in range(NTB):
                slot = tb % 2
                t.wait_ge(xl[slot], 16 * (tb // 2 + 1))
                for oc in range(NOC):
                    i = tb * NOC + oc
                    if i >= 3:
                        t.wait_ge(qev, i - 2)
                    for fc in range(NFC):
                        inst = t.matmul(
                            qp[i % 3][:, :],
                            wq_sb[:, fc * 768 + oc * 128: fc * 768 + (oc + 1) * 128],
                            xnb[:, slot * NFC * 512 + fc * 512: slot * NFC * 512 + (fc + 1) * 512],
                            start=(fc == 0), stop=(fc == NFC - 1))
                    inst.then_inc(qmm, 1)
                for h in range(HPC):
                    t.wait_ge(qev, tb * NOC + 3 * h + 3)
                    if tb >= 1:
                        t.wait_ge(vte, (tb - 1) * 2 + h + 1)
                    for j in range(4):
                        inst = t.transpose(tpp[h][:, j * 128:(j + 1) * 128],
                                           vstage[:, h * 512 + j * 128: h * 512 + (j + 1) * 128],
                                           id_sb[:, :])
                    inst.then_inc(vtm, 1)

        @block.scalar
        def _(sc):
            for tb in range(NTB):
                for oc in range(NOC):
                    i = tb * NOC + oc
                    h, kind = oc // 3, oc % 3
                    sc.wait_ge(qmm, i + 1)
                    if kind == 0:
                        dst = qk_sb[:, QOFF + h * S + tb * 512: QOFF + h * S + (tb + 1) * 512]
                    elif kind == 1:
                        dst = qk_sb[:, KOFF + h * S + tb * 512: KOFF + h * S + (tb + 1) * 512]
                    else:
                        if tb >= 1:
                            sc.wait_ge(vtm, (tb - 1) * 2 + h + 1)
                        dst = vstage[:, h * 512:(h + 1) * 512]
                    sc.activation(dst, qp[i % 3][:, :], AF.Identity,
                                  bias=bq_sb[:, oc:oc + 1]).then_inc(qev, 1)

        @block.vector
        def _(v):
            for tb in range(NTB):
                for h in range(HPC):
                    v.wait_ge(vtm, tb * 2 + h + 1)
                    v.tensor_copy(v_tm[:, h * S + tb * 512: h * S + (tb + 1) * 512],
                                  tpp[h][:, 0:512]).then_inc(vte, 1)

    if debug_qkv is not None:
        dq, dk, dv = debug_qkv
        with nc.semaphore("dbg_st") as dst_sem, nc.Block(no_gpsimd_drain=True) as block:
            @block.sync
            def _(s):
                s.dma_start(out=dq[:, :], in_=qk_sb[:, QOFF:QOFF + HPC * S]).then_inc(dst_sem, 16)
                s.dma_start(out=dk[:, :], in_=qk_sb[:, KOFF:KOFF + HPC * S]).then_inc(dst_sem, 16)
                s.dma_start(out=dv[:, :], in_=v_tm[:, :]).then_inc(dst_sem, 16)
                s.wait_ge(dst_sem, 48)
        return

    import contextlib
    with contextlib.ExitStack() as aes:
        E = aes.enter_context
        mask_sb = E(nc.sbuf_tensor("a_mask", [128, 1024], F32))
        pt_sb = E(nc.sbuf_tensor("a_pt", [128, 4 * 512], BF16))
        ones_col = E(nc.sbuf_tensor("a_ones", [128, 1], BF16))
        ones_k1 = E(nc.sbuf_tensor("a_onesk1", [1, 128], F32))
        l_sb = E(nc.sbuf_tensor("a_l", [1, 512], F32))
        l_inv = E(nc.sbuf_tensor("a_linv", [1, 512], F32))
        lbc_sb = E(nc.sbuf_tensor("a_lbc", [128, 512], F32))
        sp_l = [E(nc.psum_tensor(f"a_sp{j}", [128, 512], F32)) for j in range(4)]
        ctx_ps2 = [E(nc.psum_tensor(f"a_ctxp{j}", [128, 512], F32)) for j in range(2)]
        l_ps2 = [E(nc.psum_tensor(f"a_lp{j}", [128, 512], F32)) for j in range(2)]
        ml = E(nc.semaphore("a_ml"))
        alc = E(nc.semaphore("a_alc"))
        albr = E(nc.semaphore("a_albr"))
        amm = E(nc.semaphore("a_amm"))
        aex = E(nc.semaphore("a_aex"))
        amk = E(nc.semaphore("a_amk"))
        apv = E(nc.semaphore("a_apv"))
        alv = E(nc.semaphore("a_alv"))
        albc = E(nc.semaphore("a_albc"))
        alev = E(nc.semaphore("a_alev"))
        acev = E(nc.semaphore("a_acev"))
        cst = E(nc.semaphore("a_cst"))
        block = E(nc.Block(no_gpsimd_drain=True))
        sp = sp_l
        groups = [(h, g) for h in range(HPC) for g in range(16)]

        gj_of = {}
        mask_count_before = {}
        gj = 0
        nmask = 0
        for (h, g) in groups:
            kmax = 4 * (g + 1)
            ids = []
            for j in range(kmax):
                diag = j >= kmax - 4
                ids.append((gj, j, diag, nmask))
                if diag:
                    nmask += 1
                gj += 1
            gj_of[(h, g)] = ids
        total_pv = gj

        @block.sync
        def _(s):
            s.dma_start(out=mask_sb[:, :], in_=mask_inl.ap()).then_inc(ml, 16)

        @block.vector
        def _(v):
            v.memset(ones_col[:, :], 1.0)
            v.memset(ones_k1[:, :], 1.0).then_inc(ml, 1)

        def _emit_pv(t, gi, rec, h, kmax):
            (cj, j, diag, _mc) = rec
            ctx_ps = ctx_ps2[gi % 2]
            l_ps = l_ps2[gi % 2]
            t.wait_ge(aex, cj + 1)
            if j == 0 and gi >= 2:
                t.wait_ge(acev, gi - 1)
            if j == 0 and gi >= 1:
                t.wait_ge(alev, gi)
            t.matmul(ctx_ps[:, :],
                     v_tm[:, h * S + j * 128: h * S + (j + 1) * 128],
                     pt_sb[:, (cj % 4) * 512:(cj % 4 + 1) * 512],
                     start=(j == 0), stop=(j == kmax - 1))
            t.matmul(l_ps[0:1, :], ones_col[:, :],
                     pt_sb[:, (cj % 4) * 512:(cj % 4 + 1) * 512],
                     start=(j == 0), stop=(j == kmax - 1)).then_inc(apv, 1)

        flat = []
        flat_alc = {}
        nal = 0
        for gi, (h, g) in enumerate(groups):
            ids = gj_of[(h, g)]
            for rec in ids:
                flat.append((gi, h, g, rec, len(ids)))
            nal += len(ids)
            flat_alc[gi] = nal

        def _emit_s(t, item):
            (gi, h, g, (cj, j, diag, _mc), kmax) = item
            if cj >= 4:
                t.wait_ge(aex, cj - 3)
            t.matmul(sp[cj % 4][:, :],
                     qk_sb[:, KOFF + h * S + j * 128: KOFF + h * S + (j + 1) * 128],
                     qk_sb[:, QOFF + h * S + g * 512: QOFF + h * S + (g + 1) * 512],
                     start=True, stop=True).then_inc(amm, 1)

        @block.tensor
        def _(t):
            t.wait_ge(ml, 17)
            DEPTH = 3
            for k in range(DEPTH):
                _emit_s(t, flat[k])
            for n, item in enumerate(flat):
                if n + DEPTH < len(flat):
                    _emit_s(t, flat[n + DEPTH])
                (gi, h, g, rec, kmax) = item
                _emit_pv(t, gi, rec, h, kmax)
                if rec[1] == kmax - 1:
                    t.wait_ge(alv, 2 * gi + 2)
                    t.matmul(l_ps2[gi % 2][:, :], ones_k1[:, :], l_inv[:, :],
                             start=True, stop=True).then_inc(albc, 1)

        @block.vector
        def _(v):
            for gi, (h, g) in enumerate(groups):
                ids = gj_of[(h, g)]
                for (cj, j, diag, mc) in ids:
                    if diag:
                        jloc = j - (len(ids) - 4)
                        u0 = 512 - 128 * jloc
                        v.wait_ge(amm, cj + 1)
                        v.tensor_add(sp[cj % 4][:, :], sp[cj % 4][:, :],
                                     mask_sb[:, u0:u0 + 512]).then_inc(amk, 1)
                v.wait_ge(apv, ids[-1][0] + 1)
                v.tensor_copy(l_sb[:, :], l_ps2[gi % 2][0:1, :]).then_inc(alv, 1)
                v.wait_ge(alv, 2 * gi + 1)
                v.reciprocal(l_inv[:, :], l_sb[:, :]).then_inc(alv, 1)
                v.wait_ge(albc, gi + 1)
                v.tensor_copy(lbc_sb[:, :], l_ps2[gi % 2][:, :]).then_inc(alev, 1)
                v.wait_ge(alev, gi + 1)
                v.tensor_mul(ctx_sb[:, h * S + g * 512: h * S + (g + 1) * 512],
                             ctx_ps2[gi % 2][:, :], lbc_sb[:, :]).then_inc(acev, 1)

        @block.scalar
        def _(sc):
            for gi, (h, g) in enumerate(groups):
                for (cj, j, diag, mc) in gj_of[(h, g)]:
                    sc.wait_ge(amm, cj + 1)
                    if diag:
                        sc.wait_ge(amk, mc + 1)
                    if cj >= 4:
                        sc.wait_ge(apv, cj - 3)
                    sc.activation(pt_sb[:, (cj % 4) * 512:(cj % 4 + 1) * 512],
                                  sp[cj % 4][:, :], AF.Exp,
                                  scale=float(1.0 / np.sqrt(HD))).then_inc(aex, 1)

        if a2a is None:
            @block.sync
            def _(s):
                s.wait_ge(acev, len(groups))
                for h in range(HPC):
                    s.dma_start(out=ctx_out[h * 128:(h + 1) * 128, :],
                                in_=ctx_sb[:, h * S:(h + 1) * S]).then_inc(cst, 16)
                s.wait_ge(cst, 32)
        else:
            ag1_ins, ag1_outs, ccsem = a2a

            @block.gpsimd
            def _(g):
                from concourse.aot_env import allow_ici_all_to_all
                if wo_early is not None:
                    w_out_dram, wl2 = wo_early
                    g.wait_ge(amm, gj)
                    for jj in range(4):
                        g.dma_start(
                            out=AP2(qk_sb, jj * 4 * H, [[NFC * H, 128], [H, 4], [1, H]]),
                            in_=AP2(w_out_dram, jj * 4 * 128 * H,
                                    [[H, 128], [128 * H, 4], [1, H]])
                        ).then_inc(wl2, 16)
                pieces = [(0, 0, TPC, 16), (1, 0, TPC, 32)]
                ncst = [0]
                for pi, (h, t0, tw, gate_ev) in enumerate(pieces):
                    g.wait_ge(acev, gate_ev)
                    g.dma_start(
                        out=AP2(ag1_ins[pi], 0,
                                [[tw, 128], [128 * tw, NCORE], [1, tw]]),
                        in_=AP2(ctx_sb, h * S + t0,
                                [[HPC * S, 128], [TPC, NCORE], [1, tw]])
                    ).then_inc(cst, 16)
                    ncst[0] += 16
                    g.wait_ge(cst, ncst[0])
                    with allow_ici_all_to_all():
                        g.collective_compute(
                            "AllToAll", ALU.bypass,
                            replica_groups=[list(range(NCORE))],
                            ins=[ag1_ins[pi].ap().opt()],
                            outs=[ag1_outs[pi].ap().opt()],
                        ).then_inc(ccsem, 1)


_IDENT_CACHE = {}


def nc_ident_ap(nc):
    key = id(nc)
    if key not in _IDENT_CACHE:
        _IDENT_CACHE[key] = nc.inline_tensor(np.eye(128, dtype=BF), name="ident_a")
    return _IDENT_CACHE[key].ap()


def _mask_inline(nc):
    m = np.full((128, 1024), MASKVAL, dtype=np.float32)
    for ki in range(128):
        m[ki, ki + 512:] = 0.0
    return nc.inline_tensor(m, name="mask_master")


def build_attn(debug=False):
    nc = bass.Bass("TRN2", target_bir_lowering=False)
    xn_full = nc.dram_tensor("xn_full", [NCORE * H, TPC], BF16, kind="ExternalInput")
    wqkv = nc.dram_tensor("wqkv", [H, 768], F32, kind="ExternalInput")
    bqkv2 = nc.dram_tensor("bqkv2", [128, 6], F32, kind="ExternalInput")
    ctx_out = nc.dram_tensor("ctx", [HPC * 128, S], BF16, kind="ExternalOutput")
    dbg = None
    if debug:
        dbg = (nc.dram_tensor("dq", [128, HPC * S], BF16, kind="ExternalOutput"),
               nc.dram_tensor("dk", [128, HPC * S], BF16, kind="ExternalOutput"),
               nc.dram_tensor("dv", [128, HPC * S], BF16, kind="ExternalOutput"))
    mask_inl = _mask_inline(nc)
    import contextlib
    with contextlib.ExitStack() as es:
        qk_sb = es.enter_context(nc.sbuf_tensor("qk_sb", [128, 2 * HPC * S], BF16))
        v_tm = es.enter_context(nc.sbuf_tensor("v_tm", [128, HPC * S], BF16))
        ctx_sb = es.enter_context(nc.sbuf_tensor("ctx_sb", [128, HPC * S], BF16))
        emit_attention(nc, xn_full, wqkv, bqkv2, ctx_out, mask_inl,
                       qk_sb, v_tm, ctx_sb, debug_qkv=dbg)
    return nc



def build_l3():
    nc = bass.Bass("TRN2", target_bir_lowering=False)
    ctx_my = nc.dram_tensor("ctx_my", [H, TPC], BF16, kind="ExternalInput")
    x_in = nc.dram_tensor("x", [TPC, H], F32, kind="ExternalInput")
    ffn_in = nc.dram_tensor("ffn", [H, TPC], F32, kind="ExternalInput")
    w_out = nc.dram_tensor("w_out", [H, H], F32, kind="ExternalInput")
    bout2 = nc.dram_tensor("bout2", [128, NFC], F32, kind="ExternalInput")
    y_out = nc.dram_tensor("y", [TPC, H], F32, kind="ExternalOutput")
    emit_final(nc, ctx_my, x_in, ffn_in, w_out, bout2, y_out)
    return nc


def emit_final(nc, ctx_my, x_in, ffn_in, w_out, bout2, y_out, gate=None,
               wo_pre=None):
    import contextlib
    NIT = 2 * NFC
    with contextlib.ExitStack() as es:
        E = es.enter_context
        ctx_sb = E(nc.sbuf_tensor("o_ctx", [128, NFC * TPC], BF16))
        if wo_pre is None:
            wo_sb = E(nc.sbuf_tensor("o_wo", [128, NFC * H], BF16))
        else:
            wo_sb, wl2_pre = wo_pre
        b2_sb = E(nc.sbuf_tensor("o_b2", [128, NFC], F32))
        id_f32 = E(nc.sbuf_tensor("o_idf", [128, 128], F32))
        fsl = E(nc.sbuf_tensor("o_f", [128, 2 * 512], F32))
        ssl = E(nc.sbuf_tensor("o_s", [128, 2 * 512], F32))
        xsl = E(nc.sbuf_tensor("o_x", [128, 4 * H], F32))
        osl = E(nc.sbuf_tensor("o_o", [128, 4 * H], F32))
        op0 = E(nc.psum_tensor("o_p0", [128, 512], F32))
        op1 = E(nc.psum_tensor("o_p1", [128, 512], F32))
        tp0 = E(nc.psum_tensor("o_t0", [128, 512], F32))
        tp1 = E(nc.psum_tensor("o_t1", [128, 512], F32))
        ld = E(nc.semaphore("o_ld"))
        wl2 = E(nc.semaphore("o_wl2"))
        fld = [E(nc.semaphore("o_fl0")), E(nc.semaphore("o_fl1"))]
        xld = [E(nc.semaphore("o_xl0")), E(nc.semaphore("o_xl1"))]
        omm = E(nc.semaphore("o_omm"))
        osm = E(nc.semaphore("o_osm"))
        ovv = E(nc.semaphore("o_ovv"))
        otm = E(nc.semaphore("o_otm"))
        oev = E(nc.semaphore("o_oev"))
        ost = [E(nc.semaphore("o_st0")), E(nc.semaphore("o_st1"))]
        op = [op0, op1]
        tp = [tp0, tp1]
        idf = nc.inline_tensor(np.eye(128, dtype=np.float32), name="ident_f32")

        with nc.Block(no_gpsimd_drain=True) as block:
            if wo_pre is None:
                @block.gpsimd
                def _(g):
                    for j in range(4):
                        g.dma_start(
                            out=AP2(wo_sb, j * 4 * H, [[NFC * H, 128], [H, 4], [1, H]]),
                            in_=AP2(w_out, j * 4 * 128 * H, [[H, 128], [128 * H, 4], [1, H]])
                        ).then_inc(wl2, 16)
            else:
                wl2 = wl2_pre

            @block.sync
            def _(s):
                s.dma_start(out=b2_sb[:, :], in_=bout2[:, :]).then_inc(ld, 16)
                s.dma_start(out=id_f32[:, :], in_=idf.ap()).then_inc(ld, 16)
                for tcq in range(4):
                    s.dma_start(out=xsl[:, tcq * H:(tcq + 1) * H],
                                in_=x_in[tcq * 128:(tcq + 1) * 128, :]
                                ).then_inc(xld[0], 16)
                for i0 in range(2):
                    s.dma_start(out=fsl[:, i0 * 512:(i0 + 1) * 512],
                                in_=ffn_in[i0 * 128:(i0 + 1) * 128, 0:512]
                                ).then_inc(fld[i0], 16)
                if gate is not None:
                    s.wait_ge(gate[0], gate[1])
                if isinstance(ctx_my, list):
                    pieces = [(0, 0, TPC, ctx_my[0]),
                              (NFC // 2, 0, TPC, ctx_my[1])]
                    nparts = 2
                    for cb, co, tw, part in pieces:
                        s.dma_start(out=AP2(ctx_sb, cb * TPC + co,
                                            [[NFC * TPC, 128], [TPC, NFC // 2], [1, tw]]),
                                    in_=AP2(part, 0, [[tw, 128], [128 * tw, NFC // 2], [1, tw]])
                                    ).then_inc(ld, 16)
                else:
                    nparts = 1
                    s.dma_start(out=AP2(ctx_sb, 0, [[NFC * TPC, 128], [TPC, NFC], [1, TPC]]),
                                in_=AP2(ctx_my, 0, [[TPC, 128], [128 * TPC, NFC], [1, TPC]])
                                ).then_inc(ld, 16)
                s.wait_ge(ld, 32 + 16 * nparts)
                for tb2 in range(2):
                    if tb2 == 1:
                        s.wait_ge(oev, NFC)
                        for tc in range(4):
                            s.dma_start(out=y_out[tc * 128:(tc + 1) * 128, :],
                                        in_=osl[:, tc * H:(tc + 1) * H]
                                        ).then_inc(ost[tc % 2], 16)
                        for tcq in range(4):
                            tc = 4 + tcq
                            s.dma_start(out=xsl[:, tcq * H:(tcq + 1) * H],
                                        in_=x_in[tc * 128:(tc + 1) * 128, :]
                                        ).then_inc(xld[1], 16)
                    for hc in range(NFC):
                        i = tb2 * NFC + hc
                        sl = i % 2
                        if i < 2:
                            continue
                        s.wait_ge(ovv, i - 1)
                        s.dma_start(out=fsl[:, sl * 512:(sl + 1) * 512],
                                    in_=ffn_in[hc * 128:(hc + 1) * 128,
                                               tb2 * 512:(tb2 + 1) * 512]
                                    ).then_inc(fld[sl], 16)
                s.wait_ge(oev, 2 * NFC)
                for tc in range(4, 8):
                    s.dma_start(out=y_out[tc * 128:(tc + 1) * 128, :],
                                in_=osl[:, (tc - 4) * H:(tc - 3) * H]
                                ).then_inc(ost[tc % 2], 16)
                s.wait_ge(ost[0], 16 * 4)
                s.wait_ge(ost[1], 16 * 4)

            def _tr(t, i):
                t.wait_ge(ovv, i + 1)
                if i >= 2:
                    t.wait_ge(oev, i - 1)
                for q in range(4):
                    inst = t.transpose(
                        tp[i % 2][:, q * 128:(q + 1) * 128],
                        ssl[:, (i % 2) * 512 + q * 128:(i % 2) * 512 + (q + 1) * 128],
                        id_f32[:, :])
                inst.then_inc(otm, 1)

            @block.tensor
            def _(t):
                t.wait_ge(wl2, 64)
                t.wait_ge(ld, 32 + 16 * (2 if isinstance(ctx_my, list) else 1))
                for i in range(NIT):
                    tb2, hc = i // NFC, i % NFC
                    if i >= 2:
                        t.wait_ge(osm, i - 1)
                    for fc in range(NFC):
                        inst = t.matmul(
                            op[i % 2][:, :],
                            wo_sb[:, fc * H + hc * 128: fc * H + (hc + 1) * 128],
                            ctx_sb[:, fc * TPC + tb2 * 512: fc * TPC + (tb2 + 1) * 512],
                            start=(fc == 0), stop=(fc == NFC - 1))
                    inst.then_inc(omm, 1)
                    if i >= 1:
                        _tr(t, i - 1)
                _tr(t, NIT - 1)

            @block.scalar
            def _(sc):
                for i in range(NIT):
                    sl = i % 2
                    hc = i % NFC
                    sc.wait_ge(omm, i + 1)
                    if i >= 2:
                        sc.wait_ge(otm, i - 1)
                    sc.activation(ssl[:, sl * 512:(sl + 1) * 512], op[i % 2][:, :],
                                  AF.Identity,
                                  bias=b2_sb[:, hc:hc + 1]).then_inc(osm, 1)

            def _ev(v, i):
                tb2, hc = i // NFC, i % NFC
                v.wait_ge(otm, i + 1)
                v.wait_ge(xld[tb2], 16 * 4)
                if tb2 == 1 and hc == 0:
                    v.wait_ge(ost[0], 32)
                    v.wait_ge(ost[1], 32)
                for q in range(4):
                    inst = v.scalar_tensor_tensor(
                        osl[:, q * H + hc * 128: q * H + (hc + 1) * 128],
                        tp[i % 2][:, q * 128:(q + 1) * 128], 1.0,
                        xsl[:, q * H + hc * 128: q * H + (hc + 1) * 128],
                        ALU.mult, ALU.add)
                inst.then_inc(oev, 1)

            @block.vector
            def _(v):
                for i in range(NIT):
                    sl = i % 2
                    if i >= 1:
                        _ev(v, i - 1)
                    v.wait_ge(osm, i + 1)
                    v.wait_ge(fld[sl], 16 * (i // 2 + 1))
                    v.tensor_add(ssl[:, sl * 512:(sl + 1) * 512],
                                 ssl[:, sl * 512:(sl + 1) * 512],
                                 fsl[:, sl * 512:(sl + 1) * 512]).then_inc(ovv, 1)
                _ev(v, NIT - 1)




def kernel_unfused(L0_attn_hidden_states, L0_attn_w_qkv, L0_attn_b_qkv, L0_attn_w_out,
           L0_attn_b_out, L0_ffn_SU_ug, L0_ffn_SU_dp, L0_ffn_Wscale_ug,
           L0_ffn_Wscale_dp, L0_ffn_W_up, L0_ffn_W_gate, L0_ffn_W_down):
    x = np.ascontiguousarray(
        np.asarray(L0_attn_hidden_states, dtype=np.float32).reshape(S, H))
    w_qkv = np.asarray(L0_attn_w_qkv, dtype=np.float32)
    b_qkv = np.asarray(L0_attn_b_qkv, dtype=np.float32)
    w_out = np.ascontiguousarray(np.asarray(L0_attn_w_out, dtype=np.float32))
    b_out = np.asarray(L0_attn_b_out, dtype=np.float32)
    su_ug = np.asarray(L0_ffn_SU_ug, dtype=np.float32)
    su_dp = np.asarray(L0_ffn_SU_dp, dtype=np.float32)
    ws_ug = np.asarray(L0_ffn_Wscale_ug, dtype=np.float32)
    ws_dp = np.asarray(L0_ffn_Wscale_dp, dtype=np.float32)
    w_upT = np.ascontiguousarray(np.asarray(L0_ffn_W_up, dtype=np.float32).T)
    w_gateT = np.ascontiguousarray(np.asarray(L0_ffn_W_gate, dtype=np.float32).T)
    w_downT = np.ascontiguousarray(np.asarray(L0_ffn_W_down, dtype=np.float32).T)

    cores = list(range(NCORE))

    nc1 = build_l1()
    res1 = run_bass_kernel_spmd(
        nc1, [{"x": x[c * TPC:(c + 1) * TPC]} for c in cores], core_ids=cores)
    xn_shards = [np.asarray(r["xn_fm"]) for r in res1.results]
    xn_full = np.ascontiguousarray(np.concatenate(xn_shards, axis=0))

    nc2 = build_ffn()
    ffn_maps = []
    for c in cores:
        ffn_maps.append({
            "xn_own": xn_shards[c],
            "su_ug2": np.ascontiguousarray(su_ug.reshape(NFC, 128).T),
            "su_dp2": np.ascontiguousarray(su_dp.reshape(NIC, 128).T),
            "wsu2": np.ascontiguousarray(ws_ug[:INTER].reshape(NIC, 128).T),
            "wsg2": np.ascontiguousarray(ws_ug[INTER:].reshape(NIC, 128).T),
            "wsdp2": np.ascontiguousarray(ws_dp.reshape(NFC, 128).T),
            "w_upT": w_upT, "w_gateT": w_gateT, "w_downT": w_downT,
        })
    res2 = run_bass_kernel_spmd(nc2, ffn_maps, core_ids=cores)
    ffn_shards = [np.asarray(r["ffn_out"]) for r in res2.results]

    nc3 = build_attn()
    attn_maps = []
    for c in cores:
        attn_maps.append({
            "xn_full": xn_full,
            "wqkv": np.ascontiguousarray(w_qkv[:, 768 * c:768 * (c + 1)]),
            "bqkv2": np.ascontiguousarray(
                b_qkv[768 * c:768 * (c + 1)].reshape(6, 128).T),
        })
    res3 = run_bass_kernel_spmd(nc3, attn_maps, core_ids=cores)
    ctx_full = np.concatenate([np.asarray(r["ctx"]) for r in res3.results], axis=0)

    nc4 = build_l3()
    l3_maps = []
    for c in cores:
        l3_maps.append({
            "ctx_my": np.ascontiguousarray(ctx_full[:, c * TPC:(c + 1) * TPC]),
            "x": x[c * TPC:(c + 1) * TPC],
            "ffn": ffn_shards[c],
            "w_out": w_out,
            "bout2": np.ascontiguousarray(b_out.reshape(NFC, 128).T),
        })
    res4 = run_bass_kernel_spmd(nc4, l3_maps, core_ids=cores)
    y = np.concatenate([np.asarray(r["y"]) for r in res4.results], axis=0)
    return y.reshape(1, S, H)



def build_fused():
    nc = bass.Bass("TRN2", target_bir_lowering=False)
    x_in = nc.dram_tensor("x", [TPC, H], F32, kind="ExternalInput")
    wqkv = nc.dram_tensor("wqkv", [H, 768], F32, kind="ExternalInput")
    bqkv2 = nc.dram_tensor("bqkv2", [128, 6], F32, kind="ExternalInput")
    w_out = nc.dram_tensor("w_out", [H, H], F32, kind="ExternalInput")
    bout2 = nc.dram_tensor("bout2", [128, NFC], F32, kind="ExternalInput")
    su_ug2 = nc.dram_tensor("su_ug2", [128, NFC], F32, kind="ExternalInput")
    su_dp2 = nc.dram_tensor("su_dp2", [128, NIC], F32, kind="ExternalInput")
    wsu2 = nc.dram_tensor("wsu2", [128, NIC], F32, kind="ExternalInput")
    wsg2 = nc.dram_tensor("wsg2", [128, NIC], F32, kind="ExternalInput")
    wsdp2 = nc.dram_tensor("wsdp2", [128, NFC], F32, kind="ExternalInput")
    w_ugT = nc.dram_tensor("w_ugT", [H, 2 * INTER], F32, kind="ExternalInput")
    w_downT = nc.dram_tensor("w_downT", [INTER, H], F32, kind="ExternalInput")
    y_out = nc.dram_tensor("y", [TPC, H], F32, kind="ExternalOutput")

    h128f = nc.inline_tensor(_hadamard(128).astype(np.float32), name="h128f")
    mask_inl = _mask_inline(nc)

    ag0_in = nc.dram_tensor("ag0_in", [H, TPC], BF16, kind="Internal")
    ag0_out = nc.dram_tensor("ag0_out", [NCORE * H, TPC], BF16, kind="Internal",
                             addr_space="Shared")
    _a2a_shapes = [TPC, TPC]
    ag1_ins = [nc.dram_tensor(f"ag1_in{i}", [NCORE * 128, w], BF16, kind="Internal")
               for i, w in enumerate(_a2a_shapes)]
    ag1_outs = [nc.dram_tensor(f"ag1_out{i}", [NCORE * 128, w], BF16, kind="Internal")
                for i, w in enumerate(_a2a_shapes)]
    ffn_dram = nc.dram_tensor("ffn_sp", [H, TPC], F32, kind="Internal")

    import contextlib
    with contextlib.ExitStack() as es:
        cc = es.enter_context(nc.semaphore("cc"))
        ag0st = es.enter_context(nc.semaphore("ag0st"))
        esf = es.enter_context(contextlib.ExitStack())
        d = _ffn_persist_alloc(nc, esf)
        esu = es.enter_context(contextlib.ExitStack())
        d["tbl_ug"] = esu.enter_context(nc.sbuf_tensor("tbl_ug", [128, NFC * 128], BF16))
        with nc.named_scope("tables"):
            emit_ffn_tables(nc, d, su_ug2, su_dp2, wsu2, wsg2, wsdp2, h128f)
        def _trigger_ag0(g):
            g.wait_ge(ag0st, 16)
            g.collective_compute(
                "AllGather", ALU.bypass,
                replica_groups=[list(range(NCORE))],
                ins=[ag0_in.ap().opt()], outs=[ag0_out.ap().opt()],
            ).then_inc(cc, 1)

        with nc.sbuf_tensor("xn_sb", [128, NFC * TPC], BF16) as xn_sb:
            with nc.named_scope("ln"):
                emit_ln(nc, x_in, xn_sb)
            with nc.Block(no_gpsimd_drain=True) as block:
                @block.gpsimd
                def _(g):
                    g.dma_start(
                        out=AP2(ag0_in, 0, [[TPC, 128], [128 * TPC, NFC], [1, TPC]]),
                        in_=AP2(xn_sb, 0, [[NFC * TPC, 128], [TPC, NFC], [1, TPC]])
                    ).then_inc(ag0st, 16)
            with nc.named_scope("yhad"):
                emit_ffn_yhad(nc, d, xn_sb)
        esu.close()
        with nc.named_scope("ffn"):
            emit_ffn_main(nc, d, w_ugT, w_downT, ffn_dram,
                          pre_gpsimd=_trigger_ag0)
        esf.close()
        qk_sb = es.enter_context(nc.sbuf_tensor("qk_sb", [128, 2 * HPC * S], BF16))
        wl2e = es.enter_context(nc.semaphore("wl2e"))
        with contextlib.ExitStack() as es2:
            E = es2.enter_context
            ctx_sb = E(nc.sbuf_tensor("ctx_sb", [128, HPC * S], BF16))
            v_tm = E(nc.sbuf_tensor("v_tm", [128, HPC * S], BF16))
            with nc.named_scope("attn"):
                emit_attention(nc, ag0_out, wqkv, bqkv2, None, mask_inl,
                               qk_sb, v_tm, ctx_sb,
                               gate=(cc, 1), a2a=(ag1_ins, ag1_outs, cc),
                               wo_early=(w_out, wl2e))
        with nc.named_scope("final"):
            emit_final(nc, ag1_outs, x_in, ffn_dram, w_out, bout2, y_out,
                       gate=(cc, 3), wo_pre=(qk_sb, wl2e))
    return nc


def _host_maps(x, w_qkv, b_qkv, w_out, b_out, su_ug, su_dp, ws_ug, ws_dp,
               w_upT, w_gateT, w_downT):
    maps = []
    head_order = [2 * r for r in range(NCORE)] + [2 * r + 1 for r in range(NCORE)]
    feat_perm = np.concatenate([np.arange(128) + hd * 128 for hd in head_order])
    w_out = np.ascontiguousarray(w_out[feat_perm, :])
    WB = 128
    w_ugT = np.ascontiguousarray(
        np.stack([w_upT.reshape(H, INTER // WB, WB),
                  w_gateT.reshape(H, INTER // WB, WB)], axis=2).reshape(H, 2 * INTER))
    su_ug2 = np.ascontiguousarray(su_ug.reshape(NFC, 128).T)
    su_dp2 = np.ascontiguousarray(su_dp.reshape(NIC, 128).T)
    wsu2 = np.ascontiguousarray(ws_ug[:INTER].reshape(NIC, 128).T)
    wsg2 = np.ascontiguousarray(ws_ug[INTER:].reshape(NIC, 128).T)
    wsdp2 = np.ascontiguousarray(ws_dp.reshape(NFC, 128).T)
    bout2 = np.ascontiguousarray(b_out.reshape(NFC, 128).T)
    for c in range(NCORE):
        maps.append({
            "x": x[c * TPC:(c + 1) * TPC],
            "wqkv": np.ascontiguousarray(w_qkv[:, 768 * c:768 * (c + 1)]),
            "bqkv2": np.ascontiguousarray(
                b_qkv[768 * c:768 * (c + 1)].reshape(6, 128).T),
            "w_out": w_out, "bout2": bout2,
            "su_ug2": su_ug2, "su_dp2": su_dp2, "wsu2": wsu2, "wsg2": wsg2,
            "wsdp2": wsdp2,
            "w_ugT": w_ugT, "w_downT": w_downT,
        })
    return maps


def kernel_fused(L0_attn_hidden_states, L0_attn_w_qkv, L0_attn_b_qkv,
                 L0_attn_w_out, L0_attn_b_out, L0_ffn_SU_ug, L0_ffn_SU_dp,
                 L0_ffn_Wscale_ug, L0_ffn_Wscale_dp, L0_ffn_W_up,
                 L0_ffn_W_gate, L0_ffn_W_down):
    x = np.ascontiguousarray(
        np.asarray(L0_attn_hidden_states, dtype=np.float32).reshape(S, H))
    nc = build_fused()
    maps = _host_maps(
        x,
        np.asarray(L0_attn_w_qkv, dtype=np.float32),
        np.asarray(L0_attn_b_qkv, dtype=np.float32),
        np.ascontiguousarray(np.asarray(L0_attn_w_out, dtype=np.float32)),
        np.asarray(L0_attn_b_out, dtype=np.float32),
        np.asarray(L0_ffn_SU_ug, dtype=np.float32),
        np.asarray(L0_ffn_SU_dp, dtype=np.float32),
        np.asarray(L0_ffn_Wscale_ug, dtype=np.float32),
        np.asarray(L0_ffn_Wscale_dp, dtype=np.float32),
        np.ascontiguousarray(np.asarray(L0_ffn_W_up, dtype=np.float32).T),
        np.ascontiguousarray(np.asarray(L0_ffn_W_gate, dtype=np.float32).T),
        np.ascontiguousarray(np.asarray(L0_ffn_W_down, dtype=np.float32).T))
    res = run_bass_kernel_spmd(nc, maps, core_ids=list(range(NCORE)))
    y = np.concatenate([np.asarray(r["y"]) for r in res.results], axis=0)
    return y.reshape(1, S, H)


kernel = kernel_fused


# version 16 (final)
# speedup vs baseline: 1.0767x; 1.0192x over previous
import numpy as np
import ml_dtypes

import concourse.bass as bass
import concourse.mybir as mybir
from concourse.bass_utils import run_bass_kernel_spmd

F32 = mybir.dt.float32
BF16 = mybir.dt.bfloat16
BF = ml_dtypes.bfloat16
AF = mybir.ActivationFunctionType
ALU = mybir.AluOpType
AX = mybir.AxisListType

NCORE = 8
S = 8192
H = 2048
NH = 16
HD = 128
INTER = 8192
TPC = S // NCORE
HPC = NH // NCORE
NFC = H // 128
NIC = INTER // 128
MASKVAL = -1.0e9
EPS = 1e-5


def _hadamard(n):
    h = np.array([[1.0]], dtype=np.float64)
    while h.shape[0] < n:
        h = np.block([[h, h], [h, -h]])
    return h


def AP2(t, off, dims):
    return bass.AP(t, int(off), [list(d) for d in dims])



def build_l1():
    nc = bass.Bass("TRN2", target_bir_lowering=False)
    x_in = nc.dram_tensor("x", [TPC, H], F32, kind="ExternalInput")
    xn_out = nc.dram_tensor("xn_fm", [H, TPC], BF16, kind="ExternalOutput")
    with nc.sbuf_tensor("xnf_sb", [128, NFC * TPC], BF16) as xn_fm:
        emit_ln(nc, x_in, xn_fm)
        with nc.semaphore("l1_st") as st, nc.Block(no_gpsimd_drain=True) as block:
            @block.sync
            def _(s):
                s.dma_start(out=AP2(xn_out, 0, [[TPC, 128], [128 * TPC, NFC], [1, TPC]]),
                            in_=AP2(xn_fm, 0, [[NFC * TPC, 128], [TPC, NFC], [1, TPC]])
                            ).then_inc(st, 16)
                s.wait_ge(st, 16)
    return nc


def emit_ln(nc, x_in, xn_fm):
    ident = nc.inline_tensor(np.eye(128, dtype=BF), name="ident")

    ntc = TPC // 128
    with (
        nc.sbuf_tensor("xbuf", [128, 2 * H], F32) as xbuf,
        nc.sbuf_tensor("sq", [128, H], F32) as sq,
        nc.sbuf_tensor("stats", [128, 8], F32) as stats,
        nc.sbuf_tensor("xn_tm", [128, 2 * H], BF16) as xn_tm,
        nc.sbuf_tensor("ident_sb", [128, 128], BF16) as ident_sb,
        nc.psum_tensor("tpA", [128, 1024], BF16) as tpA,
        nc.psum_tensor("tpB", [128, 1024], BF16) as tpB,
        nc.semaphore("ld0") as ld0,
        nc.semaphore("ld1") as ld1,
        nc.semaphore("idl") as idl,
        nc.semaphore("vst") as vst,
        nc.semaphore("sst") as sst,
        nc.semaphore("tmm") as tmm,
        nc.semaphore("tev") as tev,
        nc.semaphore("vsy") as vsy,
        nc.semaphore("st") as st,
        nc.Block(no_gpsimd_drain=True) as block,
    ):
        lds = [ld0, ld1]
        tp = [tpA, tpB]

        @block.sync
        def _(s):
            s.dma_start(out=ident_sb[:, :], in_=ident.ap()).then_inc(idl, 16)
            for tc in range(ntc):
                if tc >= 2:
                    s.wait_ge(tmm, (tc - 1) * 16)
                s.dma_start(out=xbuf[:, (tc % 2) * H:(tc % 2) * H + H],
                            in_=x_in[tc * 128:(tc + 1) * 128, :]
                            ).then_inc(lds[tc % 2], 16)

        @block.scalar
        def _(sc):
            for tc in range(ntc):
                sl = (tc % 2) * H
                sc.wait_ge(lds[tc % 2], (tc // 2 + 1) * 16)
                sc.activation(sq[:, :], xbuf[:, sl:sl + H], AF.Square,
                              accum_out=stats[:, 2:3]).then_inc(sst, 1)
                sc.wait_ge(vst, 5 * tc + 3)
                sc.activation(stats[:, 4:5], stats[:, 3:4], AF.Sqrt).then_inc(sst, 1)

        @block.vector
        def _(v):
            vsn = [0]

            def vsy_inc(inst):
                vsn[0] += 1
                inst.then_inc(vsy, 1)

            def vsy_wait():
                v.wait_ge(vsy, vsn[0])

            for tc in range(ntc):
                sl = (tc % 2) * H
                v.wait_ge(lds[tc % 2], (tc // 2 + 1) * 16)
                vsy_inc(v.reduce_sum(stats[:, 0:1], xbuf[:, sl:sl + H], axis=AX.X))
                v.wait_ge(sst, 2 * tc + 1)
                vsy_wait()
                vsy_inc(v.tensor_scalar_mul(stats[:, 1:2], stats[:, 0:1], 1.0 / H))
                vsy_wait()
                v.tensor_mul(stats[:, 5:6], stats[:, 1:2], stats[:, 1:2])
                vsy_inc(v.tensor_scalar(stats[:, 3:4], stats[:, 2:3], 1.0 / H, EPS,
                                        ALU.mult, ALU.add))
                vsy_wait()
                v.tensor_sub(stats[:, 3:4], stats[:, 3:4], stats[:, 5:6]
                             ).then_inc(vst, 3)
                v.wait_ge(sst, 2 * tc + 2)
                vsy_inc(v.reciprocal(stats[:, 6:7], stats[:, 4:5]))
                vsy_wait()
                v.tensor_scalar(xn_tm[:, sl:sl + H], xbuf[:, sl:sl + H],
                                stats[:, 1:2], stats[:, 6:7],
                                ALU.subtract, ALU.mult).then_inc(vst, 2)
                if tc >= 1:
                    for half in range(2):
                        v.wait_ge(tmm, (tc - 1) * 16 + (half + 1) * 8)
                        dst = AP2(xn_fm, (half * 8) * TPC + (tc - 1) * 128,
                                  [[NFC * TPC, 128], [TPC, 8], [1, 128]])
                        v.tensor_copy(dst, tp[half][:, 0:1024]).then_inc(tev, 1)
            for half in range(2):
                v.wait_ge(tmm, (ntc - 1) * 16 + (half + 1) * 8)
                dst = AP2(xn_fm, (half * 8) * TPC + (ntc - 1) * 128,
                          [[NFC * TPC, 128], [TPC, 8], [1, 128]])
                v.tensor_copy(dst, tp[half][:, 0:1024]).then_inc(tev, 1)

        @block.tensor
        def _(t):
            t.wait_ge(idl, 16)
            for tc in range(ntc):
                sl = (tc % 2) * H
                t.wait_ge(vst, 5 * tc + 5)
                for half in range(2):
                    if tc >= 1:
                        t.wait_ge(tev, (tc - 1) * 2 + half + 1)
                    for j in range(8):
                        fc = half * 8 + j
                        inst = t.transpose(tp[half][:, j * 128:(j + 1) * 128],
                                           xn_tm[:, sl + fc * 128:sl + (fc + 1) * 128],
                                           ident_sb[:, :])
                    inst.then_inc(tmm, 8)






def _ffn_persist_alloc(nc, es):
    d = {}
    d["tbl_dp"] = es.enter_context(nc.sbuf_tensor("tbl_dp", [128, NIC * 128], BF16))
    d["wsg"] = es.enter_context(nc.sbuf_tensor("wsg", [128, NIC], F32))
    d["wsdp2"] = es.enter_context(nc.sbuf_tensor("wsdp2_sb", [128, NFC], F32))
    d["y_dram"] = nc.dram_tensor("y_dram", [H, TPC], BF16, kind="Internal")
    return d


def emit_ffn_tables(nc, d, su_ug2, su_dp2, wsu2, wsg2, wsdp2_in, h128f):
    with (
        nc.sbuf_tensor("t_suug", [128, NFC], F32) as suug,
        nc.sbuf_tensor("t_sudp", [128, NIC], F32) as sudp,
        nc.sbuf_tensor("t_wsu", [128, NIC], F32) as wsu,
        nc.sbuf_tensor("t_h128", [128, 128], F32) as h128,
        nc.sbuf_tensor("t_scomb", [128, NIC], F32) as scomb,
        nc.semaphore("tb_ld") as ld,
        nc.semaphore("tb_v") as vs,
    ):
        with nc.Block(no_gpsimd_drain=True) as block:
            @block.sync
            def _(s):
                s.dma_start(out=suug[:, :], in_=su_ug2[:, :]).then_inc(ld, 16)
                s.dma_start(out=sudp[:, :], in_=su_dp2[:, :]).then_inc(ld, 16)
                s.dma_start(out=wsu[:, :], in_=wsu2[:, :]).then_inc(ld, 16)
                s.dma_start(out=d["wsg"][:, :], in_=wsg2[:, :]).then_inc(ld, 16)
                s.dma_start(out=d["wsdp2"][:, :], in_=wsdp2_in[:, :]).then_inc(ld, 16)
                s.dma_start(out=h128[:, :], in_=h128f.ap()).then_inc(ld, 16)

            @block.vector
            def _(v):
                v.wait_ge(ld, 96)
                vseq = [0]

                def stamp(inst):
                    vseq[0] += 1
                    inst.then_inc(vs, 1)

                stamp(v.tensor_scalar_mul(suug[:, :], suug[:, :], 1.0 / np.sqrt(2048.0)))
                v.wait_ge(vs, vseq[0])
                for c in range(NFC):
                    v.tensor_scalar(d["tbl_ug"][:, c * 128:(c + 1) * 128], h128[:, :],
                                    suug[:, c:c + 1], None, ALU.mult)
                stamp(v.tensor_mul(scomb[:, :], sudp[:, :], wsu[:, :]))
                v.wait_ge(vs, vseq[0])
                stamp(v.tensor_scalar_mul(scomb[:, :], scomb[:, :], 1.0 / np.sqrt(8192.0)))
                v.wait_ge(vs, vseq[0])
                for c in range(NIC):
                    v.tensor_scalar(d["tbl_dp"][:, c * 128:(c + 1) * 128], h128[:, :],
                                    scomb[:, c:c + 1], None, ALU.mult)


def emit_ffn_yhad(nc, d, xn_own_sb, pre_gpsimd=None):
    with (
        nc.sbuf_tensor("yz_a", [128, NFC * 512], F32) as za,
        nc.sbuf_tensor("yz_b", [128, NFC * 512], F32) as zb,
        nc.sbuf_tensor("y_sb", [128, NFC * TPC], BF16) as y_sb,
        nc.psum_tensor("y_ps0", [128, 512], F32) as yp0,
        nc.psum_tensor("y_ps1", [128, 512], F32) as yp1,
        nc.semaphore("yh_m") as mm,
        nc.semaphore("yh_e") as ev,
        nc.semaphore("yh_v") as vs,
        nc.semaphore("yh_st") as yst,
        nc.Block(no_gpsimd_drain=True) as block,
    ):
        yp = [yp0, yp1]

        if pre_gpsimd is not None:
            @block.gpsimd
            def _(g):
                pre_gpsimd(g)

        @block.tensor
        def _(t):
            for th in range(2):
                for c in range(NFC):
                    i = th * NFC + c
                    if i >= 2:
                        t.wait_ge(ev, i - 1)
                    t.matmul(yp[i % 2][:, :], d["tbl_ug"][:, c * 128:(c + 1) * 128],
                             xn_own_sb[:, c * TPC + th * 512: c * TPC + th * 512 + 512],
                             start=True, stop=True).then_inc(mm, 1)

        @block.vector
        def _(v):
            vseq = [0]

            def stamp(inst):
                vseq[0] += 1
                inst.then_inc(vs, 1)

            for th in range(2):
                for c in range(NFC):
                    i = th * NFC + c
                    v.wait_ge(mm, i + 1)
                    v.tensor_copy(za[:, c * 512:(c + 1) * 512], yp[i % 2][:, :]
                                  ).then_inc(ev, 1)
                v.wait_ge(ev, (th + 1) * NFC)
                src, dst = za, zb
                for si, dd in enumerate([1, 2, 4, 8]):
                    last = si == 3
                    nq = NFC // (2 * dd)
                    sA = AP2(src, 0, [[NFC * 512, 128], [2 * dd * 512, nq], [512, dd], [1, 512]])
                    sB = AP2(src, dd * 512, [[NFC * 512, 128], [2 * dd * 512, nq], [512, dd], [1, 512]])
                    if last:
                        dlo = AP2(y_sb, th * 512, [[NFC * TPC, 128], [2 * dd * TPC, nq], [TPC, dd], [1, 512]])
                        dhi = AP2(y_sb, dd * TPC + th * 512, [[NFC * TPC, 128], [2 * dd * TPC, nq], [TPC, dd], [1, 512]])
                    else:
                        dlo = AP2(dst, 0, [[NFC * 512, 128], [2 * dd * 512, nq], [512, dd], [1, 512]])
                        dhi = AP2(dst, dd * 512, [[NFC * 512, 128], [2 * dd * 512, nq], [512, dd], [1, 512]])
                    v.tensor_tensor(dlo, sA, sB, ALU.add)
                    stamp(v.tensor_tensor(dhi, sA, sB, ALU.subtract))
                    v.wait_ge(vs, vseq[0])
                    src, dst = dst, src

        @block.sync
        def _(s):
            s.wait_ge(vs, 8)
            s.dma_start(out=AP2(d["y_dram"], 0, [[TPC, 128], [128 * TPC, NFC], [1, TPC]]),
                        in_=AP2(y_sb, 0, [[NFC * TPC, 128], [TPC, NFC], [1, TPC]])
                        ).then_inc(yst, 16)
            s.wait_ge(yst, 16)


def emit_ffn_main(nc, d, w_ugT, w_downT, ffn_out, pre_gpsimd=None):
    WB = 128
    NB = INTER // WB
    import contextlib
    with contextlib.ExitStack() as hs:
        hbuf = [hs.enter_context(nc.sbuf_tensor(f"f_h{t}", [128, NIC * 512], BF16))
                for t in range(2)]
        z1a = hs.enter_context(nc.sbuf_tensor("f_z1a", [128, NIC * 128], BF16))
        yld = hs.enter_context(nc.semaphore("f_yld"))
        ys = hs.enter_context(contextlib.ExitStack())
        yh = ys.enter_context(nc.sbuf_tensor("f_yh", [128, NFC * 512], BF16))
        nyld = [0]

        def ug_scope(es, th):
            E = es.enter_context
            g = {}
            g["wub"] = E(nc.sbuf_tensor(f"f_wu{th}", [128, 3 * NFC * 2 * WB], BF16))
            g["gb"] = E(nc.sbuf_tensor(f"f_g{th}", [128, 512], F32))
            g["pu"] = [E(nc.psum_tensor(f"f_pu{j}{th}", [128, 512], F32)) for j in range(3)]
            g["pg"] = [E(nc.psum_tensor(f"f_pg{j}{th}", [128, 512], F32)) for j in range(3)]
            g["wlu"] = [E(nc.semaphore(f"f_wlu{j}{th}")) for j in range(3)]
            g["mmu"] = E(nc.semaphore(f"f_mmu{th}"))
            g["mmg"] = E(nc.semaphore(f"f_mmg{th}"))
            g["sil"] = E(nc.semaphore(f"f_sil{th}"))
            g["hdn"] = E(nc.semaphore(f"f_hdn{th}"))
            return g

        def fw_scope(es, th):
            E = es.enter_context
            f = {}
            f["s1p"] = [E(nc.psum_tensor(f"f_s1p{j}{th}", [128, 512], F32)) for j in range(2)]
            f["s1m"] = E(nc.semaphore(f"f_s1m{th}"))
            f["s1e"] = E(nc.semaphore(f"f_s1e{th}"))
            f["bfv"] = E(nc.semaphore(f"f_bfv{th}"))
            return f

        def dn_scope(es, th):
            E = es.enter_context
            n = {}
            n["wdn"] = E(nc.sbuf_tensor(f"f_wd{th}", [128, 2 * NIC * 128], BF16))
            n["slab"] = E(nc.sbuf_tensor(f"f_sl{th}", [128, 2 * 512], F32))
            n["dp"] = [E(nc.psum_tensor(f"f_dp{j}{th}", [128, 512], F32)) for j in range(2)]
            n["wdl"] = [[E(nc.semaphore(f"f_wdl{j}{k}{th}")) for k in range(2)]
                        for j in range(2)]
            n["dmm"] = E(nc.semaphore(f"f_dmm{th}"))
            n["dev"] = E(nc.semaphore(f"f_dev{th}"))
            n["fst"] = [E(nc.semaphore(f"f_fst{j}{th}")) for j in range(2)]
            return n

        def ug_gpsimd_dma(gps, g, th, ib):
            slot = ib % 3
            if ib >= 3:
                gps.wait_ge(g["mmg"], ib - 2)
            gps.dma_start(
                out=AP2(g["wub"], slot * NFC * 2 * WB,
                        [[3 * NFC * 2 * WB, 128], [2 * WB, NFC], [1, 2 * WB]]),
                in_=AP2(w_ugT, ib * 2 * WB,
                        [[2 * INTER, 128], [128 * 2 * INTER, NFC], [1, 2 * WB]])
            ).then_inc(g["wlu"][slot], 16)

        def ug_tensor_chunk(t, g, th, c):
            ib, ps, slot = c, c % 3, c % 3
            t.wait_ge(yld, 16 * nyld[0])
            t.wait_ge(g["wlu"][slot], 16 * (ib // 3 + 1))
            if c >= 3:
                t.wait_ge(g["hdn"], c - 2)
                t.wait_ge(g["sil"], c - 2)
            base = slot * NFC * 2 * WB
            for hc in range(NFC):
                inst = t.matmul(g["pu"][ps][:, :],
                                g["wub"][:, base + hc * 2 * WB: base + hc * 2 * WB + 128],
                                yh[:, hc * 512:(hc + 1) * 512],
                                start=(hc == 0), stop=(hc == NFC - 1))
            inst.then_inc(g["mmu"], 1)
            for hc in range(NFC):
                inst = t.matmul(g["pg"][ps][:, :],
                                g["wub"][:, base + hc * 2 * WB + WB: base + hc * 2 * WB + WB + 128],
                                yh[:, hc * 512:(hc + 1) * 512],
                                start=(hc == 0), stop=(hc == NFC - 1))
            inst.then_inc(g["mmg"], 1)

        def ug_scalar_chunk(sc, g, th, c):
            sc.wait_ge(g["mmg"], c + 1)
            if c >= 1:
                sc.wait_ge(g["hdn"], c)
            sc.activation(g["gb"][:, :], g["pg"][c % 3][:, :], AF.Silu,
                          scale=d["wsg"][:, c:c + 1]).then_inc(g["sil"], 1)

        def ug_vector_chunk(v, g, th, c):
            v.wait_ge(g["mmu"], c + 1)
            v.wait_ge(g["sil"], c + 1)
            v.tensor_mul(hbuf[th][:, c * 512:(c + 1) * 512],
                         g["gb"][:, :], g["pu"][c % 3][:, :]).then_inc(g["hdn"], 1)

        def hslab(th, e, chunk_inner=128):
            return (hbuf[th], e * 128, 512)

        def bf_ap(buf_off_stride, dd, hi, cnt=128):
            buf, off, cs = buf_off_stride
            nq = NIC // (2 * dd)
            base = off + (dd * cs if hi else 0)
            return AP2(buf, base, [[NIC * 512 if cs == 512 else NIC * 128, 128],
                                   [2 * dd * cs, nq], [cs, dd], [1, cnt]])

        def fw_tensor_eighth(t, f, th, e):
            for grp in range(16):
                gg = e * 16 + grp
                if gg >= 2:
                    t.wait_ge(f["s1e"], gg - 1)
                for k in range(4):
                    c = grp * 4 + k
                    inst = t.matmul(
                        f["s1p"][gg % 2][:, k * 128:(k + 1) * 128],
                        d["tbl_dp"][:, c * 128:(c + 1) * 128],
                        hbuf[th][:, c * 512 + e * 128: c * 512 + e * 128 + 128],
                        start=True, stop=True)
                inst.then_inc(f["s1m"], 1)

        def fw_scalar_eighth(sc, f, th, e):
            for grp in range(16):
                gg = e * 16 + grp
                sc.wait_ge(f["s1m"], gg + 1)
                dst = AP2(hbuf[th], grp * 4 * 512 + e * 128,
                          [[NIC * 512, 128], [512, 4], [1, 128]])
                sc.activation(dst, f["s1p"][gg % 2][:, :], AF.Copy).then_inc(f["s1e"], 1)

        def fw_vector_gen(v, f, th, e):
            v.wait_ge(f["s1e"], (e + 1) * 16)
            hs_ = hslab(th, e)
            za_ = (z1a, 0, 128)
            src, dst = hs_, za_
            for si, dd in enumerate([1, 2, 4, 8, 16, 32]):
                sA = bf_ap(src, dd, False)
                sB = bf_ap(src, dd, True)
                dlo = bf_ap(dst, dd, False)
                dhi = bf_ap(dst, dd, True)
                v.tensor_tensor(dlo, sA, sB, ALU.add)
                yield
                v.tensor_tensor(dhi, sA, sB, ALU.subtract).then_inc(f["bfv"], 1)
                v.wait_ge(f["bfv"], e * 6 + si + 1)
                yield
                src, dst = dst, src

        def fw_vector_eighth(v, f, th, e):
            v.wait_ge(f["s1e"], (e + 1) * 16)
            hs_ = hslab(th, e)
            za_ = (z1a, 0, 128)
            src, dst = hs_, za_
            for si, dd in enumerate([1, 2, 4, 8, 16, 32]):
                sA = bf_ap(src, dd, False)
                sB = bf_ap(src, dd, True)
                dlo = bf_ap(dst, dd, False)
                dhi = bf_ap(dst, dd, True)
                v.tensor_tensor(dlo, sA, sB, ALU.add)
                v.tensor_tensor(dhi, sA, sB, ALU.subtract).then_inc(f["bfv"], 1)
                v.wait_ge(f["bfv"], e * 6 + si + 1)
                src, dst = dst, src

        def dn_gpsimd_dma(gps, n, th, hc):
            slot, fc = hc % 2, hc % NFC
            if hc >= 2:
                gps.wait_ge(n["dmm"], hc - 1)
            for half in range(2):
                gps.dma_start(
                    out=AP2(n["wdn"], slot * NIC * 128 + half * (NIC // 2) * 128,
                            [[2 * NIC * 128, 128], [128, NIC // 2], [1, 128]]),
                    in_=AP2(w_downT, fc * 128 + half * (NIC // 2) * 128 * H,
                            [[H, 128], [128 * H, NIC // 2], [1, 128]])
                ).then_inc(n["wdl"][slot][half], 16)

        def dn_tensor_hc(t, n, th, hc):
            slot = hc % 2
            if hc >= 2:
                t.wait_ge(n["dev"], hc - 1)
            for c in range(NIC):
                if c == 0:
                    t.wait_ge(n["wdl"][slot][0], 16 * (hc // 2 + 1))
                elif c == NIC // 2:
                    t.wait_ge(n["wdl"][slot][1], 16 * (hc // 2 + 1))
                inst = t.matmul(
                    n["dp"][hc % 2][:, :],
                    n["wdn"][:, slot * NIC * 128 + c * 128: slot * NIC * 128 + (c + 1) * 128],
                    hbuf[th][:, c * 512:(c + 1) * 512],
                    start=(c == 0), stop=(c == NIC - 1))
            inst.then_inc(n["dmm"], 1)

        def dn_scalar_hc(sc, n, th, hc):
            sc.wait_ge(n["dmm"], hc + 1)
            if hc >= 2:
                sc.wait_ge(n["fst"][hc % 2], 16 * (hc // 2))
            sc.activation(n["slab"][:, (hc % 2) * 512:(hc % 2) * 512 + 512],
                          n["dp"][hc % 2][:, :], AF.Identity,
                          scale=d["wsdp2"][:, hc % NFC:hc % NFC + 1]).then_inc(n["dev"], 1)

        def dn_sync_hc(s, n, th, hc):
            fc = hc % NFC
            s.wait_ge(n["dev"], hc + 1)
            s.dma_start(
                out=ffn_out[fc * 128:(fc + 1) * 128, th * 512:(th + 1) * 512],
                in_=n["slab"][:, (hc % 2) * 512:(hc % 2) * 512 + 512]
            ).then_inc(n["fst"][hc % 2], 16)

        def load_yh(s, th):
            s.dma_start(out=AP2(yh, 0, [[NFC * 512, 128], [512, NFC], [1, 512]]),
                        in_=AP2(d["y_dram"], th * 512,
                                [[TPC, 128], [128 * TPC, NFC], [1, 512]])
                        ).then_inc(yld, 16)
            nyld[0] += 1

        with contextlib.ExitStack() as es:
            g0 = ug_scope(es, 0)
            block = es.enter_context(nc.Block(no_gpsimd_drain=True))

            @block.gpsimd
            def _(gps):
                if pre_gpsimd is not None:
                    pre_gpsimd(gps)
                for ib in range(NB):
                    ug_gpsimd_dma(gps, g0, 0, ib)

            @block.sync
            def _(s):
                load_yh(s, 0)

            @block.tensor
            def _(t):
                for c in range(NIC):
                    ug_tensor_chunk(t, g0, 0, c)

            @block.scalar
            def _(sc):
                for c in range(NIC):
                    ug_scalar_chunk(sc, g0, 0, c)

            @block.vector
            def _(v):
                for c in range(NIC):
                    ug_vector_chunk(v, g0, 0, c)

        with contextlib.ExitStack() as es:
            g1 = ug_scope(es, 1)
            f0 = fw_scope(es, 0)
            block = es.enter_context(nc.Block(no_gpsimd_drain=True))

            @block.sync
            def _(s):
                load_yh(s, 1)

            @block.gpsimd
            def _(gps):
                for ib in range(NB):
                    ug_gpsimd_dma(gps, g1, 1, ib)

            @block.tensor
            def _(t):
                for e in range(4):
                    fw_tensor_eighth(t, f0, 0, e)
                    for c in range(e * 16, (e + 1) * 16):
                        ug_tensor_chunk(t, g1, 1, c)

            @block.scalar
            def _(sc):
                for e in range(4):
                    fw_scalar_eighth(sc, f0, 0, e)
                    for c in range(e * 16, (e + 1) * 16):
                        ug_scalar_chunk(sc, g1, 1, c)

            @block.vector
            def _(v):
                for e in range(4):
                    bf_gen = fw_vector_gen(v, f0, 0, e - 1) if e >= 1 else None
                    for c in range(e * 16, (e + 1) * 16):
                        ug_vector_chunk(v, g1, 1, c)
                        if bf_gen is not None:
                            next(bf_gen, None)
                    if bf_gen is not None:
                        for _ in bf_gen:
                            pass
                fw_vector_eighth(v, f0, 0, 3)

        ys.close()
        with contextlib.ExitStack() as es:
            f1 = fw_scope(es, 1)
            n0 = dn_scope(es, 0)
            block = es.enter_context(nc.Block(no_gpsimd_drain=True))

            @block.gpsimd
            def _(gps):
                for i in range(2 * NFC):
                    dn_gpsimd_dma(gps, n0, i // NFC, i)

            @block.tensor
            def _(t):
                for e in range(4):
                    fw_tensor_eighth(t, f1, 1, e)
                    for i in range(e * 4, (e + 1) * 4):
                        dn_tensor_hc(t, n0, 0, i)
                t.wait_ge(f1["bfv"], 24)
                for i in range(NFC, 2 * NFC):
                    dn_tensor_hc(t, n0, 1, i)

            @block.scalar
            def _(sc):
                for e in range(4):
                    fw_scalar_eighth(sc, f1, 1, e)
                    for i in range(e * 4, (e + 1) * 4):
                        dn_scalar_hc(sc, n0, 0, i)
                for i in range(NFC, 2 * NFC):
                    dn_scalar_hc(sc, n0, 1, i)

            @block.vector
            def _(v):
                for e in range(4):
                    fw_vector_eighth(v, f1, 1, e)

            @block.sync
            def _(s):
                for i in range(2 * NFC):
                    dn_sync_hc(s, n0, i // NFC, i)
                s.wait_ge(n0["fst"][0], 16 * NFC)
                s.wait_ge(n0["fst"][1], 16 * NFC)



def build_ffn():
    nc = bass.Bass("TRN2", target_bir_lowering=False)
    xn_in = nc.dram_tensor("xn_own", [H, TPC], BF16, kind="ExternalInput")
    su_ug2 = nc.dram_tensor("su_ug2", [128, NFC], F32, kind="ExternalInput")
    su_dp2 = nc.dram_tensor("su_dp2", [128, NIC], F32, kind="ExternalInput")
    wsu2 = nc.dram_tensor("wsu2", [128, NIC], F32, kind="ExternalInput")
    wsg2 = nc.dram_tensor("wsg2", [128, NIC], F32, kind="ExternalInput")
    wsdp2 = nc.dram_tensor("wsdp2", [128, NFC], F32, kind="ExternalInput")
    w_ugT = nc.dram_tensor("w_ugT", [H, 2 * INTER], F32, kind="ExternalInput")
    w_downT = nc.dram_tensor("w_downT", [INTER, H], F32, kind="ExternalInput")
    ffn_out = nc.dram_tensor("ffn_out", [H, TPC], F32, kind="ExternalOutput")
    h128f = nc.inline_tensor(_hadamard(128).astype(np.float32), name="h128f")

    import contextlib
    with contextlib.ExitStack() as es:
        d = _ffn_persist_alloc(nc, es)
        esu = es.enter_context(contextlib.ExitStack())
        d["tbl_ug"] = esu.enter_context(nc.sbuf_tensor("tbl_ug", [128, NFC * 128], BF16))
        with nc.sbuf_tensor("xn_sb", [128, NFC * TPC], BF16) as xn_sb:
            with (
                nc.semaphore("xl") as xl,
                nc.Block(no_gpsimd_drain=True) as block,
            ):
                @block.sync
                def _(s):
                    s.dma_start(out=AP2(xn_sb, 0, [[NFC * TPC, 128], [TPC, NFC], [1, TPC]]),
                                in_=AP2(xn_in, 0, [[TPC, 128], [128 * TPC, NFC], [1, TPC]])
                                ).then_inc(xl, 16)
                    s.wait_ge(xl, 16)
            emit_ffn_tables(nc, d, su_ug2, su_dp2, wsu2, wsg2, wsdp2, h128f)
            emit_ffn_yhad(nc, d, xn_sb)
        esu.close()
        emit_ffn_main(nc, d, w_ugT, w_downT, ffn_out)
    return nc



def emit_attention(nc, xn_full, wqkv, bqkv2, ctx_out, mask_inl, qk_sb, v_tm, ctx_sb,
                   debug_qkv=None, gate=None, a2a=None, wo_early=None):
    QOFF = 0
    KOFF = HPC * S
    NTB = S // 512
    NOC = HPC * 3
    import contextlib
    with contextlib.ExitStack() as qes:
        E = qes.enter_context
        wq_sb = E(nc.sbuf_tensor("a_wqkv", [128, NFC * 768], BF16))
        bq_sb = E(nc.sbuf_tensor("a_bqkv", [128, 6], F32))
        xnb = E(nc.sbuf_tensor("a_xnb", [128, 2 * NFC * 512], BF16))
        vstage = E(nc.sbuf_tensor("a_vst", [128, 2 * 512], BF16))
        wl = E(nc.semaphore("a_wl"))
        wl2 = E(nc.semaphore("a_wl2"))
        xl0 = E(nc.semaphore("a_xl0"))
        xl1 = E(nc.semaphore("a_xl1"))
        qmm = E(nc.semaphore("a_qmm"))
        qev = E(nc.semaphore("a_qev"))
        vtm = E(nc.semaphore("a_vtm"))
        vte = E(nc.semaphore("a_vte"))
        qp0 = E(nc.psum_tensor("a_qp0", [128, 512], F32))
        qp1 = E(nc.psum_tensor("a_qp1", [128, 512], F32))
        qp2 = E(nc.psum_tensor("a_qp2", [128, 512], F32))
        tp0 = E(nc.psum_tensor("a_tp0", [128, 1024], BF16))
        tp1 = E(nc.psum_tensor("a_tp1", [128, 1024], BF16))
        id_sb = E(nc.sbuf_tensor("a_id", [128, 128], BF16))
        block = E(nc.Block(no_gpsimd_drain=True))
        qp = [qp0, qp1, qp2]
        tpp = [tp0, tp1]
        xl = [xl0, xl1]

        @block.gpsimd
        def _(g):
            g.dma_start(out=AP2(wq_sb, 0, [[NFC * 768, 128], [768, NFC], [1, 768]]),
                        in_=AP2(wqkv, 0, [[768, 128], [128 * 768, NFC], [1, 768]])
                        ).then_inc(wl2, 16)

        @block.sync
        def _(s):
            s.dma_start(out=bq_sb[:, :], in_=bqkv2[:, :]).then_inc(wl, 16)
            s.dma_start(out=id_sb[:, :],
                        in_=nc_ident_ap(nc)).then_inc(wl, 16)
            if gate is not None:
                s.wait_ge(gate[0], gate[1])
            for tb in range(NTB):
                if tb >= 2:
                    s.wait_ge(qmm, (tb - 1) * NOC)
                r, half = tb // 2, tb % 2
                s.dma_start(
                    out=AP2(xnb, (tb % 2) * NFC * 512,
                            [[2 * NFC * 512, 128], [512, NFC], [1, 512]]),
                    in_=AP2(xn_full, r * H * TPC + half * 512,
                            [[TPC, 128], [128 * TPC, NFC], [1, 512]])
                ).then_inc(xl[tb % 2], 16)

        @block.tensor
        def _(t):
            t.wait_ge(wl, 32)
            t.wait_ge(wl2, 16)
            for tb in range(NTB):
                slot = tb % 2
                t.wait_ge(xl[slot], 16 * (tb // 2 + 1))
                for oc in range(NOC):
                    i = tb * NOC + oc
                    if i >= 3:
                        t.wait_ge(qev, i - 2)
                    for fc in range(NFC):
                        inst = t.matmul(
                            qp[i % 3][:, :],
                            wq_sb[:, fc * 768 + oc * 128: fc * 768 + (oc + 1) * 128],
                            xnb[:, slot * NFC * 512 + fc * 512: slot * NFC * 512 + (fc + 1) * 512],
                            start=(fc == 0), stop=(fc == NFC - 1))
                    inst.then_inc(qmm, 1)
                for h in range(HPC):
                    t.wait_ge(qev, tb * NOC + 3 * h + 3)
                    if tb >= 1:
                        t.wait_ge(vte, (tb - 1) * 2 + h + 1)
                    for j in range(4):
                        inst = t.transpose(tpp[h][:, j * 128:(j + 1) * 128],
                                           vstage[:, h * 512 + j * 128: h * 512 + (j + 1) * 128],
                                           id_sb[:, :])
                    inst.then_inc(vtm, 1)

        @block.scalar
        def _(sc):
            for tb in range(NTB):
                for oc in range(NOC):
                    i = tb * NOC + oc
                    h, kind = oc // 3, oc % 3
                    sc.wait_ge(qmm, i + 1)
                    if kind == 0:
                        dst = qk_sb[:, QOFF + h * S + tb * 512: QOFF + h * S + (tb + 1) * 512]
                    elif kind == 1:
                        dst = qk_sb[:, KOFF + h * S + tb * 512: KOFF + h * S + (tb + 1) * 512]
                    else:
                        if tb >= 1:
                            sc.wait_ge(vtm, (tb - 1) * 2 + h + 1)
                        dst = vstage[:, h * 512:(h + 1) * 512]
                    sc.activation(dst, qp[i % 3][:, :], AF.Identity,
                                  bias=bq_sb[:, oc:oc + 1]).then_inc(qev, 1)

        @block.vector
        def _(v):
            for tb in range(NTB):
                for h in range(HPC):
                    v.wait_ge(vtm, tb * 2 + h + 1)
                    v.tensor_copy(v_tm[:, h * S + tb * 512: h * S + (tb + 1) * 512],
                                  tpp[h][:, 0:512]).then_inc(vte, 1)

    if debug_qkv is not None:
        dq, dk, dv = debug_qkv
        with nc.semaphore("dbg_st") as dst_sem, nc.Block(no_gpsimd_drain=True) as block:
            @block.sync
            def _(s):
                s.dma_start(out=dq[:, :], in_=qk_sb[:, QOFF:QOFF + HPC * S]).then_inc(dst_sem, 16)
                s.dma_start(out=dk[:, :], in_=qk_sb[:, KOFF:KOFF + HPC * S]).then_inc(dst_sem, 16)
                s.dma_start(out=dv[:, :], in_=v_tm[:, :]).then_inc(dst_sem, 16)
                s.wait_ge(dst_sem, 48)
        return

    import contextlib
    with contextlib.ExitStack() as aes:
        E = aes.enter_context
        mask_sb = E(nc.sbuf_tensor("a_mask", [128, 1024], F32))
        pt_sb = E(nc.sbuf_tensor("a_pt", [128, 4 * 512], BF16))
        ones_col = E(nc.sbuf_tensor("a_ones", [128, 1], BF16))
        ones_k1 = E(nc.sbuf_tensor("a_onesk1", [1, 128], F32))
        l_sb = E(nc.sbuf_tensor("a_l", [1, 512], F32))
        l_inv = E(nc.sbuf_tensor("a_linv", [1, 512], F32))
        lbc_sb = E(nc.sbuf_tensor("a_lbc", [128, 512], F32))
        sp_l = [E(nc.psum_tensor(f"a_sp{j}", [128, 512], F32)) for j in range(4)]
        ctx_ps2 = [E(nc.psum_tensor(f"a_ctxp{j}", [128, 512], F32)) for j in range(2)]
        l_ps2 = [E(nc.psum_tensor(f"a_lp{j}", [128, 512], F32)) for j in range(2)]
        ml = E(nc.semaphore("a_ml"))
        alc = E(nc.semaphore("a_alc"))
        albr = E(nc.semaphore("a_albr"))
        amm = E(nc.semaphore("a_amm"))
        aex = E(nc.semaphore("a_aex"))
        amk = E(nc.semaphore("a_amk"))
        apv = E(nc.semaphore("a_apv"))
        alv = E(nc.semaphore("a_alv"))
        albc = E(nc.semaphore("a_albc"))
        alev = E(nc.semaphore("a_alev"))
        acev = E(nc.semaphore("a_acev"))
        cst = E(nc.semaphore("a_cst"))
        block = E(nc.Block(no_gpsimd_drain=True))
        sp = sp_l
        groups = [(h, g) for h in range(HPC) for g in range(16)]

        gj_of = {}
        mask_count_before = {}
        gj = 0
        nmask = 0
        for (h, g) in groups:
            kmax = 4 * (g + 1)
            ids = []
            for j in range(kmax):
                diag = j >= kmax - 4
                ids.append((gj, j, diag, nmask))
                if diag:
                    nmask += 1
                gj += 1
            gj_of[(h, g)] = ids
        total_pv = gj

        @block.sync
        def _(s):
            s.dma_start(out=mask_sb[:, :], in_=mask_inl.ap()).then_inc(ml, 16)

        @block.vector
        def _(v):
            v.memset(ones_col[:, :], 1.0)
            v.memset(ones_k1[:, :], 1.0).then_inc(ml, 1)

        def _emit_pv(t, gi, rec, h, kmax):
            (cj, j, diag, _mc) = rec
            ctx_ps = ctx_ps2[gi % 2]
            l_ps = l_ps2[gi % 2]
            t.wait_ge(aex, cj + 1)
            if j == 0 and gi >= 2:
                t.wait_ge(acev, gi - 1)
            if j == 0 and gi >= 1:
                t.wait_ge(alev, gi)
            t.matmul(ctx_ps[:, :],
                     v_tm[:, h * S + j * 128: h * S + (j + 1) * 128],
                     pt_sb[:, (cj % 4) * 512:(cj % 4 + 1) * 512],
                     start=(j == 0), stop=(j == kmax - 1))
            t.matmul(l_ps[0:1, :], ones_col[:, :],
                     pt_sb[:, (cj % 4) * 512:(cj % 4 + 1) * 512],
                     start=(j == 0), stop=(j == kmax - 1)).then_inc(apv, 1)

        flat = []
        flat_alc = {}
        nal = 0
        for gi, (h, g) in enumerate(groups):
            ids = gj_of[(h, g)]
            for rec in ids:
                flat.append((gi, h, g, rec, len(ids)))
            nal += len(ids)
            flat_alc[gi] = nal

        def _emit_s(t, item):
            (gi, h, g, (cj, j, diag, _mc), kmax) = item
            if cj >= 4:
                t.wait_ge(aex, cj - 3)
            t.matmul(sp[cj % 4][:, :],
                     qk_sb[:, KOFF + h * S + j * 128: KOFF + h * S + (j + 1) * 128],
                     qk_sb[:, QOFF + h * S + g * 512: QOFF + h * S + (g + 1) * 512],
                     start=True, stop=True).then_inc(amm, 1)

        @block.tensor
        def _(t):
            t.wait_ge(ml, 17)
            DEPTH = 3
            for k in range(DEPTH):
                _emit_s(t, flat[k])
            for n, item in enumerate(flat):
                if n + DEPTH < len(flat):
                    _emit_s(t, flat[n + DEPTH])
                (gi, h, g, rec, kmax) = item
                _emit_pv(t, gi, rec, h, kmax)
                if rec[1] == kmax - 1:
                    t.wait_ge(alv, 2 * gi + 2)
                    t.matmul(l_ps2[gi % 2][:, :], ones_k1[:, :], l_inv[:, :],
                             start=True, stop=True).then_inc(albc, 1)

        @block.vector
        def _(v):
            for gi, (h, g) in enumerate(groups):
                ids = gj_of[(h, g)]
                for (cj, j, diag, mc) in ids:
                    if diag:
                        jloc = j - (len(ids) - 4)
                        u0 = 512 - 128 * jloc
                        v.wait_ge(amm, cj + 1)
                        v.tensor_add(sp[cj % 4][:, :], sp[cj % 4][:, :],
                                     mask_sb[:, u0:u0 + 512]).then_inc(amk, 1)
                v.wait_ge(apv, ids[-1][0] + 1)
                v.tensor_copy(l_sb[:, :], l_ps2[gi % 2][0:1, :]).then_inc(alv, 1)
                v.wait_ge(alv, 2 * gi + 1)
                v.reciprocal(l_inv[:, :], l_sb[:, :]).then_inc(alv, 1)
                v.wait_ge(albc, gi + 1)
                v.tensor_copy(lbc_sb[:, :], l_ps2[gi % 2][:, :]).then_inc(alev, 1)
                v.wait_ge(alev, gi + 1)
                v.tensor_mul(ctx_sb[:, h * S + g * 512: h * S + (g + 1) * 512],
                             ctx_ps2[gi % 2][:, :], lbc_sb[:, :]).then_inc(acev, 1)

        @block.scalar
        def _(sc):
            for gi, (h, g) in enumerate(groups):
                for (cj, j, diag, mc) in gj_of[(h, g)]:
                    sc.wait_ge(amm, cj + 1)
                    if diag:
                        sc.wait_ge(amk, mc + 1)
                    if cj >= 4:
                        sc.wait_ge(apv, cj - 3)
                    sc.activation(pt_sb[:, (cj % 4) * 512:(cj % 4 + 1) * 512],
                                  sp[cj % 4][:, :], AF.Exp,
                                  scale=float(1.0 / np.sqrt(HD))).then_inc(aex, 1)

        if a2a is None:
            @block.sync
            def _(s):
                s.wait_ge(acev, len(groups))
                for h in range(HPC):
                    s.dma_start(out=ctx_out[h * 128:(h + 1) * 128, :],
                                in_=ctx_sb[:, h * S:(h + 1) * S]).then_inc(cst, 16)
                s.wait_ge(cst, 32)
        else:
            ag1_ins, ag1_outs, ccsem = a2a

            @block.gpsimd
            def _(g):
                from concourse.aot_env import allow_ici_all_to_all
                if wo_early is not None:
                    w_out_dram, wl2 = wo_early
                    g.wait_ge(amm, gj)
                    for jj in range(4):
                        g.dma_start(
                            out=AP2(qk_sb, jj * 4 * H, [[NFC * H, 128], [H, 4], [1, H]]),
                            in_=AP2(w_out_dram, jj * 4 * 128 * H,
                                    [[H, 128], [128 * H, 4], [1, H]])
                        ).then_inc(wl2, 16)
                pieces = [(0, 0, TPC, 16), (1, 0, TPC, 32)]
                ncst = [0]
                for pi, (h, t0, tw, gate_ev) in enumerate(pieces):
                    g.wait_ge(acev, gate_ev)
                    g.dma_start(
                        out=AP2(ag1_ins[pi], 0,
                                [[tw, 128], [128 * tw, NCORE], [1, tw]]),
                        in_=AP2(ctx_sb, h * S + t0,
                                [[HPC * S, 128], [TPC, NCORE], [1, tw]])
                    ).then_inc(cst, 16)
                    ncst[0] += 16
                    g.wait_ge(cst, ncst[0])
                    with allow_ici_all_to_all():
                        g.collective_compute(
                            "AllToAll", ALU.bypass,
                            replica_groups=[list(range(NCORE))],
                            ins=[ag1_ins[pi].ap().opt()],
                            outs=[ag1_outs[pi].ap().opt()],
                        ).then_inc(ccsem, 1)


_IDENT_CACHE = {}


def nc_ident_ap(nc):
    key = id(nc)
    if key not in _IDENT_CACHE:
        _IDENT_CACHE[key] = nc.inline_tensor(np.eye(128, dtype=BF), name="ident_a")
    return _IDENT_CACHE[key].ap()


def _mask_inline(nc):
    m = np.full((128, 1024), MASKVAL, dtype=np.float32)
    for ki in range(128):
        m[ki, ki + 512:] = 0.0
    return nc.inline_tensor(m, name="mask_master")


def build_attn(debug=False):
    nc = bass.Bass("TRN2", target_bir_lowering=False)
    xn_full = nc.dram_tensor("xn_full", [NCORE * H, TPC], BF16, kind="ExternalInput")
    wqkv = nc.dram_tensor("wqkv", [H, 768], F32, kind="ExternalInput")
    bqkv2 = nc.dram_tensor("bqkv2", [128, 6], F32, kind="ExternalInput")
    ctx_out = nc.dram_tensor("ctx", [HPC * 128, S], BF16, kind="ExternalOutput")
    dbg = None
    if debug:
        dbg = (nc.dram_tensor("dq", [128, HPC * S], BF16, kind="ExternalOutput"),
               nc.dram_tensor("dk", [128, HPC * S], BF16, kind="ExternalOutput"),
               nc.dram_tensor("dv", [128, HPC * S], BF16, kind="ExternalOutput"))
    mask_inl = _mask_inline(nc)
    import contextlib
    with contextlib.ExitStack() as es:
        qk_sb = es.enter_context(nc.sbuf_tensor("qk_sb", [128, 2 * HPC * S], BF16))
        v_tm = es.enter_context(nc.sbuf_tensor("v_tm", [128, HPC * S], BF16))
        ctx_sb = es.enter_context(nc.sbuf_tensor("ctx_sb", [128, HPC * S], BF16))
        emit_attention(nc, xn_full, wqkv, bqkv2, ctx_out, mask_inl,
                       qk_sb, v_tm, ctx_sb, debug_qkv=dbg)
    return nc



def build_l3():
    nc = bass.Bass("TRN2", target_bir_lowering=False)
    ctx_my = nc.dram_tensor("ctx_my", [H, TPC], BF16, kind="ExternalInput")
    x_in = nc.dram_tensor("x", [TPC, H], F32, kind="ExternalInput")
    ffn_in = nc.dram_tensor("ffn", [H, TPC], F32, kind="ExternalInput")
    w_out = nc.dram_tensor("w_out", [H, H], F32, kind="ExternalInput")
    bout2 = nc.dram_tensor("bout2", [128, NFC], F32, kind="ExternalInput")
    y_out = nc.dram_tensor("y", [TPC, H], F32, kind="ExternalOutput")
    emit_final(nc, ctx_my, x_in, ffn_in, w_out, bout2, y_out)
    return nc


def emit_final(nc, ctx_my, x_in, ffn_in, w_out, bout2, y_out, gate=None,
               wo_pre=None):
    import contextlib
    NIT = 2 * NFC
    with contextlib.ExitStack() as es:
        E = es.enter_context
        ctx_sb = E(nc.sbuf_tensor("o_ctx", [128, NFC * TPC], BF16))
        if wo_pre is None:
            wo_sb = E(nc.sbuf_tensor("o_wo", [128, NFC * H], BF16))
        else:
            wo_sb, wl2_pre = wo_pre
        b2_sb = E(nc.sbuf_tensor("o_b2", [128, NFC], F32))
        id_f32 = E(nc.sbuf_tensor("o_idf", [128, 128], F32))
        fsl = E(nc.sbuf_tensor("o_f", [128, 2 * 512], F32))
        ssl = E(nc.sbuf_tensor("o_s", [128, 2 * 512], F32))
        xsl = E(nc.sbuf_tensor("o_x", [128, 4 * H], F32))
        osl = E(nc.sbuf_tensor("o_o", [128, 4 * H], F32))
        op0 = E(nc.psum_tensor("o_p0", [128, 512], F32))
        op1 = E(nc.psum_tensor("o_p1", [128, 512], F32))
        tp0 = E(nc.psum_tensor("o_t0", [128, 512], F32))
        tp1 = E(nc.psum_tensor("o_t1", [128, 512], F32))
        ld = E(nc.semaphore("o_ld"))
        wl2 = E(nc.semaphore("o_wl2"))
        fld = [E(nc.semaphore("o_fl0")), E(nc.semaphore("o_fl1"))]
        xld = [E(nc.semaphore("o_xl0")), E(nc.semaphore("o_xl1"))]
        omm = E(nc.semaphore("o_omm"))
        osm = E(nc.semaphore("o_osm"))
        ovv = E(nc.semaphore("o_ovv"))
        otm = E(nc.semaphore("o_otm"))
        oev = E(nc.semaphore("o_oev"))
        ost = [E(nc.semaphore("o_st0")), E(nc.semaphore("o_st1"))]
        op = [op0, op1]
        tp = [tp0, tp1]
        idf = nc.inline_tensor(np.eye(128, dtype=np.float32), name="ident_f32")

        with nc.Block(no_gpsimd_drain=True) as block:
            if wo_pre is None:
                @block.gpsimd
                def _(g):
                    for j in range(4):
                        g.dma_start(
                            out=AP2(wo_sb, j * 4 * H, [[NFC * H, 128], [H, 4], [1, H]]),
                            in_=AP2(w_out, j * 4 * 128 * H, [[H, 128], [128 * H, 4], [1, H]])
                        ).then_inc(wl2, 16)
            else:
                wl2 = wl2_pre

            @block.sync
            def _(s):
                s.dma_start(out=b2_sb[:, :], in_=bout2[:, :]).then_inc(ld, 16)
                s.dma_start(out=id_f32[:, :], in_=idf.ap()).then_inc(ld, 16)
                for tcq in range(4):
                    s.dma_start(out=xsl[:, tcq * H:(tcq + 1) * H],
                                in_=x_in[tcq * 128:(tcq + 1) * 128, :]
                                ).then_inc(xld[0], 16)
                for i0 in range(2):
                    s.dma_start(out=fsl[:, i0 * 512:(i0 + 1) * 512],
                                in_=ffn_in[i0 * 128:(i0 + 1) * 128, 0:512]
                                ).then_inc(fld[i0], 16)
                if gate is not None:
                    s.wait_ge(gate[0], gate[1])
                if isinstance(ctx_my, list):
                    pieces = [(0, 0, TPC, ctx_my[0]),
                              (NFC // 2, 0, TPC, ctx_my[1])]
                    nparts = 2
                    for cb, co, tw, part in pieces:
                        s.dma_start(out=AP2(ctx_sb, cb * TPC + co,
                                            [[NFC * TPC, 128], [TPC, NFC // 2], [1, tw]]),
                                    in_=AP2(part, 0, [[tw, 128], [128 * tw, NFC // 2], [1, tw]])
                                    ).then_inc(ld, 16)
                else:
                    nparts = 1
                    s.dma_start(out=AP2(ctx_sb, 0, [[NFC * TPC, 128], [TPC, NFC], [1, TPC]]),
                                in_=AP2(ctx_my, 0, [[TPC, 128], [128 * TPC, NFC], [1, TPC]])
                                ).then_inc(ld, 16)
                s.wait_ge(ld, 32 + 16 * nparts)
                for tb2 in range(2):
                    if tb2 == 1:
                        s.wait_ge(oev, NFC)
                        for tc in range(4):
                            s.dma_start(out=y_out[tc * 128:(tc + 1) * 128, :],
                                        in_=osl[:, tc * H:(tc + 1) * H]
                                        ).then_inc(ost[tc % 2], 16)
                        for tcq in range(4):
                            tc = 4 + tcq
                            s.dma_start(out=xsl[:, tcq * H:(tcq + 1) * H],
                                        in_=x_in[tc * 128:(tc + 1) * 128, :]
                                        ).then_inc(xld[1], 16)
                    for hc in range(NFC):
                        i = tb2 * NFC + hc
                        sl = i % 2
                        if i < 2:
                            continue
                        s.wait_ge(ovv, i - 1)
                        s.dma_start(out=fsl[:, sl * 512:(sl + 1) * 512],
                                    in_=ffn_in[hc * 128:(hc + 1) * 128,
                                               tb2 * 512:(tb2 + 1) * 512]
                                    ).then_inc(fld[sl], 16)
                s.wait_ge(oev, 2 * NFC)
                for tc in range(4, 8):
                    s.dma_start(out=y_out[tc * 128:(tc + 1) * 128, :],
                                in_=osl[:, (tc - 4) * H:(tc - 3) * H]
                                ).then_inc(ost[tc % 2], 16)
                s.wait_ge(ost[0], 16 * 4)
                s.wait_ge(ost[1], 16 * 4)

            def _tr(t, i):
                t.wait_ge(ovv, i + 1)
                if i >= 2:
                    t.wait_ge(oev, i - 1)
                for q in range(4):
                    inst = t.transpose(
                        tp[i % 2][:, q * 128:(q + 1) * 128],
                        ssl[:, (i % 2) * 512 + q * 128:(i % 2) * 512 + (q + 1) * 128],
                        id_f32[:, :])
                inst.then_inc(otm, 1)

            @block.tensor
            def _(t):
                t.wait_ge(wl2, 64)
                t.wait_ge(ld, 32 + 16 * (2 if isinstance(ctx_my, list) else 1))
                for i in range(NIT):
                    tb2, hc = i // NFC, i % NFC
                    if i >= 2:
                        t.wait_ge(osm, i - 1)
                    for fc in range(NFC):
                        inst = t.matmul(
                            op[i % 2][:, :],
                            wo_sb[:, fc * H + hc * 128: fc * H + (hc + 1) * 128],
                            ctx_sb[:, fc * TPC + tb2 * 512: fc * TPC + (tb2 + 1) * 512],
                            start=(fc == 0), stop=(fc == NFC - 1))
                    inst.then_inc(omm, 1)
                    if i >= 1:
                        _tr(t, i - 1)
                _tr(t, NIT - 1)

            @block.scalar
            def _(sc):
                for i in range(NIT):
                    sl = i % 2
                    hc = i % NFC
                    sc.wait_ge(omm, i + 1)
                    if i >= 2:
                        sc.wait_ge(otm, i - 1)
                    sc.activation(ssl[:, sl * 512:(sl + 1) * 512], op[i % 2][:, :],
                                  AF.Identity,
                                  bias=b2_sb[:, hc:hc + 1]).then_inc(osm, 1)

            def _ev(v, i):
                tb2, hc = i // NFC, i % NFC
                v.wait_ge(otm, i + 1)
                v.wait_ge(xld[tb2], 16 * 4)
                if tb2 == 1 and hc == 0:
                    v.wait_ge(ost[0], 32)
                    v.wait_ge(ost[1], 32)
                for q in range(4):
                    inst = v.scalar_tensor_tensor(
                        osl[:, q * H + hc * 128: q * H + (hc + 1) * 128],
                        tp[i % 2][:, q * 128:(q + 1) * 128], 1.0,
                        xsl[:, q * H + hc * 128: q * H + (hc + 1) * 128],
                        ALU.mult, ALU.add)
                inst.then_inc(oev, 1)

            @block.vector
            def _(v):
                for i in range(NIT):
                    sl = i % 2
                    if i >= 1:
                        _ev(v, i - 1)
                    v.wait_ge(osm, i + 1)
                    v.wait_ge(fld[sl], 16 * (i // 2 + 1))
                    v.tensor_add(ssl[:, sl * 512:(sl + 1) * 512],
                                 ssl[:, sl * 512:(sl + 1) * 512],
                                 fsl[:, sl * 512:(sl + 1) * 512]).then_inc(ovv, 1)
                _ev(v, NIT - 1)




def kernel_unfused(L0_attn_hidden_states, L0_attn_w_qkv, L0_attn_b_qkv, L0_attn_w_out,
           L0_attn_b_out, L0_ffn_SU_ug, L0_ffn_SU_dp, L0_ffn_Wscale_ug,
           L0_ffn_Wscale_dp, L0_ffn_W_up, L0_ffn_W_gate, L0_ffn_W_down):
    x = np.ascontiguousarray(
        np.asarray(L0_attn_hidden_states, dtype=np.float32).reshape(S, H))
    w_qkv = np.asarray(L0_attn_w_qkv, dtype=np.float32)
    b_qkv = np.asarray(L0_attn_b_qkv, dtype=np.float32)
    w_out = np.ascontiguousarray(np.asarray(L0_attn_w_out, dtype=np.float32))
    b_out = np.asarray(L0_attn_b_out, dtype=np.float32)
    su_ug = np.asarray(L0_ffn_SU_ug, dtype=np.float32)
    su_dp = np.asarray(L0_ffn_SU_dp, dtype=np.float32)
    ws_ug = np.asarray(L0_ffn_Wscale_ug, dtype=np.float32)
    ws_dp = np.asarray(L0_ffn_Wscale_dp, dtype=np.float32)
    w_upT = np.ascontiguousarray(np.asarray(L0_ffn_W_up, dtype=np.float32).T)
    w_gateT = np.ascontiguousarray(np.asarray(L0_ffn_W_gate, dtype=np.float32).T)
    w_downT = np.ascontiguousarray(np.asarray(L0_ffn_W_down, dtype=np.float32).T)

    cores = list(range(NCORE))

    nc1 = build_l1()
    res1 = run_bass_kernel_spmd(
        nc1, [{"x": x[c * TPC:(c + 1) * TPC]} for c in cores], core_ids=cores)
    xn_shards = [np.asarray(r["xn_fm"]) for r in res1.results]
    xn_full = np.ascontiguousarray(np.concatenate(xn_shards, axis=0))

    nc2 = build_ffn()
    ffn_maps = []
    for c in cores:
        ffn_maps.append({
            "xn_own": xn_shards[c],
            "su_ug2": np.ascontiguousarray(su_ug.reshape(NFC, 128).T),
            "su_dp2": np.ascontiguousarray(su_dp.reshape(NIC, 128).T),
            "wsu2": np.ascontiguousarray(ws_ug[:INTER].reshape(NIC, 128).T),
            "wsg2": np.ascontiguousarray(ws_ug[INTER:].reshape(NIC, 128).T),
            "wsdp2": np.ascontiguousarray(ws_dp.reshape(NFC, 128).T),
            "w_upT": w_upT, "w_gateT": w_gateT, "w_downT": w_downT,
        })
    res2 = run_bass_kernel_spmd(nc2, ffn_maps, core_ids=cores)
    ffn_shards = [np.asarray(r["ffn_out"]) for r in res2.results]

    nc3 = build_attn()
    attn_maps = []
    for c in cores:
        attn_maps.append({
            "xn_full": xn_full,
            "wqkv": np.ascontiguousarray(w_qkv[:, 768 * c:768 * (c + 1)]),
            "bqkv2": np.ascontiguousarray(
                b_qkv[768 * c:768 * (c + 1)].reshape(6, 128).T),
        })
    res3 = run_bass_kernel_spmd(nc3, attn_maps, core_ids=cores)
    ctx_full = np.concatenate([np.asarray(r["ctx"]) for r in res3.results], axis=0)

    nc4 = build_l3()
    l3_maps = []
    for c in cores:
        l3_maps.append({
            "ctx_my": np.ascontiguousarray(ctx_full[:, c * TPC:(c + 1) * TPC]),
            "x": x[c * TPC:(c + 1) * TPC],
            "ffn": ffn_shards[c],
            "w_out": w_out,
            "bout2": np.ascontiguousarray(b_out.reshape(NFC, 128).T),
        })
    res4 = run_bass_kernel_spmd(nc4, l3_maps, core_ids=cores)
    y = np.concatenate([np.asarray(r["y"]) for r in res4.results], axis=0)
    return y.reshape(1, S, H)



def build_fused():
    nc = bass.Bass("TRN2", target_bir_lowering=False)
    x_in = nc.dram_tensor("x", [TPC, H], F32, kind="ExternalInput")
    wqkv = nc.dram_tensor("wqkv", [H, 768], F32, kind="ExternalInput")
    bqkv2 = nc.dram_tensor("bqkv2", [128, 6], F32, kind="ExternalInput")
    w_out = nc.dram_tensor("w_out", [H, H], F32, kind="ExternalInput")
    bout2 = nc.dram_tensor("bout2", [128, NFC], F32, kind="ExternalInput")
    su_ug2 = nc.dram_tensor("su_ug2", [128, NFC], F32, kind="ExternalInput")
    su_dp2 = nc.dram_tensor("su_dp2", [128, NIC], F32, kind="ExternalInput")
    wsu2 = nc.dram_tensor("wsu2", [128, NIC], F32, kind="ExternalInput")
    wsg2 = nc.dram_tensor("wsg2", [128, NIC], F32, kind="ExternalInput")
    wsdp2 = nc.dram_tensor("wsdp2", [128, NFC], F32, kind="ExternalInput")
    w_ugT = nc.dram_tensor("w_ugT", [H, 2 * INTER], F32, kind="ExternalInput")
    w_downT = nc.dram_tensor("w_downT", [INTER, H], F32, kind="ExternalInput")
    y_out = nc.dram_tensor("y", [TPC, H], F32, kind="ExternalOutput")

    h128f = nc.inline_tensor(_hadamard(128).astype(np.float32), name="h128f")
    mask_inl = _mask_inline(nc)

    ag0_in = nc.dram_tensor("ag0_in", [H, TPC], BF16, kind="Internal")
    ag0_out = nc.dram_tensor("ag0_out", [NCORE * H, TPC], BF16, kind="Internal",
                             addr_space="Shared")
    _a2a_shapes = [TPC, TPC]
    ag1_ins = [nc.dram_tensor(f"ag1_in{i}", [NCORE * 128, w], BF16, kind="Internal")
               for i, w in enumerate(_a2a_shapes)]
    ag1_outs = [nc.dram_tensor(f"ag1_out{i}", [NCORE * 128, w], BF16, kind="Internal")
                for i, w in enumerate(_a2a_shapes)]
    ffn_dram = nc.dram_tensor("ffn_sp", [H, TPC], F32, kind="Internal")

    import contextlib
    with contextlib.ExitStack() as es:
        cc = es.enter_context(nc.semaphore("cc"))
        ag0st = es.enter_context(nc.semaphore("ag0st"))
        esf = es.enter_context(contextlib.ExitStack())
        d = _ffn_persist_alloc(nc, esf)
        esu = es.enter_context(contextlib.ExitStack())
        d["tbl_ug"] = esu.enter_context(nc.sbuf_tensor("tbl_ug", [128, NFC * 128], BF16))
        with nc.named_scope("tables"):
            emit_ffn_tables(nc, d, su_ug2, su_dp2, wsu2, wsg2, wsdp2, h128f)
        def _trigger_ag0(g):
            g.wait_ge(ag0st, 16)
            g.collective_compute(
                "AllGather", ALU.bypass,
                replica_groups=[list(range(NCORE))],
                ins=[ag0_in.ap().opt()], outs=[ag0_out.ap().opt()],
            ).then_inc(cc, 1)

        with nc.sbuf_tensor("xn_sb", [128, NFC * TPC], BF16) as xn_sb:
            with nc.named_scope("ln"):
                emit_ln(nc, x_in, xn_sb)
            with nc.Block(no_gpsimd_drain=True) as block:
                @block.gpsimd
                def _(g):
                    g.dma_start(
                        out=AP2(ag0_in, 0, [[TPC, 128], [128 * TPC, NFC], [1, TPC]]),
                        in_=AP2(xn_sb, 0, [[NFC * TPC, 128], [TPC, NFC], [1, TPC]])
                    ).then_inc(ag0st, 16)
            with nc.named_scope("yhad"):
                emit_ffn_yhad(nc, d, xn_sb)
        esu.close()
        with nc.named_scope("ffn"):
            emit_ffn_main(nc, d, w_ugT, w_downT, ffn_dram,
                          pre_gpsimd=_trigger_ag0)
        esf.close()
        qk_sb = es.enter_context(nc.sbuf_tensor("qk_sb", [128, 2 * HPC * S], BF16))
        wl2e = es.enter_context(nc.semaphore("wl2e"))
        with contextlib.ExitStack() as es2:
            E = es2.enter_context
            ctx_sb = E(nc.sbuf_tensor("ctx_sb", [128, HPC * S], BF16))
            v_tm = E(nc.sbuf_tensor("v_tm", [128, HPC * S], BF16))
            with nc.named_scope("attn"):
                emit_attention(nc, ag0_out, wqkv, bqkv2, None, mask_inl,
                               qk_sb, v_tm, ctx_sb,
                               gate=(cc, 1), a2a=(ag1_ins, ag1_outs, cc),
                               wo_early=(w_out, wl2e))
        with nc.named_scope("final"):
            emit_final(nc, ag1_outs, x_in, ffn_dram, w_out, bout2, y_out,
                       gate=(cc, 3), wo_pre=(qk_sb, wl2e))
    return nc


def _host_maps(x, w_qkv, b_qkv, w_out, b_out, su_ug, su_dp, ws_ug, ws_dp,
               w_upT, w_gateT, w_downT):
    maps = []
    head_order = [2 * r for r in range(NCORE)] + [2 * r + 1 for r in range(NCORE)]
    feat_perm = np.concatenate([np.arange(128) + hd * 128 for hd in head_order])
    w_out = np.ascontiguousarray(w_out[feat_perm, :])
    WB = 128
    w_ugT = np.ascontiguousarray(
        np.stack([w_upT.reshape(H, INTER // WB, WB),
                  w_gateT.reshape(H, INTER // WB, WB)], axis=2).reshape(H, 2 * INTER))
    su_ug2 = np.ascontiguousarray(su_ug.reshape(NFC, 128).T)
    su_dp2 = np.ascontiguousarray(su_dp.reshape(NIC, 128).T)
    wsu2 = np.ascontiguousarray(ws_ug[:INTER].reshape(NIC, 128).T)
    wsg2 = np.ascontiguousarray(ws_ug[INTER:].reshape(NIC, 128).T)
    wsdp2 = np.ascontiguousarray(ws_dp.reshape(NFC, 128).T)
    bout2 = np.ascontiguousarray(b_out.reshape(NFC, 128).T)
    for c in range(NCORE):
        maps.append({
            "x": x[c * TPC:(c + 1) * TPC],
            "wqkv": np.ascontiguousarray(w_qkv[:, 768 * c:768 * (c + 1)]),
            "bqkv2": np.ascontiguousarray(
                b_qkv[768 * c:768 * (c + 1)].reshape(6, 128).T),
            "w_out": w_out, "bout2": bout2,
            "su_ug2": su_ug2, "su_dp2": su_dp2, "wsu2": wsu2, "wsg2": wsg2,
            "wsdp2": wsdp2,
            "w_ugT": w_ugT, "w_downT": w_downT,
        })
    return maps


def kernel_fused(L0_attn_hidden_states, L0_attn_w_qkv, L0_attn_b_qkv,
                 L0_attn_w_out, L0_attn_b_out, L0_ffn_SU_ug, L0_ffn_SU_dp,
                 L0_ffn_Wscale_ug, L0_ffn_Wscale_dp, L0_ffn_W_up,
                 L0_ffn_W_gate, L0_ffn_W_down):
    x = np.ascontiguousarray(
        np.asarray(L0_attn_hidden_states, dtype=np.float32).reshape(S, H))
    nc = build_fused()
    maps = _host_maps(
        x,
        np.asarray(L0_attn_w_qkv, dtype=np.float32),
        np.asarray(L0_attn_b_qkv, dtype=np.float32),
        np.ascontiguousarray(np.asarray(L0_attn_w_out, dtype=np.float32)),
        np.asarray(L0_attn_b_out, dtype=np.float32),
        np.asarray(L0_ffn_SU_ug, dtype=np.float32),
        np.asarray(L0_ffn_SU_dp, dtype=np.float32),
        np.asarray(L0_ffn_Wscale_ug, dtype=np.float32),
        np.asarray(L0_ffn_Wscale_dp, dtype=np.float32),
        np.ascontiguousarray(np.asarray(L0_ffn_W_up, dtype=np.float32).T),
        np.ascontiguousarray(np.asarray(L0_ffn_W_gate, dtype=np.float32).T),
        np.ascontiguousarray(np.asarray(L0_ffn_W_down, dtype=np.float32).T))
    res = run_bass_kernel_spmd(nc, maps, core_ids=list(range(NCORE)))
    y = np.concatenate([np.asarray(r["y"]) for r in res.results], axis=0)
    return y.reshape(1, S, H)


kernel = kernel_fused
